# version 04
# baseline (speedup 1.0000x reference)
import numpy as np
import ml_dtypes
import concourse.bass as bass
import concourse.mybir as mybir
from concourse.bass_utils import run_bass_kernel_spmd

F32 = mybir.dt.float32
BF16 = mybir.dt.bfloat16
AF = mybir.ActivationFunctionType
ALU = mybir.AluOpType
AX = mybir.AxisListType
bfnp = ml_dtypes.bfloat16

T = 4096
C = 1024
NH = 16
HD = 64
NCC = 8
NTT = 32
NTK = 8
EPS = 1e-5
NE = 4
DFF = 4096
NJC = 32
NTC = 4
SCALE = 0.125
STRW = 4112

DMA_QUEUES = ("sync", "gpsimd")
N_SLOTS = 4
ENGS = ("tensor", "vector", "scalar", "gpsimd", "sync")


def _rgn(prefix, lo, hi, gran):
    return [f"{prefix}:{i}" for i in range(lo // gran, (hi + gran - 1) // gran)]


class Prog:

    def __init__(self, serial=False):
        self.items = []
        self.serial = serial

    def emit(self, eng, fn, reads=(), writes=(), dma=False):
        self.items.append(dict(eng=eng, fn=fn, reads=tuple(reads),
                               writes=tuple(writes), dma=dma))
        return len(self.items) - 1

    def realize(self, eng_sems, dma_sems):
        n = len(self.items)
        deps = [set() for _ in range(n)]
        if self.serial:
            for i in range(1, n):
                deps[i].add(i - 1)
        else:
            last_w = {}
            readers = {}
            for i, it in enumerate(self.items):
                for r in it["reads"]:
                    if r in last_w:
                        deps[i].add(last_w[r])
                for w in it["writes"]:
                    if w in last_w:
                        deps[i].add(last_w[w])
                    for rd in readers.get(w, ()):
                        deps[i].add(rd)
                for r in it["reads"]:
                    readers.setdefault(r, []).append(i)
                for w in it["writes"]:
                    last_w[w] = i
                    readers[w] = []
        for i, it in enumerate(self.items):
            if it["eng"] == "tensor" and not it["dma"]:
                deps[i] = {j for j in deps[i]
                           if self.items[j]["dma"] or self.items[j]["eng"] != "tensor"}
        dma_idx = {}
        q_counts = {q: 0 for q in DMA_QUEUES}
        slot_uses = {(q, s): 0 for q in DMA_QUEUES for s in range(N_SLOTS)}
        for i, it in enumerate(self.items):
            if it["dma"]:
                q = it["eng"]
                s = q_counts[q] % N_SLOTS
                q_counts[q] += 1
                slot_uses[(q, s)] += 1
                dma_idx[i] = (q, s, slot_uses[(q, s)])
        need_mark = [False] * n
        for i in range(n):
            for j in deps[i]:
                if not self.items[j]["dma"]:
                    need_mark[j] = True
        comp_seq = {}
        ctr = {e: 0 for e in ENGS}
        for i, it in enumerate(self.items):
            if not it["dma"] and need_mark[i]:
                ctr[it["eng"]] += 1
                comp_seq[i] = (it["eng"], ctr[it["eng"]])
        final_counts = dict(ctr)
        per_eng = {e: [] for e in ENGS}
        hwm = {e: {} for e in ENGS}

        def mk_closure(i):
            it = self.items[i]
            eng_name = it["eng"]
            waits = []
            for j in sorted(deps[i]):
                jt = self.items[j]
                if jt["dma"]:
                    q, s, use = dma_idx[j]
                    waits.append((("d", q, s), 16 * use))
                else:
                    e2, seq = comp_seq[j]
                    waits.append((("c", e2), seq))
            if it["dma"]:
                q, s, use = dma_idx[i]
                if use > 1:
                    waits.append((("d", q, s), 16 * (use - 1)))
            h = hwm[eng_name]
            real_waits = []
            for key, val in waits:
                if h.get(key, 0) >= val:
                    continue
                h[key] = val
                real_waits.append((key, val))
            if it["dma"]:
                q, s, use = dma_idx[i]
                inc = (("d", q, s), 16)
            elif need_mark[i]:
                inc = (("c", eng_name), 1)
            else:
                inc = None

            def closure(eng):
                for key, val in real_waits:
                    sem = dma_sems[key[1]][key[2]] if key[0] == "d" else eng_sems[key[1]]
                    eng.wait_ge(sem, val)
                ins = it["fn"](eng)
                if inc is not None:
                    key, amt = inc
                    sem = dma_sems[key[1]][key[2]] if key[0] == "d" else eng_sems[key[1]]
                    ins.then_inc(sem, amt)
            return closure

        for i in range(n):
            per_eng[self.items[i]["eng"]].append(mk_closure(i))

        def final_join(eng):
            for q in DMA_QUEUES:
                for s in range(N_SLOTS):
                    if slot_uses[(q, s)]:
                        eng.wait_ge(dma_sems[q][s], 16 * slot_uses[(q, s)])
            for e, cnt in final_counts.items():
                if cnt and e != "sync":
                    eng.wait_ge(eng_sems[e], cnt)
        per_eng["sync"].append(final_join)
        return per_eng


def host_constants():
    ident = np.eye(128, dtype=bfnp)
    identf = np.eye(128, dtype=np.float32)
    perm = np.zeros((128, 128), dtype=bfnp)
    for h in range(2):
        b = 64 * h
        for i in range(32):
            perm[b + i, b + 32 + i] = 1.0
            perm[b + 32 + i, b + i] = 1.0
    ones_row = np.ones((1, 128), dtype=bfnp)
    inv_freq = 1.0 / (10000.0 ** (np.arange(0, HD, 2, dtype=np.float32) / HD))
    t = np.arange(T, dtype=np.float32)
    emb = np.concatenate([np.outer(t, inv_freq), np.outer(t, inv_freq)], axis=-1)
    cosT = np.cos(emb).T.astype(np.float32)
    sinT = np.sin(emb).T.astype(np.float32)
    sinT[0:32] = -sinT[0:32]
    rope_i = np.zeros((128, 2 * T), dtype=bfnp)
    rope_i[0:64, 0:T] = cosT.astype(bfnp)
    rope_i[64:128, 0:T] = cosT.astype(bfnp)
    rope_i[0:64, T:] = sinT.astype(bfnp)
    rope_i[64:128, T:] = sinT.astype(bfnp)
    masks = np.zeros((128, 4 * 512), dtype=bfnp)
    r = np.arange(128)[:, None]
    j = np.arange(512)[None, :]
    for b in range(4):
        masks[:, b * 512:(b + 1) * 512] = (j >= 128 * b + r).astype(bfnp)
    return ident, identf, perm, ones_row, rope_i, masks


SV = {k: i for i, k in enumerate(
    ["bq", "bv", "bo", "ln1w", "ln1b", "ln2w", "ln2b", "w0", "w1", "w2"])}


def pack_smallvecs(ins):
    names = ["L0_attn_bq", "L0_attn_bv", "L0_attn_bo", "L0_ffn_ln1_w",
             "L0_ffn_ln1_b", "L0_ffn_ln2_w", "L0_ffn_ln2_b"]
    rows = [np.asarray(ins[n], np.float32).reshape(C) for n in names]
    cw = np.asarray(ins["L0_ffn_conv_w"], np.float32).reshape(C, 3)
    rows += [np.ascontiguousarray(cw[:, i]) for i in range(3)]
    return np.stack(rows)


class Build:
    def __init__(self, serial=True, upto="H", dbg=()):
        self.serial = serial
        self.upto = upto
        self.dbg = set(dbg)
        self.P = Prog(serial=serial)

    def rAR1(self, lo, hi):
        return _rgn("AR1", lo, hi, 512)

    def rAR1_stripe(self, cc, lo, w):
        a = cc * STRW + 2 + lo
        return self.rAR1(a, a + w)

    def rAR2(self, lo, hi):
        return _rgn("AR2", lo, hi, 512)

    def rQB(self, lo, hi):
        return _rgn("QB", lo, hi, 512)

    def rAR3(self, slot):
        return [f"AR3:{slot}"]

    def ar1(self, cc, lo, w):
        return self.AR1[:, cc * STRW + 2 + lo: cc * STRW + 2 + lo + w]

    def psA(self, b, w=512):
        return self.PS_A[:, b * 512: b * 512 + w]

    def psB(self, b, w=512):
        return self.PS_B[:, b * 512: b * 512 + w]

    def col(self, vname, cc):
        i = SV[vname] * 8 + cc
        return self.COLS[:, i:i + 1]

    def sreg(self, ap, sub):
        return f"{ap.tensor.name}:{sub}"

    def build(self):
        nc = bass.Bass("TRN2", target_bir_lowering=False)
        self.nc = nc
        PH = "ABCDEFGH"
        on = {p: PH.index(p) <= PH.index(self.upto) for p in PH}

        din = lambda name, shape, dt: nc.dram_tensor(name, shape, dt, kind="ExternalInput").ap()

        def scratch(name, shape, dt):
            kind = "ExternalOutput" if name in self.dbg else "Internal"
            return nc.dram_tensor(name, shape, dt, kind=kind).ap()

        self.x_in = din("x", [T, C], F32)
        self.cos_o = din("cos_o", [1024, HD], F32)
        self.sin_o = din("sin_o", [1024, HD], F32)
        self.wq = din("wq", [C, C], F32)
        self.wk = din("wk", [C, C], F32)
        self.wv = din("wv", [C, C], F32)
        self.wo = din("wo", [C, C], F32)
        self.attn_w = din("attn_w", [C, 3 * C], F32)
        self.proj_w = din("proj_w", [C, C], F32)
        self.router_w = din("router_w", [C, NE], F32)
        self.ewv = din("ewv", [NE, C, 2 * DFF], F32)
        self.eproj = din("eproj", [NE, DFF, C], F32)
        self.smallv = din("smallv", [10, C], F32)
        self.c_ident = din("c_ident", [128, 128], BF16)
        self.c_identf = din("c_identf", [128, 128], F32)
        self.c_perm = din("c_perm", [128, 128], BF16)
        self.c_ones = din("c_ones", [1, 128], BF16)
        self.c_ropei = din("c_ropei", [128, 2 * T], BF16)
        self.c_masks = din("c_masks", [128, 4 * 512], BF16)

        self.y_out = nc.dram_tensor("y", [T, C], F32, kind="ExternalOutput").ap()

        self.s_ln0T = scratch("s_ln0T", [C, T], BF16)
        self.s_qT = scratch("s_qT", [C, T], BF16)
        self.s_aoT = scratch("s_aoT", [C, T], BF16)
        self.s_x1 = scratch("s_x1", [T, C], F32)
        self.s_fin = scratch("s_fin", [T, C], F32)
        self.s_hcT = scratch("s_hcT", [C, T], BF16)
        self.s_qiT = scratch("s_qiT", [C, T], BF16)
        self.s_aoiT = scratch("s_aoiT", [C, T], BF16)
        self.s_a = scratch("s_a", [T, C], F32)
        self.s_h2T = scratch("s_h2T", [C, T], BF16)
        self.s_ropeo = scratch("s_ropeo", [128, 2048], BF16)
        self.s_wvbf = scratch("s_wvbf", [NE, 16, 128, 4096], BF16)
        self.s_pjbf = scratch("s_pjbf", [NE, 8, 128, 4096], BF16)
        self.s_hT = scratch("s_hT", [C, T], BF16) if "s_hT" in self.dbg else None

        from contextlib import ExitStack
        ctx = self.ctx = ExitStack()
        sb = lambda name, shape, dt: ctx.enter_context(nc.sbuf_tensor(name, shape, dt))
        pst = lambda name, shape, dt: ctx.enter_context(nc.psum_tensor(name, shape, dt))

        self.AR1 = sb("AR1", [128, NCC * STRW], BF16)
        self.AR2 = sb("AR2", [128, 33280], BF16)
        self.AR3 = sb("AR3", [128, 3072], F32)
        self.WBUF = sb("WBUF", [128, 8192], BF16)
        self.QBUF = sb("QBUF", [128, 8192], BF16)
        self.MSK = sb("MSK", [128, 2048], BF16)
        self.BVB = self.MSK
        self._msk_loaded = False
        self._conv_jobs = self._make_conv_jobs()
        self.IDENT = sb("IDENT", [128, 128], BF16)
        self.IDENTF = sb("IDENTF", [128, 128], F32)
        self.PERM = sb("PERM", [128, 128], BF16)
        self.ONES = sb("ONES", [1, 128], BF16)
        self.COLS = sb("COLS", [128, 80], F32)
        self.PB = sb("PB", [128, 4 * 512], BF16)
        self.SAO = sb("SAO", [128, 2 * 512], BF16)
        self.RECB = sb("RECB", [128, 2 * 512], BF16)
        self.RROW = sb("RROW", [1, 2 * 512], BF16)
        self.TOK = sb("TOK", [128, 2 * C], BF16)
        self.STAT = sb("STAT", [128, 32], F32)
        self.MCOL = sb("MCOL", [128, 128], F32)
        self.SVTMP = sb("SVTMP", [80, 128], F32)
        self.ROPO = sb("ROPO", [128, 64], BF16)
        self.SILU = sb("SILU", [128, C], BF16)
        self.JUNK = self.SILU
        self.RWSB = sb("RWSB", [128, 32], BF16)
        self.STCM = sb("STCM", [128, 5120], BF16)

        self.PS_A = pst("PS_A", [128, 2048], F32)
        self.PS_B = pst("PS_B", [128, 1024], F32)
        self.PS_C = pst("PS_C", [128, 2048], BF16)

        self.setup_consts()
        if on["A"]:
            self.phase_A()
        if on["B"]:
            self.phase_B()
        if on["C"]:
            self.attention(self.s_qT, self.s_aoT, causal=False)
        if on["D"]:
            self.proj_ln_phase(which="D")
        if on["E"]:
            self.phase_E()
        if on["F"]:
            self.attention(self.s_qiT, self.s_aoiT, causal=True)
        if on["G"]:
            self.proj_ln_phase(which="G")
        if on["H"]:
            self.phase_H()

        eng_sems = {e: ctx.enter_context(nc.semaphore(f"se_{e}")) for e in ENGS}
        dma_sems = {q: [ctx.enter_context(nc.semaphore(f"sd_{q}{s}"))
                        for s in range(N_SLOTS)] for q in DMA_QUEUES}
        block = ctx.enter_context(nc.Block())
        per_eng = self.P.realize(eng_sems, dma_sems)

        @block.tensor
        def _(eng):
            for cl in per_eng["tensor"]:
                cl(eng)

        @block.vector
        def _(eng):
            for cl in per_eng["vector"]:
                cl(eng)

        @block.scalar
        def _(eng):
            for cl in per_eng["scalar"]:
                cl(eng)

        @block.gpsimd
        def _(eng):
            for cl in per_eng["gpsimd"]:
                cl(eng)

        @block.sync
        def _(eng):
            for cl in per_eng["sync"]:
                cl(eng)

        ctx.close()
        return nc

    def setup_consts(self):
        E = self.P.emit
        S = self
        E("sync", lambda g: g.dma_start(S.IDENT[:, :], S.c_ident), writes=["IDENT"], dma=True)
        E("sync", lambda g: g.dma_start(S.IDENTF[:, :], S.c_identf), writes=["IDENTF"], dma=True)
        E("sync", lambda g: g.dma_start(S.PERM[:, :], S.c_perm), writes=["PERM"], dma=True)
        E("sync", lambda g: g.dma_start(S.ONES[:, :], S.c_ones), writes=["ONES"], dma=True)
        E("sync", lambda g: g.dma_start(
            S.SVTMP[:, :], S.smallv.rearrange("v (cc p) -> (v cc) p", p=128)),
          writes=["SVTMP"], dma=True)
        E("tensor", lambda t: t.transpose(S.PS_A[:, 0:80], S.SVTMP[:, :], S.IDENTF[0:80, 0:80]),
          reads=["SVTMP", "IDENTF"], writes=["psA:0"])
        E("vector", lambda v: v.tensor_copy(S.COLS[:, :], S.PS_A[:, 0:80]),
          reads=["psA:0"], writes=["COLS"])
        E("sync", lambda g: g.dma_start(S.AR3[0:1, 0:1024],
                                        S.smallv[SV["bv"]:SV["bv"] + 1, :]),
          writes=["AR3:0"], dma=True)
        E("vector", lambda v: v.tensor_copy(S.TOK[0:1, 0:1024], S.AR3[0:1, 0:1024]),
          reads=["AR3:0"], writes=["TOK:0"])
        for h in range(2):
            E("tensor", lambda t, hh=h: t.matmul(
                S.psA(hh), S.ONES[0:1, 0:128], S.TOK[0:1, hh * 512:(hh + 1) * 512],
                start=True, stop=True),
              reads=["TOK:0", "ONES"], writes=[f"psA:{h}"])
        E("vector", lambda v: v.tensor_copy(S.BVB[:, 0:1024], S.PS_A[:, 0:1024]),
          reads=["psA:0", "psA:1"], writes=["MSK:0", "MSK:1"])
        for w, src in ((0, S.cos_o), (1, S.sin_o)):
            for i in range(8):
                E("gpsimd", lambda g, s=src, ii=i: g.dma_start(
                    S.ROPO[:, 0:64], s[ii * 128:(ii + 1) * 128, :]),
                  writes=["ROPO"], dma=True)
                E("tensor", lambda t: t.transpose(S.PS_C[0:64, 0:128],
                                                  S.ROPO[:, 0:64], S.IDENT[:, :]),
                  reads=["ROPO", "IDENT"], writes=["psC:0"])
                E("vector", lambda v, ww=w, ii=i: v.tensor_copy(
                    S.TOK[0:64, ww * 1024 + ii * 128: ww * 1024 + (ii + 1) * 128],
                    S.PS_C[0:64, 0:128]),
                  reads=["psC:0"], writes=[f"TOK:{w}"])
        E("vector", lambda v: v.tensor_scalar_mul(S.TOK[0:32, 1024:2048],
                                                  S.TOK[0:32, 1024:2048], -1.0),
          reads=["TOK:1"], writes=["TOK:1"])
        for r in range(2):
            E("gpsimd", lambda g, rr=r: g.dma_start(
                S.s_ropeo[rr * 64:(rr + 1) * 64, :], S.TOK[0:64, 0:2048]),
              reads=["TOK:0", "TOK:1"], writes=["s_ropeo"], dma=True)

    def ln_stats(self, src_ap, src_regions, o):
        E = self.P.emit
        S = self
        E("vector", lambda v: v.reduce_sum(S.STAT[:, o + 2:o + 3], src_ap, AX.X),
          reads=list(src_regions), writes=["STAT"])
        E("scalar", lambda s: s.activation(S.JUNK[:, :], src_ap, AF.Square,
                                           accum_out=S.STAT[:, o + 3:o + 4]),
          reads=list(src_regions), writes=["SILU", "STAT"])
        E("vector", lambda v: v.tensor_scalar_mul(S.STAT[:, o:o + 1],
                                                  S.STAT[:, o + 2:o + 3], 1.0 / C),
          reads=["STAT"], writes=["STAT"])
        E("vector", lambda v: v.tensor_scalar_mul(S.STAT[:, o + 4:o + 5],
                                                  S.STAT[:, o + 3:o + 4], 1.0 / C),
          reads=["STAT"], writes=["STAT"])
        E("vector", lambda v: v.scalar_tensor_tensor(
            S.STAT[:, o + 5:o + 6], S.STAT[:, o:o + 1], S.STAT[:, o:o + 1],
            S.STAT[:, o + 4:o + 5], ALU.mult, ALU.subtract),
          reads=["STAT"], writes=["STAT"])
        E("vector", lambda v: v.tensor_scalar(S.STAT[:, o + 5:o + 6],
                                              S.STAT[:, o + 5:o + 6], -1.0, EPS,
                                              ALU.mult, ALU.add),
          reads=["STAT"], writes=["STAT"])
        E("scalar", lambda s: s.activation(S.STAT[:, o + 6:o + 7],
                                           S.STAT[:, o + 5:o + 6], AF.Sqrt),
          reads=["STAT"], writes=["STAT"])
        E("vector", lambda v: v.reciprocal(S.STAT[:, o + 1:o + 2],
                                           S.STAT[:, o + 6:o + 7]),
          reads=["STAT"], writes=["STAT"])

    def phase_A(self):
        E = self.P.emit
        S = self
        xs = S.AR3[:, 0:1024]
        for tt in range(NTT):
            par = tt % 2
            E("sync", lambda g, i=tt: g.dma_start(xs, S.x_in[i * 128:(i + 1) * 128, :]),
              writes=["AR3:0"], dma=True)
            self.ln_stats(xs, ["AR3:0"], 0)
            E("vector", lambda v: v.tensor_scalar(
                S.TOK[:, 0:1024], xs, S.STAT[:, 0:1], S.STAT[:, 1:2],
                ALU.subtract, ALU.mult),
              reads=["AR3:0", "STAT"], writes=["TOK:0"])
            for cc in range(NCC):
                E("tensor", lambda t, c=cc, p=par: t.transpose(
                    S.PS_C[:, p * 1024 + c * 128: p * 1024 + (c + 1) * 128],
                    S.TOK[:, c * 128:(c + 1) * 128], S.IDENT[:, :]),
                  reads=["TOK:0", "IDENT"], writes=[f"psC:{par}"])
            q = tt % 4
            for cc in range(NCC):
                E("vector", lambda v, c=cc, p=par, qq=q: v.tensor_copy(
                    S.QBUF[:, c * 512 + qq * 128: c * 512 + qq * 128 + 128],
                    S.PS_C[:, p * 1024 + c * 128: p * 1024 + (c + 1) * 128]),
                  reads=[f"psC:{par}"], writes=self.rQB(cc * 512, cc * 512 + 512))
            if q == 3:
                tk = tt // 4
                E("gpsimd", lambda g, k=tk: g.dma_start(
                    S.s_ln0T[:, k * 512:(k + 1) * 512].rearrange(
                        "(cc p) n -> p cc n", p=128),
                    S.QBUF[:, 0:4096].rearrange("p (cc n) -> p cc n", cc=NCC)),
                  reads=self.rQB(0, 4096),
                  writes=[self.sreg(S.s_ln0T, tk)], dma=True)

    def qk_sweep(self, w_dram, w_cols, srcT, dstT, to_ar1, rope_tbl_off,
                 rope_tks, bias_name):
        E = self.P.emit
        S = self
        lo, hi = w_cols
        E("gpsimd", lambda g: g.dma_start(
            S.WBUF[:, :].rearrange("p (cc n) -> p cc n", cc=NCC),
            w_dram[:, lo:hi].rearrange("(cc p) n -> p cc n", p=128)),
          writes=["WBUF"], dma=True)
        for tk in range(NTK):
            spar = tk % 2
            slab = S.QBUF[:, spar * 4096:(spar + 1) * 4096]
            sregs = self.rQB(spar * 4096, (spar + 1) * 4096)
            E("sync", lambda g, k=tk, sl=slab: g.dma_start(
                sl.rearrange("p (cc n) -> p cc n", cc=NCC),
                srcT[:, k * 512:(k + 1) * 512].rearrange(
                    "(cc p) n -> p cc n", p=128)),
              reads=[self.sreg(srcT, tk)], writes=sregs, dma=True)
            for ccp in range(NCC):
                bank = ccp % 2
                for cc in range(NCC):
                    E("tensor", lambda t, c=cc, cp=ccp, b=bank, sl=slab: t.matmul(
                        S.psA(b),
                        S.WBUF[:, c * 1024 + cp * 128: c * 1024 + (cp + 1) * 128],
                        sl[:, c * 512:(c + 1) * 512],
                        start=(c == 0), stop=(c == NCC - 1)),
                      reads=["WBUF"] + sregs,
                      writes=[f"psA:{bank}"])
                if to_ar1:
                    out_ap = self.ar1(ccp, tk * 512, 512)
                    out_reg = self.rAR1_stripe(ccp, tk * 512, 512)
                else:
                    po = (tk % 2) * 512
                    out_ap = S.PB[:, po:po + 512]
                    out_reg = [f"PB:{tk % 2}"]
                if bias_name is not None:
                    E("scalar", lambda s, b=bank, o=out_ap, c=ccp: s.activation(
                        o, S.psA(b), AF.Identity, bias=S.col(bias_name, c)),
                      reads=[f"psA:{bank}", "COLS"], writes=out_reg)
                else:
                    E("scalar", lambda s, b=bank, o=out_ap: s.activation(
                        o, S.psA(b), AF.Copy),
                      reads=[f"psA:{bank}"], writes=out_reg)
                if rope_tbl_off is not None and tk in rope_tks:
                    co, so = rope_tbl_off(tk)
                    pb2o = 1024 + (tk % 2) * 512
                    pb2 = S.PB[:, pb2o:pb2o + 512]
                    pb2r = [f"PB:{2 + tk % 2}"]
                    bank2 = ccp % 2
                    E("tensor", lambda t, o=out_ap, b=bank2: t.matmul(
                        S.psB(b), S.PERM[:, :], o, start=True, stop=True),
                      reads=out_reg + ["PERM"], writes=[f"psB:{bank2}"])
                    E("vector", lambda v, b=bank2, p=pb2, s=so: v.tensor_mul(
                        p, S.psB(b), S.AR2[:, s:s + 512]),
                      reads=[f"psB:{bank2}"] + self.rAR2(so, so + 512),
                      writes=pb2r)
                    E("vector", lambda v, o=out_ap, c=co: v.tensor_mul(
                        S.RECB[:, 0:512], o, S.AR2[:, c:c + 512]),
                      reads=out_reg + self.rAR2(co, co + 512), writes=["RECB:0"])
                    E("vector", lambda v, o=out_ap, p=pb2: v.tensor_add(
                        o, S.RECB[:, 0:512], p),
                      reads=["RECB:0"] + pb2r, writes=out_reg)
                if not to_ar1:
                    E("gpsimd", lambda g, o=out_ap, c=ccp, k=tk: g.dma_start(
                        dstT[c * 128:(c + 1) * 128, k * 512:(k + 1) * 512], o),
                      reads=out_reg, writes=[self.sreg(dstT, tk)], dma=True)

    def v_sweep(self, w_dram, w_cols, srcT, with_bias):
        E = self.P.emit
        S = self
        lo, hi = w_cols
        E("gpsimd", lambda g: g.dma_start(
            S.WBUF[:, :].rearrange("p (cc n) -> p cc n", cc=NCC),
            w_dram[:, lo:hi].rearrange("(cc p) n -> p cc n", p=128)),
          writes=["WBUF"], dma=True)
        E("vector", lambda v: v.memset(S.AR2[:, 64:33280:65], 1.0),
          writes=self.rAR2(0, 33280))
        for tt in range(NTT):
            tk, q = tt // 4, tt % 4
            spar = tk % 2
            slab = S.QBUF[:, spar * 4096:(spar + 1) * 4096]
            sregs = self.rQB(spar * 4096, (spar + 1) * 4096)
            if q == 0:
                E("sync", lambda g, k=tk, sl=slab: g.dma_start(
                    sl.rearrange("p (cc n) -> p cc n", cc=NCC),
                    srcT[:, k * 512:(k + 1) * 512].rearrange(
                        "(cc p) n -> p cc n", p=128)),
                  reads=[self.sreg(srcT, tk)], writes=sregs, dma=True)
            for half in range(2):
                bank = 2 + half
                for cc in range(NCC):
                    E("tensor", lambda t, c=cc, h=half, b=bank, qq=q, sl=slab: t.matmul(
                        S.psA(b),
                        sl[:, c * 512 + qq * 128: c * 512 + qq * 128 + 128],
                        S.WBUF[:, c * 1024 + h * 512: c * 1024 + (h + 1) * 512],
                        start=(c == 0), stop=(c == NCC - 1)),
                      reads=["WBUF"] + sregs,
                      writes=[f"psA:{bank}"])
            for h in range(NH):
                bank = 2 + (h // 8)
                off = 1024 + (h % 8) * 64 + (h // 8) * 512
                pos = (tt * NH + h) * 65
                dst = S.AR2[:, pos:pos + 64]
                regs = self.rAR2(pos, pos + 64)
                if with_bias:
                    E("vector", lambda v, d=dst, o=off, hh=h: v.scalar_tensor_tensor(
                        d, S.PS_A[:, o:o + 64], 1.0, S.BVB[:, hh * 64:(hh + 1) * 64],
                        ALU.mult, ALU.add),
                      reads=[f"psA:{bank}", "MSK:0", "MSK:1"], writes=regs)
                else:
                    E("vector", lambda v, d=dst, o=off: v.tensor_copy(
                        d, S.PS_A[:, o:o + 64]),
                      reads=[f"psA:{bank}"], writes=regs)

    def phase_B(self):
        E = self.P.emit
        S = self
        E("sync", lambda g: g.dma_start(S.AR2[:, 0:2048], S.s_ropeo),
          reads=["s_ropeo"], writes=self.rAR2(0, 2048), dma=True)
        rope_off = lambda tk: ((tk - 6) * 512, 1024 + (tk - 6) * 512)
        self.qk_sweep(S.wq, (0, C), S.s_ln0T, S.s_qT, False, rope_off, (6, 7), "bq")
        self.qk_sweep(S.wk, (0, C), S.s_ln0T, None, True, rope_off, (6, 7), None)
        self.v_sweep(S.wv, (0, C), S.s_ln0T, True)

    def phase_E(self):
        E = self.P.emit
        S = self
        for cc in range(NCC):
            E("vector", lambda v, c=cc: v.memset(
                S.AR1[:, c * STRW: c * STRW + 2], 0.0),
              writes=self.rAR1(cc * STRW, cc * STRW + 2))
        for cc in range(NCC):
            for tk in range(NTK):
                rd = self.rAR1(cc * STRW + tk * 512, cc * STRW + 2 + (tk + 1) * 512)
                E("vector", lambda v, c=cc, k=tk: v.tensor_scalar_mul(
                    S.TOK[:, 0:512],
                    S.AR1[:, c * STRW + k * 512: c * STRW + k * 512 + 512],
                    S.col("w0", c)),
                  reads=rd + ["COLS"], writes=["TOK:0"])
                E("vector", lambda v, c=cc, k=tk: v.scalar_tensor_tensor(
                    S.TOK[:, 512:1024],
                    S.AR1[:, c * STRW + 1 + k * 512: c * STRW + 1 + k * 512 + 512],
                    S.col("w1", c), S.TOK[:, 0:512], ALU.mult, ALU.add),
                  reads=rd + ["COLS", "TOK:0"], writes=["TOK:0"])
                E("vector", lambda v, c=cc, k=tk: v.scalar_tensor_tensor(
                    S.TOK[:, 0:512], self.ar1(c, k * 512, 512),
                    S.col("w2", c), S.TOK[:, 512:1024], ALU.mult, ALU.add),
                  reads=rd + ["COLS", "TOK:0"], writes=["TOK:0"])
                E("gpsimd", lambda g, c=cc, k=tk: g.dma_start(
                    S.s_hcT[c * 128:(c + 1) * 128, k * 512:(k + 1) * 512],
                    S.TOK[:, 0:512]),
                  reads=["TOK:0"], writes=[self.sreg(S.s_hcT, tk)], dma=True)
        E("sync", lambda g: g.dma_start(S.AR2[:, 0:4096], S.c_ropei[:, 0:4096]),
          writes=self.rAR2(0, 4096), dma=True)
        E("sync", lambda g: g.dma_start(S.AR2[:, 4096:8192], S.c_ropei[:, 4096:8192]),
          writes=self.rAR2(4096, 8192), dma=True)
        rope_off = lambda tk: (tk * 512, 4096 + tk * 512)
        self.qk_sweep(S.attn_w, (0, C), S.s_hcT, S.s_qiT, False, rope_off,
                      tuple(range(8)), None)
        self.qk_sweep(S.attn_w, (C, 2 * C), S.s_hcT, None, True, rope_off,
                      tuple(range(8)), None)
        self.v_sweep(S.attn_w, (2 * C, 3 * C), S.s_hcT, False)


    def _make_conv_jobs(self):
        S = self
        jobs = []
        for e in range(NE):
            for g2 in range(16):
                def j(E, ee=e, gg=g2):
                    E("gpsimd", lambda g: g.dma_start(
                        S.s_wvbf[ee, gg].rearrange("p (gv cc j) -> p gv cc j",
                                                   gv=2, cc=NCC),
                        S.ewv[ee].rearrange(
                            "(cc p) (gv gr j) -> gr p gv cc j",
                            p=128, gv=2, j=256)[gg]),
                      writes=["s_wvbf"], dma=True)
                jobs.append(j)
            for j4 in range(8):
                def j(E, ee=e, jj4=j4):
                    E("gpsimd", lambda g: g.dma_start(
                        S.s_pjbf[ee, jj4].rearrange("p (jj n) -> p jj n", jj=4),
                        S.eproj[ee, jj4 * 512:(jj4 + 1) * 512, :].rearrange(
                            "(jj p) n -> p jj n", p=128)),
                      writes=["s_pjbf"], dma=True)
                jobs.append(j)
        return jobs

    def drain_conv_jobs(self, k):
        for _ in range(k):
            if self._conv_jobs:
                self._conv_jobs.pop(0)(self.P.emit)

    def attention(self, qT_dram, aoT_dram, causal):
        E = self.P.emit
        S = self
        if causal and not self._msk_loaded:
            self._msk_loaded = True
            E("sync", lambda g: g.dma_start(S.MSK[:, :], S.c_masks),
              writes=["MSK:0", "MSK:1"], dma=True)
        pending = None

        def emit_scores(h2, qc, kt, qregs, qbuf):
            for hh in range(2):
                sb_ = 2 * hh + (kt % 2)
                po = sb_ * 512
                preg = [f"PB:{sb_}"]
                kregs = self.rAR1_stripe(h2, kt * 128, 128)
                E("tensor", lambda t, p=h2, k=kt, b=sb_, r=hh, qb=qbuf,
                  q=qc: t.matmul(
                    S.psA(b),
                    S.AR1[r * 64:r * 64 + 64,
                          p * STRW + 2 + k * 128: p * STRW + 2 + k * 128 + 128],
                    qb[r * 64:r * 64 + 64, q * 512:(q + 1) * 512],
                    start=True, stop=True, tile_position=(r * 64, 0)),
                  reads=kregs + qregs, writes=[f"psA:{sb_}"])
                E("scalar", lambda s, b=sb_, o=po: s.activation(
                    S.PB[:, o:o + 512], S.psA(b), AF.Exp, scale=SCALE),
                  reads=[f"psA:{sb_}"], writes=preg)

        def emit_mask(qc, kt, n_kt):
            if not (causal and kt >= n_kt - 4):
                return
            bnd = kt - (n_kt - 4)
            for hh in range(2):
                po = (2 * hh + (kt % 2)) * 512
                preg = [f"PB:{2 * hh + (kt % 2)}"]
                E("vector", lambda v, o=po, bb=bnd: v.tensor_mul(
                    S.PB[:, o:o + 512], S.PB[:, o:o + 512],
                    S.MSK[:, bb * 512:(bb + 1) * 512]),
                  reads=preg + [f"MSK:{bnd // 2}"], writes=preg)

        def emit_pv(h2, qc, kt):
            for hh in range(2):
                h = 2 * h2 + hh
                po = (2 * hh + (kt % 2)) * 512
                preg = [f"PB:{2 * hh + (kt % 2)}"]
                pos = (kt * NH + h) * 65
                E("tensor", lambda t, o=po, hh_=hh, ps=pos,
                  first=(kt == 0): t.matmul(
                    S.PS_B[0:65, hh_ * 512:hh_ * 512 + 512],
                    S.AR2[:, ps:ps + 65], S.PB[:, o:o + 512],
                    start=first, stop=True),
                  reads=preg + self.rAR2(pos, pos + 65),
                  writes=[f"psB:{hh}"])

        def emit_stage(qc):
            p = qc % 2
            for hh in range(2):
                E("vector", lambda v, o=hh, pp=p: v.tensor_copy(
                    S.STCM[0:65, pp * 1024 + o * 512: pp * 1024 + (o + 1) * 512],
                    S.PS_B[0:65, o * 512:o * 512 + 512]),
                  reads=[f"psB:{hh}"],
                  writes=[f"STCM:{2 * p + hh}"])

        def make_evac(h2, qc):
            sao_par = qc % 2
            sao_reg = [f"SAO:{sao_par}"]
            p = qc % 2

            def emit_evac():
                for hh in range(2):
                    sreg_ = [f"STCM:{2 * p + hh}"]
                    E("vector", lambda v, o=hh, pp=p: v.reciprocal(
                        S.AR3[0:1, o * 512:(o + 1) * 512],
                        S.STCM[64:65, pp * 1024 + o * 512: pp * 1024 + (o + 1) * 512]),
                      reads=sreg_, writes=["AR3:0"])
                    E("vector", lambda v, o=hh: v.tensor_copy(
                        S.RROW[0:1, o * 512:(o + 1) * 512],
                        S.AR3[0:1, o * 512:(o + 1) * 512]),
                      reads=["AR3:0"], writes=["RROW"])
                    E("tensor", lambda t, o=hh: t.matmul(
                        S.PS_A[0:64, 2 * o * 512: 2 * o * 512 + 512],
                        S.ONES[0:1, 0:64],
                        S.RROW[0:1, o * 512:(o + 1) * 512], start=True, stop=True),
                      reads=["RROW", "ONES"], writes=[f"psA:{2 * hh}"])
                    E("vector", lambda v, o=hh: v.tensor_copy(
                        S.RECB[0:64, o * 512:(o + 1) * 512],
                        S.PS_A[0:64, 2 * o * 512: 2 * o * 512 + 512]),
                      reads=[f"psA:{2 * hh}"], writes=[f"RECB:{hh}"])
                    E("vector", lambda v, o=hh, sp=sao_par, pp=p: v.tensor_mul(
                        S.SAO[o * 64:(o + 1) * 64, sp * 512:(sp + 1) * 512],
                        S.STCM[0:64, pp * 1024 + o * 512: pp * 1024 + (o + 1) * 512],
                        S.RECB[0:64, o * 512:(o + 1) * 512]),
                      reads=sreg_ + [f"RECB:{hh}"], writes=sao_reg)
                E("gpsimd", lambda g, p2=h2, q=qc, sp=sao_par: g.dma_start(
                    aoT_dram[p2 * 128:(p2 + 1) * 128, q * 512:(q + 1) * 512],
                    S.SAO[:, sp * 512:(sp + 1) * 512]),
                  reads=sao_reg, writes=[self.sreg(aoT_dram, qc)], dma=True)
            return emit_evac

        for h2 in range(NH // 2):
            par = h2 % 2
            qbuf = S.QBUF[:, par * 4096:(par + 1) * 4096]
            qregs = self.rQB(par * 4096, (par + 1) * 4096)
            E("sync", lambda g, p=h2, qb=qbuf: g.dma_start(
                qb, qT_dram[p * 128:(p + 1) * 128, :]),
              reads=[self.sreg(qT_dram, k) for k in range(8)],
              writes=qregs, dma=True)
            for qc in range(NTK):
                n_kt = (4 * qc + 4) if causal else 32
                for kt in range(n_kt):
                    emit_scores(h2, qc, kt, qregs, qbuf)
                    emit_mask(qc, kt, n_kt)
                    if kt == 1 and pending is not None:
                        pending()
                        pending = None
                    if kt >= 1:
                        emit_pv(h2, qc, kt - 1)
                if pending is not None:
                    pending()
                    pending = None
                emit_pv(h2, qc, n_kt - 1)
                emit_stage(qc)
                pending = make_evac(h2, qc)
                self.drain_conv_jobs(1)
        pending()

    def proj_ln_phase(self, which):
        E = self.P.emit
        S = self
        is_d = which == "D"
        w_dram = S.wo if is_d else S.proj_w
        aoT = S.s_aoT if is_d else S.s_aoiT
        x_src = S.x_in if is_d else S.s_fin
        x1_dst = S.s_x1 if is_d else S.s_a
        E("gpsimd", lambda g: g.dma_start(
            S.WBUF[:, :].rearrange("p (cc n) -> p cc n", cc=NCC),
            w_dram.rearrange("(cc p) n -> p cc n", p=128)),
          writes=["WBUF"], dma=True)
        if not is_d:
            E("gpsimd", lambda g: g.dma_start(
                S.RWSB[:, :].rearrange("p (cc e) -> p cc e", cc=NCC),
                S.router_w.rearrange("(cc p) e -> p cc e", p=128)),
              writes=["RWSB"], dma=True)
        for tk in range(NTK):
            spar = tk % 2
            slab = S.QBUF[:, spar * 4096:(spar + 1) * 4096]
            sregs = self.rQB(spar * 4096, (spar + 1) * 4096)
            E("sync", lambda g, k=tk, sl=slab: g.dma_start(
                sl.rearrange("p (cc n) -> p cc n", cc=NCC),
                aoT[:, k * 512:(k + 1) * 512].rearrange(
                    "(cc p) n -> p cc n", p=128)),
              reads=[self.sreg(aoT, tk)], writes=sregs, dma=True)
            for ccp in range(NCC):
                bank = ccp % 2
                for cc in range(NCC):
                    E("tensor", lambda t, c=cc, cp=ccp, b=bank, sl=slab: t.matmul(
                        S.psA(b),
                        S.WBUF[:, c * 1024 + cp * 128: c * 1024 + (cp + 1) * 128],
                        sl[:, c * 512:(c + 1) * 512],
                        start=(c == 0), stop=(c == NCC - 1)),
                      reads=["WBUF"] + sregs,
                      writes=[f"psA:{bank}"])
                dst = S.STCM[:, ccp * 512:(ccp + 1) * 512]
                sreg = [f"STCM:{ccp}"]
                if is_d:
                    E("scalar", lambda s, b=bank, d=dst, c=ccp: s.activation(
                        d, S.psA(b), AF.Identity, bias=S.col("bo", c)),
                      reads=[f"psA:{bank}", "COLS"], writes=sreg)
                else:
                    E("scalar", lambda s, b=bank, d=dst: s.activation(
                        d, S.psA(b), AF.Copy),
                      reads=[f"psA:{bank}"], writes=sreg)
            for q in range(4):
                tt = tk * 4 + q
                parc = tt % 2
                for cc in range(NCC):
                    E("tensor", lambda t, c=cc, p=parc, qq=q: t.transpose(
                        S.PS_C[:, p * 1024 + c * 128: p * 1024 + (c + 1) * 128],
                        S.STCM[:, c * 512 + qq * 128: c * 512 + qq * 128 + 128],
                        S.IDENT[:, :]),
                      reads=[f"STCM:{cc}", "IDENT"], writes=[f"psC:{parc}"])
                E("vector", lambda v, p=parc: v.tensor_copy(
                    S.TOK[:, 0:1024], S.PS_C[:, p * 1024:(p + 1) * 1024]),
                  reads=[f"psC:{parc}"], writes=["TOK:0"])
                E("sync", lambda g, i=tt: g.dma_start(
                    S.AR3[:, 0:1024], x_src[i * 128:(i + 1) * 128, :]),
                  reads=[self.sreg(x_src, "t")] if not is_d else [],
                  writes=["AR3:0"], dma=True)
                if is_d:
                    self.ln_stats(S.TOK[:, 0:1024], ["TOK:0"], 0)
                    E("vector", lambda v: v.tensor_scalar(
                        S.AR3[:, 1024:2048], S.TOK[:, 0:1024],
                        S.STAT[:, 0:1], S.STAT[:, 1:2], ALU.subtract, ALU.mult),
                      reads=["TOK:0", "STAT"], writes=["AR3:1"])
                    E("vector", lambda v: v.tensor_add(
                        S.AR3[:, 1024:2048], S.AR3[:, 1024:2048], S.AR3[:, 0:1024]),
                      reads=["AR3:1", "AR3:0"], writes=["AR3:1"])
                else:
                    E("vector", lambda v: v.tensor_add(
                        S.AR3[:, 1024:2048], S.TOK[:, 0:1024], S.AR3[:, 0:1024]),
                      reads=["TOK:0", "AR3:0"], writes=["AR3:1"])
                E("gpsimd", lambda g, i=tt: g.dma_start(
                    x1_dst[i * 128:(i + 1) * 128, :], S.AR3[:, 1024:2048]),
                  reads=["AR3:1"], writes=[self.sreg(x1_dst, "t")], dma=True)
                self.ln_stats(S.AR3[:, 1024:2048], ["AR3:1"], 8)
                E("vector", lambda v: v.tensor_scalar(
                    S.AR3[:, 2048:3072], S.AR3[:, 1024:2048],
                    S.STAT[:, 8:9], S.STAT[:, 9:10], ALU.subtract, ALU.mult),
                  reads=["AR3:1", "STAT"], writes=["AR3:2"])
                if is_d:
                    E("gpsimd", lambda g, i=tt: g.dma_start(
                        S.s_fin[i * 128:(i + 1) * 128, :], S.AR3[:, 2048:3072]),
                      reads=["AR3:2"], writes=[self.sreg(S.s_fin, "t")], dma=True)
                E("vector", lambda v: v.tensor_copy(
                    S.TOK[:, 1024:2048], S.AR3[:, 2048:3072]),
                  reads=["AR3:2"], writes=["TOK:1"])
                for cc in range(NCC):
                    E("tensor", lambda t, c=cc, p=parc: t.transpose(
                        S.PS_C[:, p * 1024 + c * 128: p * 1024 + (c + 1) * 128],
                        S.TOK[:, 1024 + c * 128: 1024 + (c + 1) * 128],
                        S.IDENT[:, :]),
                      reads=["TOK:1", "IDENT"], writes=[f"psC:{parc}"])
                lnw = "ln1w" if is_d else "ln2w"
                lnb = "ln1b" if is_d else "ln2b"
                for cc in range(NCC):
                    if is_d:
                        dsth = self.ar1(cc, tt * 128, 128)
                        rr = self.rAR1_stripe(cc, tt * 128, 128)
                    else:
                        dsth = S.STCM[:, 4096 + cc * 128: 4096 + (cc + 1) * 128]
                        rr = [f"STCM:{8 + cc // 4}"]
                    E("vector", lambda v, c=cc, p=parc, d=dsth, rg=rr: v.tensor_scalar(
                        d, S.PS_C[:, p * 1024 + c * 128: p * 1024 + (c + 1) * 128],
                        S.col(lnw, c), S.col(lnb, c), ALU.mult, ALU.add),
                      reads=[f"psC:{parc}", "COLS"], writes=rr)
                if is_d and S.s_hT is not None:
                    for cc in range(NCC):
                        E("gpsimd", lambda g, c=cc, i=tt: g.dma_start(
                            S.s_hT[c * 128:(c + 1) * 128, i * 128:(i + 1) * 128],
                            self.ar1(c, i * 128, 128)),
                          reads=self.rAR1_stripe(cc, tt * 128, 128),
                          writes=["hTdbg"], dma=True)
                if not is_d:
                    for cc in range(NCC):
                        E("gpsimd", lambda g, c=cc, i=tt: g.dma_start(
                            S.s_h2T[c * 128:(c + 1) * 128, i * 128:(i + 1) * 128],
                            S.STCM[:, 4096 + c * 128: 4096 + (c + 1) * 128]),
                          reads=[f"STCM:{8 + cc // 4}"],
                          writes=[self.sreg(S.s_h2T, tt // 4)], dma=True)
                    for cc in range(NCC):
                        E("tensor", lambda t, c=cc: t.matmul(
                            S.PS_B[:, 0:4],
                            S.STCM[:, 4096 + c * 128: 4096 + (c + 1) * 128],
                            S.RWSB[:, c * 4:(c + 1) * 4],
                            start=(c == 0), stop=(c == NCC - 1)),
                          reads=[f"STCM:{8 + cc // 4}", "RWSB"], writes=["psB:0"])
                    E("vector", lambda v: v.reduce_max(
                        S.STAT[:, 16:17], S.PS_B[:, 0:4], AX.X),
                      reads=["psB:0"], writes=["STAT"])
                    E("vector", lambda v: v.tensor_single_scalar(
                        S.STAT[:, 20:24], S.PS_B[:, 0:4], S.STAT[:, 16:17],
                        ALU.is_ge),
                      reads=["psB:0", "STAT"], writes=["STAT"])
                    E("vector", lambda v, i=tt: v.tensor_copy(
                        S.MCOL[:, i * 4:i * 4 + 1], S.STAT[:, 20:21]),
                      reads=["STAT"], writes=["MCOL"])
                    E("vector", lambda v: v.tensor_scalar(
                        S.STAT[:, 17:18], S.STAT[:, 20:21], -1.0, 1.0,
                        ALU.mult, ALU.add),
                      reads=["STAT"], writes=["STAT"])
                    for e in range(1, NE):
                        E("vector", lambda v, i=tt, ee=e: v.tensor_mul(
                            S.MCOL[:, i * 4 + ee:i * 4 + ee + 1],
                            S.STAT[:, 20 + ee:21 + ee], S.STAT[:, 17:18]),
                          reads=["STAT"], writes=["MCOL"])
                        if e < NE - 1:
                            E("vector", lambda v, i=tt, ee=e: v.tensor_sub(
                                S.STAT[:, 17:18], S.STAT[:, 17:18],
                                S.MCOL[:, i * 4 + ee:i * 4 + ee + 1]),
                              reads=["STAT", "MCOL"], writes=["STAT"])

    def phase_H(self):
        E = self.P.emit
        S = self
        WV_OFF = (8192, 12288)
        PJ_OFF = 16384
        for tc in range(NTC):
            E("sync", lambda g, k=tc: g.dma_start(
                S.AR2[:, 0:8192].rearrange("p (cc n) -> p cc n", cc=NCC),
                S.s_h2T[:, k * 1024:(k + 1) * 1024].rearrange(
                    "(cc p) n -> p cc n", p=128)),
              reads=[self.sreg(S.s_h2T, tc * 2), self.sreg(S.s_h2T, tc * 2 + 1)],
              writes=self.rAR2(0, 8192), dma=True)
            for e in range(NE):
                for g2 in range(16):
                    woff = WV_OFF[g2 % 2]
                    wslot = S.AR2[:, woff:woff + 4096]
                    wregs = self.rAR2(woff, woff + 4096)
                    E("sync", lambda g, ee=e, gg=g2, w=wslot: g.dma_start(
                        w, S.s_wvbf[ee, gg]),
                      reads=["s_wvbf"], writes=wregs, dma=True)
                    for jh in range(2):
                        jc = 2 * g2 + jh
                        for half in range(2):
                            for cc in range(NCC):
                                E("tensor", lambda t, c=cc, h=half, w=wslot,
                                  j=jh: t.matmul(
                                    S.psA(h),
                                    w[:, c * 256 + j * 128: c * 256 + j * 128 + 128],
                                    S.AR2[:, c * 1024 + h * 512: c * 1024 + (h + 1) * 512],
                                    start=(c == 0), stop=(c == NCC - 1)),
                                  reads=wregs[:4] + self.rAR2(cc * 1024, (cc + 1) * 1024),
                                  writes=[f"psA:{half}"])
                            for cc in range(NCC):
                                E("tensor", lambda t, c=cc, h=half, w=wslot,
                                  j=jh: t.matmul(
                                    S.psA(2 + h),
                                    w[:, 2048 + c * 256 + j * 128: 2048 + c * 256 + j * 128 + 128],
                                    S.AR2[:, c * 1024 + h * 512: c * 1024 + (h + 1) * 512],
                                    start=(c == 0), stop=(c == NCC - 1)),
                                  reads=wregs[4:] + self.rAR2(cc * 1024, (cc + 1) * 1024),
                                  writes=[f"psA:{2 + half}"])
                        E("scalar", lambda s: s.activation(
                            S.SILU[:, :], S.PS_A[:, 0:1024], AF.Silu),
                          reads=["psA:0", "psA:1"], writes=["SILU"])
                        E("vector", lambda v, j=jc: v.tensor_mul(
                            S.AR1[:, j * 1024:(j + 1) * 1024], S.SILU[:, :],
                            S.PS_A[:, 1024:2048]),
                          reads=["SILU", "psA:2", "psA:3"],
                          writes=self.rAR1(jc * 1024, (jc + 1) * 1024))
                for grp, tts in enumerate(((0, 1, 2), (3, 4, 5), (6, 7))):
                    for jc in range(NJC):
                        if jc % 4 == 0:
                            poff4 = PJ_OFF + ((jc // 4) % 2) * 4096
                            pslot4 = S.AR2[:, poff4:poff4 + 4096]
                            E("sync", lambda g, ee=e, j4=jc // 4,
                              ps=pslot4: g.dma_start(ps, S.s_pjbf[ee, j4]),
                              reads=["s_pjbf"],
                              writes=self.rAR2(poff4, poff4 + 4096), dma=True)
                        poff = PJ_OFF + ((jc // 4) % 2) * 4096 + (jc % 4) * 1024
                        pslot = S.AR2[:, poff:poff + 1024]
                        pregs = self.rAR2(poff, poff + 1024)
                        for i, tt8 in enumerate(tts):
                            for half in range(2):
                                bank = 2 * i + half
                                bs = S.psA(bank) if bank < 4 else S.psB(bank - 4)
                                breg = (f"psA:{bank}" if bank < 4
                                        else f"psB:{bank - 4}")
                                E("tensor", lambda t, j=jc, t8=tt8, h=half, b=bs,
                                  ps=pslot: t.matmul(
                                    b,
                                    S.AR1[:, j * 1024 + t8 * 128: j * 1024 + t8 * 128 + 128],
                                    ps[:, h * 512:(h + 1) * 512],
                                    start=(j == 0), stop=(j == NJC - 1)),
                                  reads=pregs + self.rAR1(jc * 1024 + tt8 * 128,
                                                          jc * 1024 + tt8 * 128 + 128),
                                  writes=[breg])
                    for i, tt8 in enumerate(tts):
                        tile = tc * 8 + tt8
                        if i < 2:
                            src = S.PS_A[:, 2 * i * 512: 2 * i * 512 + 1024]
                            regs = [f"psA:{2 * i}", f"psA:{2 * i + 1}"]
                        else:
                            src = S.PS_B[:, 0:1024]
                            regs = ["psB:0", "psB:1"]
                        dst = S.QBUF[:, tt8 * 1024:(tt8 + 1) * 1024]
                        dregs = self.rQB(tt8 * 1024, (tt8 + 1) * 1024)
                        if e == 0:
                            E("vector", lambda v, s=src, d=dst, ti=tile: v.tensor_scalar_mul(
                                d, s, S.MCOL[:, ti * 4:ti * 4 + 1]),
                              reads=regs + ["MCOL"], writes=dregs)
                        else:
                            E("vector", lambda v, s=src, d=dst, ti=tile,
                              ee=e: v.scalar_tensor_tensor(
                                d, s, S.MCOL[:, ti * 4 + ee:ti * 4 + ee + 1], d,
                                ALU.mult, ALU.add),
                              reads=regs + ["MCOL"] + dregs, writes=dregs)
            for tt8 in range(8):
                tt = tc * 8 + tt8
                E("gpsimd", lambda g, i=tt: g.dma_start(
                    S.AR3[:, 0:1024], S.s_a[i * 128:(i + 1) * 128, :]),
                  reads=[self.sreg(S.s_a, "t")], writes=["AR3:0"], dma=True)
                E("gpsimd", lambda g, i=tt: g.dma_start(
                    S.AR3[:, 1024:2048], S.s_x1[i * 128:(i + 1) * 128, :]),
                  reads=[self.sreg(S.s_x1, "t")], writes=["AR3:1"], dma=True)
                E("vector", lambda v, t8=tt8: v.tensor_add(
                    S.AR3[:, 2048:3072], S.AR3[:, 0:1024],
                    S.QBUF[:, t8 * 1024:(t8 + 1) * 1024]),
                  reads=["AR3:0"] + self.rQB(tt8 * 1024, (tt8 + 1) * 1024),
                  writes=["AR3:2"])
                self.ln_stats(S.AR3[:, 2048:3072], ["AR3:2"], 24)
                E("vector", lambda v: v.tensor_scalar(
                    S.AR3[:, 0:1024], S.AR3[:, 2048:3072],
                    S.STAT[:, 24:25], S.STAT[:, 25:26], ALU.subtract, ALU.mult),
                  reads=["AR3:2", "STAT"], writes=["AR3:0"])
                E("vector", lambda v: v.tensor_add(
                    S.AR3[:, 0:1024], S.AR3[:, 0:1024], S.AR3[:, 1024:2048]),
                  reads=["AR3:0", "AR3:1"], writes=["AR3:0"])
                E("gpsimd", lambda g, i=tt: g.dma_start(
                    S.y_out[i * 128:(i + 1) * 128, :], S.AR3[:, 0:1024]),
                  reads=["AR3:0"], writes=["y_out"], dma=True)


_BUILD_CACHE = {}


def get_nc(serial=True, upto="H", dbg=()):
    key = (serial, upto, tuple(sorted(dbg)))
    if key not in _BUILD_CACHE:
        _BUILD_CACHE[key] = Build(serial=serial, upto=upto, dbg=dbg).build()
    return _BUILD_CACHE[key]


def make_inmaps(inputs, cores=8):
    ident, identf, perm, ones_row, rope_i, masks = host_constants()
    sv = pack_smallvecs(inputs)
    x = np.ascontiguousarray(np.asarray(inputs["L0_attn_hidden_states"], np.float32))
    f32c = lambda k: np.ascontiguousarray(np.asarray(inputs[k], np.float32))
    shared = dict(
        cos_o=f32c("L0_attn_cos"), sin_o=f32c("L0_attn_sin"),
        wq=f32c("L0_attn_wq"), wk=f32c("L0_attn_wk"), wv=f32c("L0_attn_wv"),
        wo=f32c("L0_attn_wo"), attn_w=f32c("L0_ffn_attn_w"),
        proj_w=f32c("L0_ffn_proj_w"), router_w=f32c("L0_ffn_router_w"),
        ewv=f32c("L0_ffn_expert_wv"), eproj=f32c("L0_ffn_expert_proj"),
        smallv=sv, c_ident=ident, c_identf=identf, c_perm=perm,
        c_ones=ones_row, c_ropei=rope_i, c_masks=masks,
    )
    return [dict(shared, x=np.ascontiguousarray(x[b % x.shape[0]]))
            for b in range(cores)]


def kernel(**inputs):
    nc = get_nc(serial=False, upto="H", dbg=())
    in_maps = make_inmaps(inputs, cores=8)
    res = run_bass_kernel_spmd(nc, in_maps, core_ids=list(range(8)))
    return np.stack([np.asarray(r["y"], np.float32) for r in res.results])


# version 06
# speedup vs baseline: 1.0246x; 1.0246x over previous
import numpy as np
import ml_dtypes
import concourse.bass as bass
import concourse.mybir as mybir
from concourse.bass_utils import run_bass_kernel_spmd

F32 = mybir.dt.float32
BF16 = mybir.dt.bfloat16
AF = mybir.ActivationFunctionType
ALU = mybir.AluOpType
AX = mybir.AxisListType
bfnp = ml_dtypes.bfloat16

T = 4096
C = 1024
NH = 16
HD = 64
NCC = 8
NTT = 32
NTK = 8
EPS = 1e-5
NE = 4
DFF = 4096
NJC = 32
NTC = 4
SCALE = 0.125
STRW = 4112

DMA_QUEUES = ("sync", "gpsimd")
N_SLOTS = 4
ENGS = ("tensor", "vector", "scalar", "gpsimd", "sync")


def _rgn(prefix, lo, hi, gran):
    return [f"{prefix}:{i}" for i in range(lo // gran, (hi + gran - 1) // gran)]


class Prog:

    def __init__(self, serial=False):
        self.items = []
        self.serial = serial

    def emit(self, eng, fn, reads=(), writes=(), dma=False, slot=None):
        self.items.append(dict(eng=eng, fn=fn, reads=tuple(reads),
                               writes=tuple(writes), dma=dma, slot=slot))
        return len(self.items) - 1

    def realize(self, eng_sems, dma_sems):
        n = len(self.items)
        deps = [set() for _ in range(n)]
        if self.serial:
            for i in range(1, n):
                deps[i].add(i - 1)
        else:
            last_w = {}
            readers = {}
            for i, it in enumerate(self.items):
                for r in it["reads"]:
                    if r in last_w:
                        deps[i].add(last_w[r])
                for w in it["writes"]:
                    if w in last_w:
                        deps[i].add(last_w[w])
                    for rd in readers.get(w, ()):
                        deps[i].add(rd)
                for r in it["reads"]:
                    readers.setdefault(r, []).append(i)
                for w in it["writes"]:
                    last_w[w] = i
                    readers[w] = []
        for i, it in enumerate(self.items):
            if it["eng"] == "tensor" and not it["dma"]:
                deps[i] = {j for j in deps[i]
                           if self.items[j]["dma"] or self.items[j]["eng"] != "tensor"}
        dma_idx = {}
        q_counts = {q: 0 for q in DMA_QUEUES}
        default_slots = {"sync": (0, 1, 2, 3), "gpsimd": (0, 1, 2)}
        slot_uses = {(q, s): 0 for q in DMA_QUEUES for s in range(N_SLOTS)}
        for i, it in enumerate(self.items):
            if it["dma"]:
                q = it["eng"]
                if it.get("slot") is not None:
                    s = it["slot"]
                else:
                    cands = default_slots[q]
                    s = cands[q_counts[q] % len(cands)]
                    q_counts[q] += 1
                slot_uses[(q, s)] += 1
                dma_idx[i] = (q, s, slot_uses[(q, s)])
        need_mark = [False] * n
        for i in range(n):
            for j in deps[i]:
                if not self.items[j]["dma"]:
                    need_mark[j] = True
        comp_seq = {}
        ctr = {e: 0 for e in ENGS}
        for i, it in enumerate(self.items):
            if not it["dma"] and need_mark[i]:
                ctr[it["eng"]] += 1
                comp_seq[i] = (it["eng"], ctr[it["eng"]])
        final_counts = dict(ctr)
        per_eng = {e: [] for e in ENGS}
        hwm = {e: {} for e in ENGS}

        def mk_closure(i):
            it = self.items[i]
            eng_name = it["eng"]
            waits = []
            for j in sorted(deps[i]):
                jt = self.items[j]
                if jt["dma"]:
                    q, s, use = dma_idx[j]
                    waits.append((("d", q, s), 16 * use))
                else:
                    e2, seq = comp_seq[j]
                    waits.append((("c", e2), seq))
            if it["dma"]:
                q, s, use = dma_idx[i]
                if use > 1:
                    waits.append((("d", q, s), 16 * (use - 1)))
            h = hwm[eng_name]
            real_waits = []
            for key, val in waits:
                if h.get(key, 0) >= val:
                    continue
                h[key] = val
                real_waits.append((key, val))
            if it["dma"]:
                q, s, use = dma_idx[i]
                inc = (("d", q, s), 16)
            elif need_mark[i]:
                inc = (("c", eng_name), 1)
            else:
                inc = None

            def closure(eng):
                for key, val in real_waits:
                    sem = dma_sems[key[1]][key[2]] if key[0] == "d" else eng_sems[key[1]]
                    eng.wait_ge(sem, val)
                ins = it["fn"](eng)
                if inc is not None:
                    key, amt = inc
                    sem = dma_sems[key[1]][key[2]] if key[0] == "d" else eng_sems[key[1]]
                    ins.then_inc(sem, amt)
            return closure

        for i in range(n):
            per_eng[self.items[i]["eng"]].append(mk_closure(i))

        def final_join(eng):
            for q in DMA_QUEUES:
                for s in range(N_SLOTS):
                    if slot_uses[(q, s)]:
                        eng.wait_ge(dma_sems[q][s], 16 * slot_uses[(q, s)])
            for e, cnt in final_counts.items():
                if cnt and e != "sync":
                    eng.wait_ge(eng_sems[e], cnt)
        per_eng["sync"].append(final_join)
        return per_eng


def host_constants():
    ident = np.eye(128, dtype=bfnp)
    identf = np.eye(128, dtype=np.float32)
    perm = np.zeros((128, 128), dtype=bfnp)
    for h in range(2):
        b = 64 * h
        for i in range(32):
            perm[b + i, b + 32 + i] = 1.0
            perm[b + 32 + i, b + i] = 1.0
    ones_row = np.ones((1, 128), dtype=bfnp)
    inv_freq = 1.0 / (10000.0 ** (np.arange(0, HD, 2, dtype=np.float32) / HD))
    t = np.arange(T, dtype=np.float32)
    emb = np.concatenate([np.outer(t, inv_freq), np.outer(t, inv_freq)], axis=-1)
    cosT = np.cos(emb).T.astype(np.float32)
    sinT = np.sin(emb).T.astype(np.float32)
    sinT[0:32] = -sinT[0:32]
    rope_i = np.zeros((128, 2 * T), dtype=bfnp)
    rope_i[0:64, 0:T] = cosT.astype(bfnp)
    rope_i[64:128, 0:T] = cosT.astype(bfnp)
    rope_i[0:64, T:] = sinT.astype(bfnp)
    rope_i[64:128, T:] = sinT.astype(bfnp)
    masks = np.zeros((128, 4 * 512), dtype=bfnp)
    r = np.arange(128)[:, None]
    j = np.arange(512)[None, :]
    for b in range(4):
        masks[:, b * 512:(b + 1) * 512] = (j >= 128 * b + r).astype(bfnp)
    return ident, identf, perm, ones_row, rope_i, masks


SV = {k: i for i, k in enumerate(
    ["bq", "bv", "bo", "ln1w", "ln1b", "ln2w", "ln2b", "w0", "w1", "w2"])}


def pack_smallvecs(ins):
    names = ["L0_attn_bq", "L0_attn_bv", "L0_attn_bo", "L0_ffn_ln1_w",
             "L0_ffn_ln1_b", "L0_ffn_ln2_w", "L0_ffn_ln2_b"]
    rows = [np.asarray(ins[n], np.float32).reshape(C) for n in names]
    cw = np.asarray(ins["L0_ffn_conv_w"], np.float32).reshape(C, 3)
    rows += [np.ascontiguousarray(cw[:, i]) for i in range(3)]
    return np.stack(rows)


class Build:
    def __init__(self, serial=True, upto="H", dbg=()):
        self.serial = serial
        self.upto = upto
        self.dbg = set(dbg)
        self.P = Prog(serial=serial)

    def rAR1(self, lo, hi):
        return _rgn("AR1", lo, hi, 512)

    def rAR1_stripe(self, cc, lo, w):
        a = cc * STRW + 2 + lo
        return self.rAR1(a, a + w)

    def rAR2(self, lo, hi):
        return _rgn("AR2", lo, hi, 512)

    def rQB(self, lo, hi):
        return _rgn("QB", lo, hi, 512)

    def rAR3(self, slot):
        return [f"AR3:{slot}"]

    def ar1(self, cc, lo, w):
        return self.AR1[:, cc * STRW + 2 + lo: cc * STRW + 2 + lo + w]

    def psA(self, b, w=512):
        return self.PS_A[:, b * 512: b * 512 + w]

    def psB(self, b, w=512):
        return self.PS_B[:, b * 512: b * 512 + w]

    def col(self, vname, cc):
        i = SV[vname] * 8 + cc
        return self.COLS[:, i:i + 1]

    def sreg(self, ap, sub):
        return f"{ap.tensor.name}:{sub}"

    def build(self):
        nc = bass.Bass("TRN2", target_bir_lowering=False)
        self.nc = nc
        PH = "ABCDEFGH"
        on = {p: PH.index(p) <= PH.index(self.upto) for p in PH}

        din = lambda name, shape, dt: nc.dram_tensor(name, shape, dt, kind="ExternalInput").ap()

        def scratch(name, shape, dt):
            kind = "ExternalOutput" if name in self.dbg else "Internal"
            return nc.dram_tensor(name, shape, dt, kind=kind).ap()

        self.x_in = din("x", [T, C], F32)
        self.cos_o = din("cos_o", [1024, HD], F32)
        self.sin_o = din("sin_o", [1024, HD], F32)
        self.wq = din("wq", [C, C], F32)
        self.wk = din("wk", [C, C], F32)
        self.wv = din("wv", [C, C], F32)
        self.wo = din("wo", [C, C], F32)
        self.attn_w = din("attn_w", [C, 3 * C], F32)
        self.proj_w = din("proj_w", [C, C], F32)
        self.router_w = din("router_w", [C, NE], F32)
        self.ewv = din("ewv", [NE, C, 2 * DFF], F32)
        self.eproj = din("eproj", [NE, DFF, C], F32)
        self.smallv = din("smallv", [10, C], F32)
        self.c_ident = din("c_ident", [128, 128], BF16)
        self.c_identf = din("c_identf", [128, 128], F32)
        self.c_perm = din("c_perm", [128, 128], BF16)
        self.c_ones = din("c_ones", [1, 128], BF16)
        self.c_ropei = din("c_ropei", [128, 2 * T], BF16)
        self.c_masks = din("c_masks", [128, 4 * 512], BF16)

        self.y_out = nc.dram_tensor("y", [T, C], F32, kind="ExternalOutput").ap()

        self.s_ln0T = scratch("s_ln0T", [C, T], BF16)
        self.s_qT = scratch("s_qT", [C, T], BF16)
        self.s_aoT = scratch("s_aoT", [C, T], BF16)
        self.s_x1 = scratch("s_x1", [T, C], F32)
        self.s_fin = scratch("s_fin", [T, C], F32)
        self.s_hcT = scratch("s_hcT", [C, T], BF16)
        self.s_qiT = scratch("s_qiT", [C, T], BF16)
        self.s_aoiT = scratch("s_aoiT", [C, T], BF16)
        self.s_a = scratch("s_a", [T, C], F32)
        self.s_h2T = scratch("s_h2T", [C, T], BF16)
        self.s_ropeo = scratch("s_ropeo", [128, 2048], BF16)
        self.s_wvbf = scratch("s_wvbf", [NE, 16, 128, 4096], BF16)
        self.s_pjbf = scratch("s_pjbf", [NE, 8, 128, 4096], BF16)
        self.s_hT = scratch("s_hT", [C, T], BF16) if "s_hT" in self.dbg else None

        from contextlib import ExitStack
        ctx = self.ctx = ExitStack()
        sb = lambda name, shape, dt: ctx.enter_context(nc.sbuf_tensor(name, shape, dt))
        pst = lambda name, shape, dt: ctx.enter_context(nc.psum_tensor(name, shape, dt))

        self.AR1 = sb("AR1", [128, NCC * STRW], BF16)
        self.AR2 = sb("AR2", [128, 33280], BF16)
        self.AR3 = sb("AR3", [128, 3072], F32)
        self.WBUF = sb("WBUF", [128, 8192], BF16)
        self.QBUF = sb("QBUF", [128, 8192], BF16)
        self.MSK = sb("MSK", [128, 2048], BF16)
        self.BVB = self.MSK
        self._msk_loaded = False
        self._conv_jobs = self._make_conv_jobs()
        self.IDENT = sb("IDENT", [128, 128], BF16)
        self.IDENTF = sb("IDENTF", [128, 128], F32)
        self.PERM = sb("PERM", [128, 128], BF16)
        self.ONES = sb("ONES", [1, 128], BF16)
        self.COLS = sb("COLS", [128, 80], F32)
        self.PB = sb("PB", [128, 4 * 512], BF16)
        self.SAO = sb("SAO", [128, 2 * 512], BF16)
        self.RECB = sb("RECB", [128, 2 * 512], BF16)
        self.RROW = sb("RROW", [1, 2 * 512], BF16)
        self.TOK = sb("TOK", [128, 2 * C], BF16)
        self.STAT = sb("STAT", [128, 32], F32)
        self.MCOL = sb("MCOL", [128, 128], F32)
        self.SVTMP = sb("SVTMP", [80, 128], F32)
        self.ROPO = sb("ROPO", [128, 64], BF16)
        self.SILU = sb("SILU", [128, C], BF16)
        self.JUNK = self.SILU
        self.RWSB = sb("RWSB", [128, 32], BF16)
        self.STCM = sb("STCM", [128, 5120], BF16)

        self.PS_A = pst("PS_A", [128, 2048], F32)
        self.PS_B = pst("PS_B", [128, 1024], F32)
        self.PS_C = pst("PS_C", [128, 2048], BF16)

        self.setup_consts()
        if on["A"]:
            self.phase_A()
        if on["B"]:
            self.phase_B()
        if on["C"]:
            self.attention(self.s_qT, self.s_aoT, causal=False)
        if on["D"]:
            self.proj_ln_phase(which="D")
        if on["E"]:
            self.phase_E()
        if on["F"]:
            self.attention(self.s_qiT, self.s_aoiT, causal=True)
        if on["G"]:
            self.proj_ln_phase(which="G")
        if on["H"]:
            self.phase_H()

        eng_sems = {e: ctx.enter_context(nc.semaphore(f"se_{e}")) for e in ENGS}
        dma_sems = {q: [ctx.enter_context(nc.semaphore(f"sd_{q}{s}"))
                        for s in range(N_SLOTS)] for q in DMA_QUEUES}
        block = ctx.enter_context(nc.Block())
        per_eng = self.P.realize(eng_sems, dma_sems)

        @block.tensor
        def _(eng):
            for cl in per_eng["tensor"]:
                cl(eng)

        @block.vector
        def _(eng):
            for cl in per_eng["vector"]:
                cl(eng)

        @block.scalar
        def _(eng):
            for cl in per_eng["scalar"]:
                cl(eng)

        @block.gpsimd
        def _(eng):
            for cl in per_eng["gpsimd"]:
                cl(eng)

        @block.sync
        def _(eng):
            for cl in per_eng["sync"]:
                cl(eng)

        ctx.close()
        return nc

    def setup_consts(self):
        E = self.P.emit
        S = self
        E("sync", lambda g: g.dma_start(S.IDENT[:, :], S.c_ident), writes=["IDENT"], dma=True)
        E("sync", lambda g: g.dma_start(S.IDENTF[:, :], S.c_identf), writes=["IDENTF"], dma=True)
        E("sync", lambda g: g.dma_start(S.PERM[:, :], S.c_perm), writes=["PERM"], dma=True)
        E("sync", lambda g: g.dma_start(S.ONES[:, :], S.c_ones), writes=["ONES"], dma=True)
        E("sync", lambda g: g.dma_start(
            S.SVTMP[:, :], S.smallv.rearrange("v (cc p) -> (v cc) p", p=128)),
          writes=["SVTMP"], dma=True)
        E("tensor", lambda t: t.transpose(S.PS_A[:, 0:80], S.SVTMP[:, :], S.IDENTF[0:80, 0:80]),
          reads=["SVTMP", "IDENTF"], writes=["psA:0"])
        E("vector", lambda v: v.tensor_copy(S.COLS[:, :], S.PS_A[:, 0:80]),
          reads=["psA:0"], writes=["COLS"])
        E("sync", lambda g: g.dma_start(S.AR3[0:1, 0:1024],
                                        S.smallv[SV["bv"]:SV["bv"] + 1, :]),
          writes=["AR3:0"], dma=True)
        E("vector", lambda v: v.tensor_copy(S.TOK[0:1, 0:1024], S.AR3[0:1, 0:1024]),
          reads=["AR3:0"], writes=["TOK:0"])
        for h in range(2):
            E("tensor", lambda t, hh=h: t.matmul(
                S.psA(hh), S.ONES[0:1, 0:128], S.TOK[0:1, hh * 512:(hh + 1) * 512],
                start=True, stop=True),
              reads=["TOK:0", "ONES"], writes=[f"psA:{h}"])
        E("vector", lambda v: v.tensor_copy(S.BVB[:, 0:1024], S.PS_A[:, 0:1024]),
          reads=["psA:0", "psA:1"], writes=["MSK:0", "MSK:1"])
        for w, src in ((0, S.cos_o), (1, S.sin_o)):
            for i in range(8):
                E("gpsimd", lambda g, s=src, ii=i: g.dma_start(
                    S.ROPO[:, 0:64], s[ii * 128:(ii + 1) * 128, :]),
                  writes=["ROPO"], dma=True)
                E("tensor", lambda t: t.transpose(S.PS_C[0:64, 0:128],
                                                  S.ROPO[:, 0:64], S.IDENT[:, :]),
                  reads=["ROPO", "IDENT"], writes=["psC:0"])
                E("vector", lambda v, ww=w, ii=i: v.tensor_copy(
                    S.TOK[0:64, ww * 1024 + ii * 128: ww * 1024 + (ii + 1) * 128],
                    S.PS_C[0:64, 0:128]),
                  reads=["psC:0"], writes=[f"TOK:{w}"])
        E("vector", lambda v: v.tensor_scalar_mul(S.TOK[0:32, 1024:2048],
                                                  S.TOK[0:32, 1024:2048], -1.0),
          reads=["TOK:1"], writes=["TOK:1"])
        for r in range(2):
            E("gpsimd", lambda g, rr=r: g.dma_start(
                S.s_ropeo[rr * 64:(rr + 1) * 64, :], S.TOK[0:64, 0:2048]),
              reads=["TOK:0", "TOK:1"], writes=["s_ropeo"], dma=True)

    def ln_stats(self, src_ap, src_regions, o):
        E = self.P.emit
        S = self
        E("vector", lambda v: v.reduce_sum(S.STAT[:, o + 2:o + 3], src_ap, AX.X),
          reads=list(src_regions), writes=["STAT"])
        E("scalar", lambda s: s.activation(S.JUNK[:, :], src_ap, AF.Square,
                                           accum_out=S.STAT[:, o + 3:o + 4]),
          reads=list(src_regions), writes=["SILU", "STAT"])
        E("vector", lambda v: v.tensor_scalar_mul(S.STAT[:, o:o + 1],
                                                  S.STAT[:, o + 2:o + 3], 1.0 / C),
          reads=["STAT"], writes=["STAT"])
        E("vector", lambda v: v.tensor_scalar_mul(S.STAT[:, o + 4:o + 5],
                                                  S.STAT[:, o + 3:o + 4], 1.0 / C),
          reads=["STAT"], writes=["STAT"])
        E("vector", lambda v: v.scalar_tensor_tensor(
            S.STAT[:, o + 5:o + 6], S.STAT[:, o:o + 1], S.STAT[:, o:o + 1],
            S.STAT[:, o + 4:o + 5], ALU.mult, ALU.subtract),
          reads=["STAT"], writes=["STAT"])
        E("vector", lambda v: v.tensor_scalar(S.STAT[:, o + 5:o + 6],
                                              S.STAT[:, o + 5:o + 6], -1.0, EPS,
                                              ALU.mult, ALU.add),
          reads=["STAT"], writes=["STAT"])
        E("scalar", lambda s: s.activation(S.STAT[:, o + 6:o + 7],
                                           S.STAT[:, o + 5:o + 6], AF.Sqrt),
          reads=["STAT"], writes=["STAT"])
        E("vector", lambda v: v.reciprocal(S.STAT[:, o + 1:o + 2],
                                           S.STAT[:, o + 6:o + 7]),
          reads=["STAT"], writes=["STAT"])

    def phase_A(self):
        E = self.P.emit
        S = self
        xs = S.AR3[:, 0:1024]
        for tt in range(NTT):
            par = tt % 2
            E("sync", lambda g, i=tt: g.dma_start(xs, S.x_in[i * 128:(i + 1) * 128, :]),
              writes=["AR3:0"], dma=True)
            self.ln_stats(xs, ["AR3:0"], 0)
            E("vector", lambda v: v.tensor_scalar(
                S.TOK[:, 0:1024], xs, S.STAT[:, 0:1], S.STAT[:, 1:2],
                ALU.subtract, ALU.mult),
              reads=["AR3:0", "STAT"], writes=["TOK:0"])
            for cc in range(NCC):
                E("tensor", lambda t, c=cc, p=par: t.transpose(
                    S.PS_C[:, p * 1024 + c * 128: p * 1024 + (c + 1) * 128],
                    S.TOK[:, c * 128:(c + 1) * 128], S.IDENT[:, :]),
                  reads=["TOK:0", "IDENT"], writes=[f"psC:{par}"])
            q = tt % 4
            for cc in range(NCC):
                E("vector", lambda v, c=cc, p=par, qq=q: v.tensor_copy(
                    S.QBUF[:, c * 512 + qq * 128: c * 512 + qq * 128 + 128],
                    S.PS_C[:, p * 1024 + c * 128: p * 1024 + (c + 1) * 128]),
                  reads=[f"psC:{par}"], writes=self.rQB(cc * 512, cc * 512 + 512))
            if q == 3:
                tk = tt // 4
                E("gpsimd", lambda g, k=tk: g.dma_start(
                    S.s_ln0T[:, k * 512:(k + 1) * 512].rearrange(
                        "(cc p) n -> p cc n", p=128),
                    S.QBUF[:, 0:4096].rearrange("p (cc n) -> p cc n", cc=NCC)),
                  reads=self.rQB(0, 4096),
                  writes=[self.sreg(S.s_ln0T, tk)], dma=True)

    def qk_sweep(self, w_dram, w_cols, srcT, dstT, to_ar1, rope_tbl_off,
                 rope_tks, bias_name):
        E = self.P.emit
        S = self
        lo, hi = w_cols
        E("gpsimd", lambda g: g.dma_start(
            S.WBUF[:, :].rearrange("p (cc n) -> p cc n", cc=NCC),
            w_dram[:, lo:hi].rearrange("(cc p) n -> p cc n", p=128)),
          writes=["WBUF"], dma=True)
        for tk in range(NTK):
            spar = tk % 2
            slab = S.QBUF[:, spar * 4096:(spar + 1) * 4096]
            sregs = self.rQB(spar * 4096, (spar + 1) * 4096)
            E("sync", lambda g, k=tk, sl=slab: g.dma_start(
                sl.rearrange("p (cc n) -> p cc n", cc=NCC),
                srcT[:, k * 512:(k + 1) * 512].rearrange(
                    "(cc p) n -> p cc n", p=128)),
              reads=[self.sreg(srcT, tk)], writes=sregs, dma=True)
            for ccp in range(NCC):
                bank = ccp % 2
                for cc in range(NCC):
                    E("tensor", lambda t, c=cc, cp=ccp, b=bank, sl=slab: t.matmul(
                        S.psA(b),
                        S.WBUF[:, c * 1024 + cp * 128: c * 1024 + (cp + 1) * 128],
                        sl[:, c * 512:(c + 1) * 512],
                        start=(c == 0), stop=(c == NCC - 1)),
                      reads=["WBUF"] + sregs,
                      writes=[f"psA:{bank}"])
                if to_ar1:
                    out_ap = self.ar1(ccp, tk * 512, 512)
                    out_reg = self.rAR1_stripe(ccp, tk * 512, 512)
                else:
                    po = (tk % 2) * 512
                    out_ap = S.PB[:, po:po + 512]
                    out_reg = [f"PB:{tk % 2}"]
                if bias_name is not None:
                    E("scalar", lambda s, b=bank, o=out_ap, c=ccp: s.activation(
                        o, S.psA(b), AF.Identity, bias=S.col(bias_name, c)),
                      reads=[f"psA:{bank}", "COLS"], writes=out_reg)
                else:
                    E("scalar", lambda s, b=bank, o=out_ap: s.activation(
                        o, S.psA(b), AF.Copy),
                      reads=[f"psA:{bank}"], writes=out_reg)
                if rope_tbl_off is not None and tk in rope_tks:
                    co, so = rope_tbl_off(tk)
                    pb2o = 1024 + (tk % 2) * 512
                    pb2 = S.PB[:, pb2o:pb2o + 512]
                    pb2r = [f"PB:{2 + tk % 2}"]
                    bank2 = ccp % 2
                    E("tensor", lambda t, o=out_ap, b=bank2: t.matmul(
                        S.psB(b), S.PERM[:, :], o, start=True, stop=True),
                      reads=out_reg + ["PERM"], writes=[f"psB:{bank2}"])
                    E("vector", lambda v, b=bank2, p=pb2, s=so: v.tensor_mul(
                        p, S.psB(b), S.AR2[:, s:s + 512]),
                      reads=[f"psB:{bank2}"] + self.rAR2(so, so + 512),
                      writes=pb2r)
                    E("vector", lambda v, o=out_ap, c=co: v.tensor_mul(
                        S.RECB[:, 0:512], o, S.AR2[:, c:c + 512]),
                      reads=out_reg + self.rAR2(co, co + 512), writes=["RECB:0"])
                    E("vector", lambda v, o=out_ap, p=pb2: v.tensor_add(
                        o, S.RECB[:, 0:512], p),
                      reads=["RECB:0"] + pb2r, writes=out_reg)
                if not to_ar1:
                    E("gpsimd", lambda g, o=out_ap, c=ccp, k=tk: g.dma_start(
                        dstT[c * 128:(c + 1) * 128, k * 512:(k + 1) * 512], o),
                      reads=out_reg, writes=[self.sreg(dstT, tk)], dma=True)

    def v_sweep(self, w_dram, w_cols, srcT, with_bias):
        E = self.P.emit
        S = self
        lo, hi = w_cols
        E("gpsimd", lambda g: g.dma_start(
            S.WBUF[:, :].rearrange("p (cc n) -> p cc n", cc=NCC),
            w_dram[:, lo:hi].rearrange("(cc p) n -> p cc n", p=128)),
          writes=["WBUF"], dma=True)
        E("vector", lambda v: v.memset(S.AR2[:, 64:33280:65], 1.0),
          writes=self.rAR2(0, 33280))
        for tt in range(NTT):
            tk, q = tt // 4, tt % 4
            spar = tk % 2
            slab = S.QBUF[:, spar * 4096:(spar + 1) * 4096]
            sregs = self.rQB(spar * 4096, (spar + 1) * 4096)
            if q == 0:
                E("sync", lambda g, k=tk, sl=slab: g.dma_start(
                    sl.rearrange("p (cc n) -> p cc n", cc=NCC),
                    srcT[:, k * 512:(k + 1) * 512].rearrange(
                        "(cc p) n -> p cc n", p=128)),
                  reads=[self.sreg(srcT, tk)], writes=sregs, dma=True)
            for half in range(2):
                bank = 2 + half
                for cc in range(NCC):
                    E("tensor", lambda t, c=cc, h=half, b=bank, qq=q, sl=slab: t.matmul(
                        S.psA(b),
                        sl[:, c * 512 + qq * 128: c * 512 + qq * 128 + 128],
                        S.WBUF[:, c * 1024 + h * 512: c * 1024 + (h + 1) * 512],
                        start=(c == 0), stop=(c == NCC - 1)),
                      reads=["WBUF"] + sregs,
                      writes=[f"psA:{bank}"])
            for h in range(NH):
                bank = 2 + (h // 8)
                off = 1024 + (h % 8) * 64 + (h // 8) * 512
                pos = (tt * NH + h) * 65
                dst = S.AR2[:, pos:pos + 64]
                regs = self.rAR2(pos, pos + 64)
                if with_bias:
                    E("vector", lambda v, d=dst, o=off, hh=h: v.scalar_tensor_tensor(
                        d, S.PS_A[:, o:o + 64], 1.0, S.BVB[:, hh * 64:(hh + 1) * 64],
                        ALU.mult, ALU.add),
                      reads=[f"psA:{bank}", "MSK:0", "MSK:1"], writes=regs)
                else:
                    E("vector", lambda v, d=dst, o=off: v.tensor_copy(
                        d, S.PS_A[:, o:o + 64]),
                      reads=[f"psA:{bank}"], writes=regs)

    def phase_B(self):
        E = self.P.emit
        S = self
        E("sync", lambda g: g.dma_start(S.AR2[:, 0:2048], S.s_ropeo),
          reads=["s_ropeo"], writes=self.rAR2(0, 2048), dma=True)
        rope_off = lambda tk: ((tk - 6) * 512, 1024 + (tk - 6) * 512)
        self.qk_sweep(S.wq, (0, C), S.s_ln0T, S.s_qT, False, rope_off, (6, 7), "bq")
        self.qk_sweep(S.wk, (0, C), S.s_ln0T, None, True, rope_off, (6, 7), None)
        self.v_sweep(S.wv, (0, C), S.s_ln0T, True)

    def phase_E(self):
        E = self.P.emit
        S = self
        for cc in range(NCC):
            E("vector", lambda v, c=cc: v.memset(
                S.AR1[:, c * STRW: c * STRW + 2], 0.0),
              writes=self.rAR1(cc * STRW, cc * STRW + 2))
        for cc in range(NCC):
            for tk in range(NTK):
                rd = self.rAR1(cc * STRW + tk * 512, cc * STRW + 2 + (tk + 1) * 512)
                E("vector", lambda v, c=cc, k=tk: v.tensor_scalar_mul(
                    S.TOK[:, 0:512],
                    S.AR1[:, c * STRW + k * 512: c * STRW + k * 512 + 512],
                    S.col("w0", c)),
                  reads=rd + ["COLS"], writes=["TOK:0"])
                E("vector", lambda v, c=cc, k=tk: v.scalar_tensor_tensor(
                    S.TOK[:, 512:1024],
                    S.AR1[:, c * STRW + 1 + k * 512: c * STRW + 1 + k * 512 + 512],
                    S.col("w1", c), S.TOK[:, 0:512], ALU.mult, ALU.add),
                  reads=rd + ["COLS", "TOK:0"], writes=["TOK:0"])
                E("vector", lambda v, c=cc, k=tk: v.scalar_tensor_tensor(
                    S.TOK[:, 0:512], self.ar1(c, k * 512, 512),
                    S.col("w2", c), S.TOK[:, 512:1024], ALU.mult, ALU.add),
                  reads=rd + ["COLS", "TOK:0"], writes=["TOK:0"])
                E("gpsimd", lambda g, c=cc, k=tk: g.dma_start(
                    S.s_hcT[c * 128:(c + 1) * 128, k * 512:(k + 1) * 512],
                    S.TOK[:, 0:512]),
                  reads=["TOK:0"], writes=[self.sreg(S.s_hcT, tk)], dma=True)
        E("sync", lambda g: g.dma_start(S.AR2[:, 0:4096], S.c_ropei[:, 0:4096]),
          writes=self.rAR2(0, 4096), dma=True)
        E("sync", lambda g: g.dma_start(S.AR2[:, 4096:8192], S.c_ropei[:, 4096:8192]),
          writes=self.rAR2(4096, 8192), dma=True)
        rope_off = lambda tk: (tk * 512, 4096 + tk * 512)
        self.qk_sweep(S.attn_w, (0, C), S.s_hcT, S.s_qiT, False, rope_off,
                      tuple(range(8)), None)
        self.qk_sweep(S.attn_w, (C, 2 * C), S.s_hcT, None, True, rope_off,
                      tuple(range(8)), None)
        self.v_sweep(S.attn_w, (2 * C, 3 * C), S.s_hcT, False)


    def _make_conv_jobs(self):
        S = self
        jobs = []
        for e in range(NE):
            for g2 in range(16):
                def j(E, ee=e, gg=g2):
                    E("gpsimd", lambda g: g.dma_start(
                        S.s_wvbf[ee, gg].rearrange("p (gv cc j) -> p gv cc j",
                                                   gv=2, cc=NCC),
                        S.ewv[ee].rearrange(
                            "(cc p) (gv gr j) -> gr p gv cc j",
                            p=128, gv=2, j=256)[gg]),
                      writes=["s_wvbf"], dma=True, slot=3)
                jobs.append(j)
            for j4 in range(8):
                def j(E, ee=e, jj4=j4):
                    E("gpsimd", lambda g: g.dma_start(
                        S.s_pjbf[ee, jj4].rearrange("p (jj n) -> p jj n", jj=4),
                        S.eproj[ee, jj4 * 512:(jj4 + 1) * 512, :].rearrange(
                            "(jj p) n -> p jj n", p=128)),
                      writes=["s_pjbf"], dma=True, slot=3)
                jobs.append(j)
        return jobs

    def drain_conv_jobs(self, k):
        for _ in range(k):
            if self._conv_jobs:
                self._conv_jobs.pop(0)(self.P.emit)

    def attention(self, qT_dram, aoT_dram, causal):
        E = self.P.emit
        S = self
        if causal and not self._msk_loaded:
            self._msk_loaded = True
            E("sync", lambda g: g.dma_start(S.MSK[:, :], S.c_masks),
              writes=["MSK:0", "MSK:1"], dma=True)
        pending = None

        def emit_scores(h2, qc, kt, qregs, qbuf):
            for hh in range(2):
                sb_ = 2 * hh + (kt % 2)
                po = sb_ * 512
                preg = [f"PB:{sb_}"]
                kregs = self.rAR1_stripe(h2, kt * 128, 128)
                E("tensor", lambda t, p=h2, k=kt, b=sb_, r=hh, qb=qbuf,
                  q=qc: t.matmul(
                    S.psA(b),
                    S.AR1[r * 64:r * 64 + 64,
                          p * STRW + 2 + k * 128: p * STRW + 2 + k * 128 + 128],
                    qb[r * 64:r * 64 + 64, q * 512:(q + 1) * 512],
                    start=True, stop=True, tile_position=(r * 64, 0)),
                  reads=kregs + qregs, writes=[f"psA:{sb_}"])
                E("scalar", lambda s, b=sb_, o=po: s.activation(
                    S.PB[:, o:o + 512], S.psA(b), AF.Exp, scale=SCALE),
                  reads=[f"psA:{sb_}"], writes=preg)

        def emit_mask(qc, kt, n_kt):
            if not (causal and kt >= n_kt - 4):
                return
            bnd = kt - (n_kt - 4)
            for hh in range(2):
                po = (2 * hh + (kt % 2)) * 512
                preg = [f"PB:{2 * hh + (kt % 2)}"]
                E("vector", lambda v, o=po, bb=bnd: v.tensor_mul(
                    S.PB[:, o:o + 512], S.PB[:, o:o + 512],
                    S.MSK[:, bb * 512:(bb + 1) * 512]),
                  reads=preg + [f"MSK:{bnd // 2}"], writes=preg)

        def emit_pv(h2, qc, kt):
            for hh in range(2):
                h = 2 * h2 + hh
                po = (2 * hh + (kt % 2)) * 512
                preg = [f"PB:{2 * hh + (kt % 2)}"]
                pos = (kt * NH + h) * 65
                E("tensor", lambda t, o=po, hh_=hh, ps=pos,
                  first=(kt == 0): t.matmul(
                    S.PS_B[0:65, hh_ * 512:hh_ * 512 + 512],
                    S.AR2[:, ps:ps + 65], S.PB[:, o:o + 512],
                    start=first, stop=True),
                  reads=preg + self.rAR2(pos, pos + 65),
                  writes=[f"psB:{hh}"])

        def emit_stage(qc):
            p = qc % 2
            for hh in range(2):
                E("vector", lambda v, o=hh, pp=p: v.tensor_copy(
                    S.STCM[0:65, pp * 1024 + o * 512: pp * 1024 + (o + 1) * 512],
                    S.PS_B[0:65, o * 512:o * 512 + 512]),
                  reads=[f"psB:{hh}"],
                  writes=[f"STCM:{2 * p + hh}"])

        def make_evac(h2, qc):
            sao_par = qc % 2
            sao_reg = [f"SAO:{sao_par}"]
            p = qc % 2

            def emit_evac():
                for hh in range(2):
                    sreg_ = [f"STCM:{2 * p + hh}"]
                    E("vector", lambda v, o=hh, pp=p: v.reciprocal(
                        S.AR3[0:1, o * 512:(o + 1) * 512],
                        S.STCM[64:65, pp * 1024 + o * 512: pp * 1024 + (o + 1) * 512]),
                      reads=sreg_, writes=["AR3:0"])
                    E("vector", lambda v, o=hh: v.tensor_copy(
                        S.RROW[0:1, o * 512:(o + 1) * 512],
                        S.AR3[0:1, o * 512:(o + 1) * 512]),
                      reads=["AR3:0"], writes=["RROW"])
                    E("tensor", lambda t, o=hh: t.matmul(
                        S.PS_A[0:64, 2 * o * 512: 2 * o * 512 + 512],
                        S.ONES[0:1, 0:64],
                        S.RROW[0:1, o * 512:(o + 1) * 512], start=True, stop=True),
                      reads=["RROW", "ONES"], writes=[f"psA:{2 * hh}"])
                    E("vector", lambda v, o=hh: v.tensor_copy(
                        S.RECB[0:64, o * 512:(o + 1) * 512],
                        S.PS_A[0:64, 2 * o * 512: 2 * o * 512 + 512]),
                      reads=[f"psA:{2 * hh}"], writes=[f"RECB:{hh}"])
                    E("vector", lambda v, o=hh, sp=sao_par, pp=p: v.tensor_mul(
                        S.SAO[o * 64:(o + 1) * 64, sp * 512:(sp + 1) * 512],
                        S.STCM[0:64, pp * 1024 + o * 512: pp * 1024 + (o + 1) * 512],
                        S.RECB[0:64, o * 512:(o + 1) * 512]),
                      reads=sreg_ + [f"RECB:{hh}"], writes=sao_reg)
                E("gpsimd", lambda g, p2=h2, q=qc, sp=sao_par: g.dma_start(
                    aoT_dram[p2 * 128:(p2 + 1) * 128, q * 512:(q + 1) * 512],
                    S.SAO[:, sp * 512:(sp + 1) * 512]),
                  reads=sao_reg, writes=[self.sreg(aoT_dram, qc)], dma=True)
            return emit_evac

        for h2 in range(NH // 2):
            par = h2 % 2
            qbuf = S.QBUF[:, par * 4096:(par + 1) * 4096]
            qregs = self.rQB(par * 4096, (par + 1) * 4096)
            E("sync", lambda g, p=h2, qb=qbuf: g.dma_start(
                qb, qT_dram[p * 128:(p + 1) * 128, :]),
              reads=[self.sreg(qT_dram, k) for k in range(8)],
              writes=qregs, dma=True)
            for qc in range(NTK):
                n_kt = (4 * qc + 4) if causal else 32
                for kt in range(n_kt):
                    emit_scores(h2, qc, kt, qregs, qbuf)
                    emit_mask(qc, kt, n_kt)
                    if kt == 1 and pending is not None:
                        pending()
                        pending = None
                    if kt >= 1:
                        emit_pv(h2, qc, kt - 1)
                if pending is not None:
                    pending()
                    pending = None
                emit_pv(h2, qc, n_kt - 1)
                emit_stage(qc)
                pending = make_evac(h2, qc)
                self.drain_conv_jobs(1)
        pending()

    def proj_ln_phase(self, which):
        E = self.P.emit
        S = self
        is_d = which == "D"
        w_dram = S.wo if is_d else S.proj_w
        aoT = S.s_aoT if is_d else S.s_aoiT
        x_src = S.x_in if is_d else S.s_fin
        x1_dst = S.s_x1 if is_d else S.s_a
        E("gpsimd", lambda g: g.dma_start(
            S.WBUF[:, :].rearrange("p (cc n) -> p cc n", cc=NCC),
            w_dram.rearrange("(cc p) n -> p cc n", p=128)),
          writes=["WBUF"], dma=True)
        if not is_d:
            E("gpsimd", lambda g: g.dma_start(
                S.RWSB[:, :].rearrange("p (cc e) -> p cc e", cc=NCC),
                S.router_w.rearrange("(cc p) e -> p cc e", p=128)),
              writes=["RWSB"], dma=True)
        for tk in range(NTK):
            spar = tk % 2
            slab = S.QBUF[:, spar * 4096:(spar + 1) * 4096]
            sregs = self.rQB(spar * 4096, (spar + 1) * 4096)
            E("sync", lambda g, k=tk, sl=slab: g.dma_start(
                sl.rearrange("p (cc n) -> p cc n", cc=NCC),
                aoT[:, k * 512:(k + 1) * 512].rearrange(
                    "(cc p) n -> p cc n", p=128)),
              reads=[self.sreg(aoT, tk)], writes=sregs, dma=True)
            for ccp in range(NCC):
                bank = ccp % 2
                for cc in range(NCC):
                    E("tensor", lambda t, c=cc, cp=ccp, b=bank, sl=slab: t.matmul(
                        S.psA(b),
                        S.WBUF[:, c * 1024 + cp * 128: c * 1024 + (cp + 1) * 128],
                        sl[:, c * 512:(c + 1) * 512],
                        start=(c == 0), stop=(c == NCC - 1)),
                      reads=["WBUF"] + sregs,
                      writes=[f"psA:{bank}"])
                dst = S.STCM[:, ccp * 512:(ccp + 1) * 512]
                sreg = [f"STCM:{ccp}"]
                if is_d:
                    E("scalar", lambda s, b=bank, d=dst, c=ccp: s.activation(
                        d, S.psA(b), AF.Identity, bias=S.col("bo", c)),
                      reads=[f"psA:{bank}", "COLS"], writes=sreg)
                else:
                    E("scalar", lambda s, b=bank, d=dst: s.activation(
                        d, S.psA(b), AF.Copy),
                      reads=[f"psA:{bank}"], writes=sreg)
            for q in range(4):
                tt = tk * 4 + q
                parc = tt % 2
                for cc in range(NCC):
                    E("tensor", lambda t, c=cc, p=parc, qq=q: t.transpose(
                        S.PS_C[:, p * 1024 + c * 128: p * 1024 + (c + 1) * 128],
                        S.STCM[:, c * 512 + qq * 128: c * 512 + qq * 128 + 128],
                        S.IDENT[:, :]),
                      reads=[f"STCM:{cc}", "IDENT"], writes=[f"psC:{parc}"])
                E("vector", lambda v, p=parc: v.tensor_copy(
                    S.TOK[:, 0:1024], S.PS_C[:, p * 1024:(p + 1) * 1024]),
                  reads=[f"psC:{parc}"], writes=["TOK:0"])
                E("sync", lambda g, i=tt: g.dma_start(
                    S.AR3[:, 0:1024], x_src[i * 128:(i + 1) * 128, :]),
                  reads=[self.sreg(x_src, "t")] if not is_d else [],
                  writes=["AR3:0"], dma=True)
                if is_d:
                    self.ln_stats(S.TOK[:, 0:1024], ["TOK:0"], 0)
                    E("vector", lambda v: v.tensor_scalar(
                        S.AR3[:, 1024:2048], S.TOK[:, 0:1024],
                        S.STAT[:, 0:1], S.STAT[:, 1:2], ALU.subtract, ALU.mult),
                      reads=["TOK:0", "STAT"], writes=["AR3:1"])
                    E("vector", lambda v: v.tensor_add(
                        S.AR3[:, 1024:2048], S.AR3[:, 1024:2048], S.AR3[:, 0:1024]),
                      reads=["AR3:1", "AR3:0"], writes=["AR3:1"])
                else:
                    E("vector", lambda v: v.tensor_add(
                        S.AR3[:, 1024:2048], S.TOK[:, 0:1024], S.AR3[:, 0:1024]),
                      reads=["TOK:0", "AR3:0"], writes=["AR3:1"])
                E("gpsimd", lambda g, i=tt: g.dma_start(
                    x1_dst[i * 128:(i + 1) * 128, :], S.AR3[:, 1024:2048]),
                  reads=["AR3:1"], writes=[self.sreg(x1_dst, "t")], dma=True)
                self.ln_stats(S.AR3[:, 1024:2048], ["AR3:1"], 8)
                E("vector", lambda v: v.tensor_scalar(
                    S.AR3[:, 2048:3072], S.AR3[:, 1024:2048],
                    S.STAT[:, 8:9], S.STAT[:, 9:10], ALU.subtract, ALU.mult),
                  reads=["AR3:1", "STAT"], writes=["AR3:2"])
                if is_d:
                    E("gpsimd", lambda g, i=tt: g.dma_start(
                        S.s_fin[i * 128:(i + 1) * 128, :], S.AR3[:, 2048:3072]),
                      reads=["AR3:2"], writes=[self.sreg(S.s_fin, "t")], dma=True)
                E("vector", lambda v: v.tensor_copy(
                    S.TOK[:, 1024:2048], S.AR3[:, 2048:3072]),
                  reads=["AR3:2"], writes=["TOK:1"])
                for cc in range(NCC):
                    E("tensor", lambda t, c=cc, p=parc: t.transpose(
                        S.PS_C[:, p * 1024 + c * 128: p * 1024 + (c + 1) * 128],
                        S.TOK[:, 1024 + c * 128: 1024 + (c + 1) * 128],
                        S.IDENT[:, :]),
                      reads=["TOK:1", "IDENT"], writes=[f"psC:{parc}"])
                lnw = "ln1w" if is_d else "ln2w"
                lnb = "ln1b" if is_d else "ln2b"
                for cc in range(NCC):
                    if is_d:
                        dsth = self.ar1(cc, tt * 128, 128)
                        rr = self.rAR1_stripe(cc, tt * 128, 128)
                    else:
                        dsth = S.STCM[:, 4096 + cc * 128: 4096 + (cc + 1) * 128]
                        rr = [f"STCM:{8 + cc // 4}"]
                    E("vector", lambda v, c=cc, p=parc, d=dsth, rg=rr: v.tensor_scalar(
                        d, S.PS_C[:, p * 1024 + c * 128: p * 1024 + (c + 1) * 128],
                        S.col(lnw, c), S.col(lnb, c), ALU.mult, ALU.add),
                      reads=[f"psC:{parc}", "COLS"], writes=rr)
                if is_d and S.s_hT is not None:
                    for cc in range(NCC):
                        E("gpsimd", lambda g, c=cc, i=tt: g.dma_start(
                            S.s_hT[c * 128:(c + 1) * 128, i * 128:(i + 1) * 128],
                            self.ar1(c, i * 128, 128)),
                          reads=self.rAR1_stripe(cc, tt * 128, 128),
                          writes=["hTdbg"], dma=True)
                if not is_d:
                    for cc in range(NCC):
                        E("gpsimd", lambda g, c=cc, i=tt: g.dma_start(
                            S.s_h2T[c * 128:(c + 1) * 128, i * 128:(i + 1) * 128],
                            S.STCM[:, 4096 + c * 128: 4096 + (c + 1) * 128]),
                          reads=[f"STCM:{8 + cc // 4}"],
                          writes=[self.sreg(S.s_h2T, tt // 4)], dma=True)
                    for cc in range(NCC):
                        E("tensor", lambda t, c=cc: t.matmul(
                            S.PS_B[:, 0:4],
                            S.STCM[:, 4096 + c * 128: 4096 + (c + 1) * 128],
                            S.RWSB[:, c * 4:(c + 1) * 4],
                            start=(c == 0), stop=(c == NCC - 1)),
                          reads=[f"STCM:{8 + cc // 4}", "RWSB"], writes=["psB:0"])
                    E("vector", lambda v: v.reduce_max(
                        S.STAT[:, 16:17], S.PS_B[:, 0:4], AX.X),
                      reads=["psB:0"], writes=["STAT"])
                    E("vector", lambda v: v.tensor_single_scalar(
                        S.STAT[:, 20:24], S.PS_B[:, 0:4], S.STAT[:, 16:17],
                        ALU.is_ge),
                      reads=["psB:0", "STAT"], writes=["STAT"])
                    E("vector", lambda v, i=tt: v.tensor_copy(
                        S.MCOL[:, i * 4:i * 4 + 1], S.STAT[:, 20:21]),
                      reads=["STAT"], writes=["MCOL"])
                    E("vector", lambda v: v.tensor_scalar(
                        S.STAT[:, 17:18], S.STAT[:, 20:21], -1.0, 1.0,
                        ALU.mult, ALU.add),
                      reads=["STAT"], writes=["STAT"])
                    for e in range(1, NE):
                        E("vector", lambda v, i=tt, ee=e: v.tensor_mul(
                            S.MCOL[:, i * 4 + ee:i * 4 + ee + 1],
                            S.STAT[:, 20 + ee:21 + ee], S.STAT[:, 17:18]),
                          reads=["STAT"], writes=["MCOL"])
                        if e < NE - 1:
                            E("vector", lambda v, i=tt, ee=e: v.tensor_sub(
                                S.STAT[:, 17:18], S.STAT[:, 17:18],
                                S.MCOL[:, i * 4 + ee:i * 4 + ee + 1]),
                              reads=["STAT", "MCOL"], writes=["STAT"])

    def phase_H(self):
        E = self.P.emit
        S = self
        WV_OFF = (8192, 12288)
        PJ_OFF = 16384
        for tc in range(NTC):
            E("sync", lambda g, k=tc: g.dma_start(
                S.AR2[:, 0:8192].rearrange("p (cc n) -> p cc n", cc=NCC),
                S.s_h2T[:, k * 1024:(k + 1) * 1024].rearrange(
                    "(cc p) n -> p cc n", p=128)),
              reads=[self.sreg(S.s_h2T, tc * 2), self.sreg(S.s_h2T, tc * 2 + 1)],
              writes=self.rAR2(0, 8192), dma=True)
            for e in range(NE):
                for g2 in range(16):
                    woff = WV_OFF[g2 % 2]
                    wslot = S.AR2[:, woff:woff + 4096]
                    wregs = self.rAR2(woff, woff + 4096)
                    E("sync", lambda g, ee=e, gg=g2, w=wslot: g.dma_start(
                        w, S.s_wvbf[ee, gg]),
                      reads=["s_wvbf"], writes=wregs, dma=True)
                    for jh in range(2):
                        jc = 2 * g2 + jh
                        for half in range(2):
                            for cc in range(NCC):
                                E("tensor", lambda t, c=cc, h=half, w=wslot,
                                  j=jh: t.matmul(
                                    S.psA(h),
                                    w[:, c * 256 + j * 128: c * 256 + j * 128 + 128],
                                    S.AR2[:, c * 1024 + h * 512: c * 1024 + (h + 1) * 512],
                                    start=(c == 0), stop=(c == NCC - 1)),
                                  reads=wregs[:4] + self.rAR2(cc * 1024, (cc + 1) * 1024),
                                  writes=[f"psA:{half}"])
                            for cc in range(NCC):
                                E("tensor", lambda t, c=cc, h=half, w=wslot,
                                  j=jh: t.matmul(
                                    S.psA(2 + h),
                                    w[:, 2048 + c * 256 + j * 128: 2048 + c * 256 + j * 128 + 128],
                                    S.AR2[:, c * 1024 + h * 512: c * 1024 + (h + 1) * 512],
                                    start=(c == 0), stop=(c == NCC - 1)),
                                  reads=wregs[4:] + self.rAR2(cc * 1024, (cc + 1) * 1024),
                                  writes=[f"psA:{2 + half}"])
                        E("scalar", lambda s: s.activation(
                            S.SILU[:, :], S.PS_A[:, 0:1024], AF.Silu),
                          reads=["psA:0", "psA:1"], writes=["SILU"])
                        E("vector", lambda v, j=jc: v.tensor_mul(
                            S.AR1[:, j * 1024:(j + 1) * 1024], S.SILU[:, :],
                            S.PS_A[:, 1024:2048]),
                          reads=["SILU", "psA:2", "psA:3"],
                          writes=self.rAR1(jc * 1024, (jc + 1) * 1024))
                for grp, tts in enumerate(((0, 1, 2), (3, 4, 5), (6, 7))):
                    for jc in range(NJC):
                        if jc % 4 == 0:
                            poff4 = PJ_OFF + ((jc // 4) % 2) * 4096
                            pslot4 = S.AR2[:, poff4:poff4 + 4096]
                            E("sync", lambda g, ee=e, j4=jc // 4,
                              ps=pslot4: g.dma_start(ps, S.s_pjbf[ee, j4]),
                              reads=["s_pjbf"],
                              writes=self.rAR2(poff4, poff4 + 4096), dma=True)
                        poff = PJ_OFF + ((jc // 4) % 2) * 4096 + (jc % 4) * 1024
                        pslot = S.AR2[:, poff:poff + 1024]
                        pregs = self.rAR2(poff, poff + 1024)
                        for i, tt8 in enumerate(tts):
                            for half in range(2):
                                bank = 2 * i + half
                                bs = S.psA(bank) if bank < 4 else S.psB(bank - 4)
                                breg = (f"psA:{bank}" if bank < 4
                                        else f"psB:{bank - 4}")
                                E("tensor", lambda t, j=jc, t8=tt8, h=half, b=bs,
                                  ps=pslot: t.matmul(
                                    b,
                                    S.AR1[:, j * 1024 + t8 * 128: j * 1024 + t8 * 128 + 128],
                                    ps[:, h * 512:(h + 1) * 512],
                                    start=(j == 0), stop=(j == NJC - 1)),
                                  reads=pregs + self.rAR1(jc * 1024 + tt8 * 128,
                                                          jc * 1024 + tt8 * 128 + 128),
                                  writes=[breg])
                    for i, tt8 in enumerate(tts):
                        tile = tc * 8 + tt8
                        if i < 2:
                            src = S.PS_A[:, 2 * i * 512: 2 * i * 512 + 1024]
                            regs = [f"psA:{2 * i}", f"psA:{2 * i + 1}"]
                        else:
                            src = S.PS_B[:, 0:1024]
                            regs = ["psB:0", "psB:1"]
                        dst = S.QBUF[:, tt8 * 1024:(tt8 + 1) * 1024]
                        dregs = self.rQB(tt8 * 1024, (tt8 + 1) * 1024)
                        if e == 0:
                            E("vector", lambda v, s=src, d=dst, ti=tile: v.tensor_scalar_mul(
                                d, s, S.MCOL[:, ti * 4:ti * 4 + 1]),
                              reads=regs + ["MCOL"], writes=dregs)
                        else:
                            E("vector", lambda v, s=src, d=dst, ti=tile,
                              ee=e: v.scalar_tensor_tensor(
                                d, s, S.MCOL[:, ti * 4 + ee:ti * 4 + ee + 1], d,
                                ALU.mult, ALU.add),
                              reads=regs + ["MCOL"] + dregs, writes=dregs)
            for tt8 in range(8):
                tt = tc * 8 + tt8
                E("gpsimd", lambda g, i=tt: g.dma_start(
                    S.AR3[:, 0:1024], S.s_a[i * 128:(i + 1) * 128, :]),
                  reads=[self.sreg(S.s_a, "t")], writes=["AR3:0"], dma=True)
                E("gpsimd", lambda g, i=tt: g.dma_start(
                    S.AR3[:, 1024:2048], S.s_x1[i * 128:(i + 1) * 128, :]),
                  reads=[self.sreg(S.s_x1, "t")], writes=["AR3:1"], dma=True)
                E("vector", lambda v, t8=tt8: v.tensor_add(
                    S.AR3[:, 2048:3072], S.AR3[:, 0:1024],
                    S.QBUF[:, t8 * 1024:(t8 + 1) * 1024]),
                  reads=["AR3:0"] + self.rQB(tt8 * 1024, (tt8 + 1) * 1024),
                  writes=["AR3:2"])
                self.ln_stats(S.AR3[:, 2048:3072], ["AR3:2"], 24)
                E("vector", lambda v: v.tensor_scalar(
                    S.AR3[:, 0:1024], S.AR3[:, 2048:3072],
                    S.STAT[:, 24:25], S.STAT[:, 25:26], ALU.subtract, ALU.mult),
                  reads=["AR3:2", "STAT"], writes=["AR3:0"])
                E("vector", lambda v: v.tensor_add(
                    S.AR3[:, 0:1024], S.AR3[:, 0:1024], S.AR3[:, 1024:2048]),
                  reads=["AR3:0", "AR3:1"], writes=["AR3:0"])
                E("gpsimd", lambda g, i=tt: g.dma_start(
                    S.y_out[i * 128:(i + 1) * 128, :], S.AR3[:, 0:1024]),
                  reads=["AR3:0"], writes=["y_out"], dma=True)


_BUILD_CACHE = {}


def get_nc(serial=True, upto="H", dbg=()):
    key = (serial, upto, tuple(sorted(dbg)))
    if key not in _BUILD_CACHE:
        _BUILD_CACHE[key] = Build(serial=serial, upto=upto, dbg=dbg).build()
    return _BUILD_CACHE[key]


def make_inmaps(inputs, cores=8):
    ident, identf, perm, ones_row, rope_i, masks = host_constants()
    sv = pack_smallvecs(inputs)
    x = np.ascontiguousarray(np.asarray(inputs["L0_attn_hidden_states"], np.float32))
    f32c = lambda k: np.ascontiguousarray(np.asarray(inputs[k], np.float32))
    shared = dict(
        cos_o=f32c("L0_attn_cos"), sin_o=f32c("L0_attn_sin"),
        wq=f32c("L0_attn_wq"), wk=f32c("L0_attn_wk"), wv=f32c("L0_attn_wv"),
        wo=f32c("L0_attn_wo"), attn_w=f32c("L0_ffn_attn_w"),
        proj_w=f32c("L0_ffn_proj_w"), router_w=f32c("L0_ffn_router_w"),
        ewv=f32c("L0_ffn_expert_wv"), eproj=f32c("L0_ffn_expert_proj"),
        smallv=sv, c_ident=ident, c_identf=identf, c_perm=perm,
        c_ones=ones_row, c_ropei=rope_i, c_masks=masks,
    )
    return [dict(shared, x=np.ascontiguousarray(x[b % x.shape[0]]))
            for b in range(cores)]


def kernel(**inputs):
    nc = get_nc(serial=False, upto="H", dbg=())
    in_maps = make_inmaps(inputs, cores=8)
    res = run_bass_kernel_spmd(nc, in_maps, core_ids=list(range(8)))
    return np.stack([np.asarray(r["y"], np.float32) for r in res.results])


# version 07
# speedup vs baseline: 1.0467x; 1.0215x over previous
import numpy as np
import ml_dtypes
import concourse.bass as bass
import concourse.mybir as mybir
from concourse.bass_utils import run_bass_kernel_spmd

F32 = mybir.dt.float32
BF16 = mybir.dt.bfloat16
AF = mybir.ActivationFunctionType
ALU = mybir.AluOpType
AX = mybir.AxisListType
bfnp = ml_dtypes.bfloat16

T = 4096
C = 1024
NH = 16
HD = 64
NCC = 8
NTT = 32
NTK = 8
EPS = 1e-5
NE = 4
DFF = 4096
NJC = 32
NTC = 4
SCALE = 0.125
STRW = 4112

DMA_QUEUES = ("sync", "gpsimd")
N_SLOTS = 4
ENGS = ("tensor", "vector", "scalar", "gpsimd", "sync")


def _rgn(prefix, lo, hi, gran):
    return [f"{prefix}:{i}" for i in range(lo // gran, (hi + gran - 1) // gran)]


class Prog:

    def __init__(self, serial=False):
        self.items = []
        self.serial = serial

    def emit(self, eng, fn, reads=(), writes=(), dma=False, slot=None):
        self.items.append(dict(eng=eng, fn=fn, reads=tuple(reads),
                               writes=tuple(writes), dma=dma, slot=slot))
        return len(self.items) - 1

    def realize(self, eng_sems, dma_sems):
        n = len(self.items)
        deps = [set() for _ in range(n)]
        if self.serial:
            for i in range(1, n):
                deps[i].add(i - 1)
        else:
            last_w = {}
            readers = {}
            for i, it in enumerate(self.items):
                for r in it["reads"]:
                    if r in last_w:
                        deps[i].add(last_w[r])
                for w in it["writes"]:
                    if w in last_w:
                        deps[i].add(last_w[w])
                    for rd in readers.get(w, ()):
                        deps[i].add(rd)
                for r in it["reads"]:
                    readers.setdefault(r, []).append(i)
                for w in it["writes"]:
                    last_w[w] = i
                    readers[w] = []
        for i, it in enumerate(self.items):
            if it["eng"] == "tensor" and not it["dma"]:
                deps[i] = {j for j in deps[i]
                           if self.items[j]["dma"] or self.items[j]["eng"] != "tensor"}
        dma_idx = {}
        q_counts = {q: 0 for q in DMA_QUEUES}
        default_slots = {"sync": (0, 1, 2, 3), "gpsimd": (0, 1, 2)}
        slot_uses = {(q, s): 0 for q in DMA_QUEUES for s in range(N_SLOTS)}
        for i, it in enumerate(self.items):
            if it["dma"]:
                q = it["eng"]
                if it.get("slot") is not None:
                    s = it["slot"]
                else:
                    cands = default_slots[q]
                    s = cands[q_counts[q] % len(cands)]
                    q_counts[q] += 1
                slot_uses[(q, s)] += 1
                dma_idx[i] = (q, s, slot_uses[(q, s)])
        need_mark = [False] * n
        for i in range(n):
            for j in deps[i]:
                if not self.items[j]["dma"]:
                    need_mark[j] = True
        comp_seq = {}
        ctr = {e: 0 for e in ENGS}
        for i, it in enumerate(self.items):
            if not it["dma"] and need_mark[i]:
                ctr[it["eng"]] += 1
                comp_seq[i] = (it["eng"], ctr[it["eng"]])
        final_counts = dict(ctr)
        per_eng = {e: [] for e in ENGS}
        hwm = {e: {} for e in ENGS}

        def mk_closure(i):
            it = self.items[i]
            eng_name = it["eng"]
            waits = []
            for j in sorted(deps[i]):
                jt = self.items[j]
                if jt["dma"]:
                    q, s, use = dma_idx[j]
                    waits.append((("d", q, s), 16 * use))
                else:
                    e2, seq = comp_seq[j]
                    waits.append((("c", e2), seq))
            if it["dma"]:
                q, s, use = dma_idx[i]
                if use > 1:
                    waits.append((("d", q, s), 16 * (use - 1)))
            h = hwm[eng_name]
            real_waits = []
            for key, val in waits:
                if h.get(key, 0) >= val:
                    continue
                h[key] = val
                real_waits.append((key, val))
            if it["dma"]:
                q, s, use = dma_idx[i]
                inc = (("d", q, s), 16)
            elif need_mark[i]:
                inc = (("c", eng_name), 1)
            else:
                inc = None

            def closure(eng):
                for key, val in real_waits:
                    sem = dma_sems[key[1]][key[2]] if key[0] == "d" else eng_sems[key[1]]
                    eng.wait_ge(sem, val)
                ins = it["fn"](eng)
                if inc is not None:
                    key, amt = inc
                    sem = dma_sems[key[1]][key[2]] if key[0] == "d" else eng_sems[key[1]]
                    ins.then_inc(sem, amt)
            return closure

        for i in range(n):
            per_eng[self.items[i]["eng"]].append(mk_closure(i))

        def final_join(eng):
            for q in DMA_QUEUES:
                for s in range(N_SLOTS):
                    if slot_uses[(q, s)]:
                        eng.wait_ge(dma_sems[q][s], 16 * slot_uses[(q, s)])
            for e, cnt in final_counts.items():
                if cnt and e != "sync":
                    eng.wait_ge(eng_sems[e], cnt)
        per_eng["sync"].append(final_join)
        return per_eng


def host_constants():
    ident = np.eye(128, dtype=bfnp)
    identf = np.eye(128, dtype=np.float32)
    perm = np.zeros((128, 128), dtype=bfnp)
    for h in range(2):
        b = 64 * h
        for i in range(32):
            perm[b + i, b + 32 + i] = 1.0
            perm[b + 32 + i, b + i] = 1.0
    ones_row = np.ones((1, 128), dtype=bfnp)
    inv_freq = 1.0 / (10000.0 ** (np.arange(0, HD, 2, dtype=np.float32) / HD))
    t = np.arange(T, dtype=np.float32)
    emb = np.concatenate([np.outer(t, inv_freq), np.outer(t, inv_freq)], axis=-1)
    cosT = np.cos(emb).T.astype(np.float32)
    sinT = np.sin(emb).T.astype(np.float32)
    sinT[0:32] = -sinT[0:32]
    rope_i = np.zeros((128, 2 * T), dtype=bfnp)
    rope_i[0:64, 0:T] = cosT.astype(bfnp)
    rope_i[64:128, 0:T] = cosT.astype(bfnp)
    rope_i[0:64, T:] = sinT.astype(bfnp)
    rope_i[64:128, T:] = sinT.astype(bfnp)
    masks = np.zeros((128, 4 * 512), dtype=bfnp)
    r = np.arange(128)[:, None]
    j = np.arange(512)[None, :]
    for b in range(4):
        masks[:, b * 512:(b + 1) * 512] = (j >= 128 * b + r).astype(bfnp)
    return ident, identf, perm, ones_row, rope_i, masks


SV = {k: i for i, k in enumerate(
    ["bq", "bv", "bo", "ln1w", "ln1b", "ln2w", "ln2b", "w0", "w1", "w2"])}


def pack_smallvecs(ins):
    names = ["L0_attn_bq", "L0_attn_bv", "L0_attn_bo", "L0_ffn_ln1_w",
             "L0_ffn_ln1_b", "L0_ffn_ln2_w", "L0_ffn_ln2_b"]
    rows = [np.asarray(ins[n], np.float32).reshape(C) for n in names]
    cw = np.asarray(ins["L0_ffn_conv_w"], np.float32).reshape(C, 3)
    rows += [np.ascontiguousarray(cw[:, i]) for i in range(3)]
    return np.stack(rows)


class Build:
    def __init__(self, serial=True, upto="H", dbg=()):
        self.serial = serial
        self.upto = upto
        self.dbg = set(dbg)
        self.P = Prog(serial=serial)

    def rAR1(self, lo, hi):
        return _rgn("AR1", lo, hi, 512)

    def rAR1_stripe(self, cc, lo, w):
        a = cc * STRW + 2 + lo
        return self.rAR1(a, a + w)

    def rAR2(self, lo, hi):
        return _rgn("AR2", lo, hi, 512)

    def rQB(self, lo, hi):
        return _rgn("QB", lo, hi, 512)

    def rAR3(self, slot):
        return [f"AR3:{slot}"]

    def ar1(self, cc, lo, w):
        return self.AR1[:, cc * STRW + 2 + lo: cc * STRW + 2 + lo + w]

    def psA(self, b, w=512):
        return self.PS_A[:, b * 512: b * 512 + w]

    def psB(self, b, w=512):
        return self.PS_B[:, b * 512: b * 512 + w]

    def col(self, vname, cc):
        i = SV[vname] * 8 + cc
        return self.COLS[:, i:i + 1]

    def sreg(self, ap, sub):
        return f"{ap.tensor.name}:{sub}"

    def build(self):
        nc = bass.Bass("TRN2", target_bir_lowering=False)
        self.nc = nc
        PH = "ABCDEFGH"
        on = {p: PH.index(p) <= PH.index(self.upto) for p in PH}

        din = lambda name, shape, dt: nc.dram_tensor(name, shape, dt, kind="ExternalInput").ap()

        def scratch(name, shape, dt):
            kind = "ExternalOutput" if name in self.dbg else "Internal"
            return nc.dram_tensor(name, shape, dt, kind=kind).ap()

        self.x_in = din("x", [T, C], F32)
        self.cos_o = din("cos_o", [1024, HD], F32)
        self.sin_o = din("sin_o", [1024, HD], F32)
        self.wq = din("wq", [C, C], F32)
        self.wk = din("wk", [C, C], F32)
        self.wv = din("wv", [C, C], F32)
        self.wo = din("wo", [C, C], F32)
        self.attn_w = din("attn_w", [C, 3 * C], F32)
        self.proj_w = din("proj_w", [C, C], F32)
        self.router_w = din("router_w", [C, NE], F32)
        self.ewv = din("ewv", [NE, C, 2 * DFF], F32)
        self.eproj = din("eproj", [NE, DFF, C], F32)
        self.smallv = din("smallv", [10, C], F32)
        self.c_ident = din("c_ident", [128, 128], BF16)
        self.c_identf = din("c_identf", [128, 128], F32)
        self.c_perm = din("c_perm", [128, 128], BF16)
        self.c_ones = din("c_ones", [1, 128], BF16)
        self.c_ropei = din("c_ropei", [128, 2 * T], BF16)
        self.c_masks = din("c_masks", [128, 4 * 512], BF16)

        self.y_out = nc.dram_tensor("y", [T, C], F32, kind="ExternalOutput").ap()

        self.s_ln0T = scratch("s_ln0T", [C, T], BF16)
        self.s_qT = scratch("s_qT", [C, T], BF16)
        self.s_aoT = scratch("s_aoT", [C, T], BF16)
        self.s_x1 = scratch("s_x1", [T, C], F32)
        self.s_fin = scratch("s_fin", [T, C], F32)
        self.s_hcT = scratch("s_hcT", [C, T], BF16)
        self.s_qiT = scratch("s_qiT", [C, T], BF16)
        self.s_aoiT = scratch("s_aoiT", [C, T], BF16)
        self.s_a = scratch("s_a", [T, C], F32)
        self.s_h2T = scratch("s_h2T", [C, T], BF16)
        self.s_ropeo = scratch("s_ropeo", [128, 2048], BF16)
        self.s_wvbf = scratch("s_wvbf", [NE, 16, 128, 4096], BF16)
        self.s_pjbf = scratch("s_pjbf", [NE, 8, 128, 4096], BF16)
        self.s_hT = scratch("s_hT", [C, T], BF16) if "s_hT" in self.dbg else None

        from contextlib import ExitStack
        ctx = self.ctx = ExitStack()
        sb = lambda name, shape, dt: ctx.enter_context(nc.sbuf_tensor(name, shape, dt))
        pst = lambda name, shape, dt: ctx.enter_context(nc.psum_tensor(name, shape, dt))

        self.AR1 = sb("AR1", [128, NCC * STRW], BF16)
        self.AR2 = sb("AR2", [128, 33280], BF16)
        self.AR3 = sb("AR3", [128, 3072], F32)
        self.WBUF = sb("WBUF", [128, 8192], BF16)
        self.QBUF = sb("QBUF", [128, 8192], BF16)
        self.MSK = sb("MSK", [128, 2048], BF16)
        self.BVB = self.MSK
        self._msk_loaded = False
        self._conv_jobs = self._make_conv_jobs()
        self.IDENT = sb("IDENT", [128, 128], BF16)
        self.IDENTF = sb("IDENTF", [128, 128], F32)
        self.PERM = sb("PERM", [128, 128], BF16)
        self.ONES = sb("ONES", [1, 128], BF16)
        self.COLS = sb("COLS", [128, 80], F32)
        self.PB = sb("PB", [128, 4 * 512], BF16)
        self.SAO = sb("SAO", [128, 2 * 512], BF16)
        self.RECB = sb("RECB", [128, 2 * 512], BF16)
        self.RROW = sb("RROW", [1, 2 * 512], BF16)
        self.TOK = sb("TOK", [128, 2 * C], BF16)
        self.STAT = sb("STAT", [128, 32], F32)
        self.MCOL = sb("MCOL", [128, 128], F32)
        self.SVTMP = sb("SVTMP", [80, 128], F32)
        self.ROPO = sb("ROPO", [128, 64], BF16)
        self.SILU = sb("SILU", [128, C], BF16)
        self.JUNK = self.SILU
        self.RWSB = sb("RWSB", [128, 32], BF16)
        self.STCM = sb("STCM", [128, 5120], BF16)

        self.PS_A = pst("PS_A", [128, 2048], F32)
        self.PS_B = pst("PS_B", [128, 1024], F32)
        self.PS_C = pst("PS_C", [128, 2048], BF16)

        self.setup_consts()
        if on["A"]:
            self.phase_A()
        if on["B"]:
            self.phase_B()
        if on["C"]:
            self.attention(self.s_qT, self.s_aoT, causal=False)
        if on["D"]:
            self.proj_ln_phase(which="D")
        if on["E"]:
            self.phase_E()
        if on["F"]:
            self.attention(self.s_qiT, self.s_aoiT, causal=True)
        if on["G"]:
            self.proj_ln_phase(which="G")
        if on["H"]:
            self.phase_H()

        eng_sems = {e: ctx.enter_context(nc.semaphore(f"se_{e}")) for e in ENGS}
        dma_sems = {q: [ctx.enter_context(nc.semaphore(f"sd_{q}{s}"))
                        for s in range(N_SLOTS)] for q in DMA_QUEUES}
        block = ctx.enter_context(nc.Block())
        per_eng = self.P.realize(eng_sems, dma_sems)

        @block.tensor
        def _(eng):
            for cl in per_eng["tensor"]:
                cl(eng)

        @block.vector
        def _(eng):
            for cl in per_eng["vector"]:
                cl(eng)

        @block.scalar
        def _(eng):
            for cl in per_eng["scalar"]:
                cl(eng)

        @block.gpsimd
        def _(eng):
            for cl in per_eng["gpsimd"]:
                cl(eng)

        @block.sync
        def _(eng):
            for cl in per_eng["sync"]:
                cl(eng)

        ctx.close()
        return nc

    def setup_consts(self):
        E = self.P.emit
        S = self
        E("sync", lambda g: g.dma_start(S.IDENT[:, :], S.c_ident), writes=["IDENT"], dma=True)
        E("sync", lambda g: g.dma_start(S.IDENTF[:, :], S.c_identf), writes=["IDENTF"], dma=True)
        E("sync", lambda g: g.dma_start(S.PERM[:, :], S.c_perm), writes=["PERM"], dma=True)
        E("sync", lambda g: g.dma_start(S.ONES[:, :], S.c_ones), writes=["ONES"], dma=True)
        E("sync", lambda g: g.dma_start(
            S.SVTMP[:, :], S.smallv.rearrange("v (cc p) -> (v cc) p", p=128)),
          writes=["SVTMP"], dma=True)
        E("tensor", lambda t: t.transpose(S.PS_A[:, 0:80], S.SVTMP[:, :], S.IDENTF[0:80, 0:80]),
          reads=["SVTMP", "IDENTF"], writes=["psA:0"])
        E("vector", lambda v: v.tensor_copy(S.COLS[:, :], S.PS_A[:, 0:80]),
          reads=["psA:0"], writes=["COLS"])
        E("sync", lambda g: g.dma_start(S.AR3[0:1, 0:1024],
                                        S.smallv[SV["bv"]:SV["bv"] + 1, :]),
          writes=["AR3:0"], dma=True)
        E("vector", lambda v: v.tensor_copy(S.TOK[0:1, 0:1024], S.AR3[0:1, 0:1024]),
          reads=["AR3:0"], writes=["TOK:0"])
        for h in range(2):
            E("tensor", lambda t, hh=h: t.matmul(
                S.psA(hh), S.ONES[0:1, 0:128], S.TOK[0:1, hh * 512:(hh + 1) * 512],
                start=True, stop=True),
              reads=["TOK:0", "ONES"], writes=[f"psA:{h}"])
        E("vector", lambda v: v.tensor_copy(S.BVB[:, 0:1024], S.PS_A[:, 0:1024]),
          reads=["psA:0", "psA:1"], writes=["MSK:0", "MSK:1"])
        for w, src in ((0, S.cos_o), (1, S.sin_o)):
            for i in range(8):
                E("gpsimd", lambda g, s=src, ii=i: g.dma_start(
                    S.ROPO[:, 0:64], s[ii * 128:(ii + 1) * 128, :]),
                  writes=["ROPO"], dma=True)
                E("tensor", lambda t: t.transpose(S.PS_C[0:64, 0:128],
                                                  S.ROPO[:, 0:64], S.IDENT[:, :]),
                  reads=["ROPO", "IDENT"], writes=["psC:0"])
                E("vector", lambda v, ww=w, ii=i: v.tensor_copy(
                    S.TOK[0:64, ww * 1024 + ii * 128: ww * 1024 + (ii + 1) * 128],
                    S.PS_C[0:64, 0:128]),
                  reads=["psC:0"], writes=[f"TOK:{w}"])
        E("vector", lambda v: v.tensor_scalar_mul(S.TOK[0:32, 1024:2048],
                                                  S.TOK[0:32, 1024:2048], -1.0),
          reads=["TOK:1"], writes=["TOK:1"])
        for r in range(2):
            E("gpsimd", lambda g, rr=r: g.dma_start(
                S.s_ropeo[rr * 64:(rr + 1) * 64, :], S.TOK[0:64, 0:2048]),
              reads=["TOK:0", "TOK:1"], writes=["s_ropeo"], dma=True)

    def ln_stats(self, src_ap, src_regions, o):
        E = self.P.emit
        S = self
        E("vector", lambda v: v.reduce_sum(S.STAT[:, o + 2:o + 3], src_ap, AX.X),
          reads=list(src_regions), writes=["STAT"])
        E("scalar", lambda s: s.activation(S.JUNK[:, :], src_ap, AF.Square,
                                           accum_out=S.STAT[:, o + 3:o + 4]),
          reads=list(src_regions), writes=["SILU", "STAT"])
        E("vector", lambda v: v.tensor_scalar_mul(S.STAT[:, o:o + 1],
                                                  S.STAT[:, o + 2:o + 3], 1.0 / C),
          reads=["STAT"], writes=["STAT"])
        E("vector", lambda v: v.tensor_scalar_mul(S.STAT[:, o + 4:o + 5],
                                                  S.STAT[:, o + 3:o + 4], 1.0 / C),
          reads=["STAT"], writes=["STAT"])
        E("vector", lambda v: v.scalar_tensor_tensor(
            S.STAT[:, o + 5:o + 6], S.STAT[:, o:o + 1], S.STAT[:, o:o + 1],
            S.STAT[:, o + 4:o + 5], ALU.mult, ALU.subtract),
          reads=["STAT"], writes=["STAT"])
        E("vector", lambda v: v.tensor_scalar(S.STAT[:, o + 5:o + 6],
                                              S.STAT[:, o + 5:o + 6], -1.0, EPS,
                                              ALU.mult, ALU.add),
          reads=["STAT"], writes=["STAT"])
        E("scalar", lambda s: s.activation(S.STAT[:, o + 6:o + 7],
                                           S.STAT[:, o + 5:o + 6], AF.Sqrt),
          reads=["STAT"], writes=["STAT"])
        E("vector", lambda v: v.reciprocal(S.STAT[:, o + 1:o + 2],
                                           S.STAT[:, o + 6:o + 7]),
          reads=["STAT"], writes=["STAT"])

    def phase_A(self):
        E = self.P.emit
        S = self
        xs = S.AR3[:, 0:1024]
        for tt in range(NTT):
            par = tt % 2
            E("sync", lambda g, i=tt: g.dma_start(xs, S.x_in[i * 128:(i + 1) * 128, :]),
              writes=["AR3:0"], dma=True)
            self.ln_stats(xs, ["AR3:0"], 0)
            E("vector", lambda v: v.tensor_scalar(
                S.TOK[:, 0:1024], xs, S.STAT[:, 0:1], S.STAT[:, 1:2],
                ALU.subtract, ALU.mult),
              reads=["AR3:0", "STAT"], writes=["TOK:0"])
            for cc in range(NCC):
                E("tensor", lambda t, c=cc, p=par: t.transpose(
                    S.PS_C[:, p * 1024 + c * 128: p * 1024 + (c + 1) * 128],
                    S.TOK[:, c * 128:(c + 1) * 128], S.IDENT[:, :]),
                  reads=["TOK:0", "IDENT"], writes=[f"psC:{par}"])
            q = tt % 4
            for cc in range(NCC):
                E("vector", lambda v, c=cc, p=par, qq=q: v.tensor_copy(
                    S.QBUF[:, c * 512 + qq * 128: c * 512 + qq * 128 + 128],
                    S.PS_C[:, p * 1024 + c * 128: p * 1024 + (c + 1) * 128]),
                  reads=[f"psC:{par}"], writes=self.rQB(cc * 512, cc * 512 + 512))
            if q == 3:
                tk = tt // 4
                E("gpsimd", lambda g, k=tk: g.dma_start(
                    S.s_ln0T[:, k * 512:(k + 1) * 512].rearrange(
                        "(cc p) n -> p cc n", p=128),
                    S.QBUF[:, 0:4096].rearrange("p (cc n) -> p cc n", cc=NCC)),
                  reads=self.rQB(0, 4096),
                  writes=[self.sreg(S.s_ln0T, tk)], dma=True)

    def qk_sweep(self, w_dram, w_cols, srcT, dstT, to_ar1, rope_tbl_off,
                 rope_tks, bias_name):
        E = self.P.emit
        S = self
        lo, hi = w_cols
        E("gpsimd", lambda g: g.dma_start(
            S.WBUF[:, :].rearrange("p (cc n) -> p cc n", cc=NCC),
            w_dram[:, lo:hi].rearrange("(cc p) n -> p cc n", p=128)),
          writes=["WBUF"], dma=True)
        for tk in range(NTK):
            spar = tk % 2
            slab = S.QBUF[:, spar * 4096:(spar + 1) * 4096]
            sregs = self.rQB(spar * 4096, (spar + 1) * 4096)
            E("sync", lambda g, k=tk, sl=slab: g.dma_start(
                sl.rearrange("p (cc n) -> p cc n", cc=NCC),
                srcT[:, k * 512:(k + 1) * 512].rearrange(
                    "(cc p) n -> p cc n", p=128)),
              reads=[self.sreg(srcT, tk)], writes=sregs, dma=True)
            for ccp in range(NCC):
                bank = ccp % 2
                for cc in range(NCC):
                    E("tensor", lambda t, c=cc, cp=ccp, b=bank, sl=slab: t.matmul(
                        S.psA(b),
                        S.WBUF[:, c * 1024 + cp * 128: c * 1024 + (cp + 1) * 128],
                        sl[:, c * 512:(c + 1) * 512],
                        start=(c == 0), stop=(c == NCC - 1)),
                      reads=["WBUF"] + sregs,
                      writes=[f"psA:{bank}"])
                if to_ar1:
                    out_ap = self.ar1(ccp, tk * 512, 512)
                    out_reg = self.rAR1_stripe(ccp, tk * 512, 512)
                else:
                    po = (tk % 2) * 512
                    out_ap = S.PB[:, po:po + 512]
                    out_reg = [f"PB:{tk % 2}"]
                if bias_name is not None:
                    E("scalar", lambda s, b=bank, o=out_ap, c=ccp: s.activation(
                        o, S.psA(b), AF.Identity, bias=S.col(bias_name, c)),
                      reads=[f"psA:{bank}", "COLS"], writes=out_reg)
                else:
                    E("scalar", lambda s, b=bank, o=out_ap: s.activation(
                        o, S.psA(b), AF.Copy),
                      reads=[f"psA:{bank}"], writes=out_reg)
                if rope_tbl_off is not None and tk in rope_tks:
                    co, so = rope_tbl_off(tk)
                    pb2o = 1024 + (tk % 2) * 512
                    pb2 = S.PB[:, pb2o:pb2o + 512]
                    pb2r = [f"PB:{2 + tk % 2}"]
                    bank2 = ccp % 2
                    E("tensor", lambda t, o=out_ap, b=bank2: t.matmul(
                        S.psB(b), S.PERM[:, :], o, start=True, stop=True),
                      reads=out_reg + ["PERM"], writes=[f"psB:{bank2}"])
                    E("vector", lambda v, b=bank2, p=pb2, s=so: v.tensor_mul(
                        p, S.psB(b), S.AR2[:, s:s + 512]),
                      reads=[f"psB:{bank2}"] + self.rAR2(so, so + 512),
                      writes=pb2r)
                    E("vector", lambda v, o=out_ap, c=co: v.tensor_mul(
                        S.RECB[:, 0:512], o, S.AR2[:, c:c + 512]),
                      reads=out_reg + self.rAR2(co, co + 512), writes=["RECB:0"])
                    E("vector", lambda v, o=out_ap, p=pb2: v.tensor_add(
                        o, S.RECB[:, 0:512], p),
                      reads=["RECB:0"] + pb2r, writes=out_reg)
                if not to_ar1:
                    E("gpsimd", lambda g, o=out_ap, c=ccp, k=tk: g.dma_start(
                        dstT[c * 128:(c + 1) * 128, k * 512:(k + 1) * 512], o),
                      reads=out_reg, writes=[self.sreg(dstT, tk)], dma=True)

    def v_sweep(self, w_dram, w_cols, srcT, with_bias):
        E = self.P.emit
        S = self
        lo, hi = w_cols
        E("gpsimd", lambda g: g.dma_start(
            S.WBUF[:, :].rearrange("p (cc n) -> p cc n", cc=NCC),
            w_dram[:, lo:hi].rearrange("(cc p) n -> p cc n", p=128)),
          writes=["WBUF"], dma=True)
        E("vector", lambda v: v.memset(S.AR2[:, 64:33280:65], 1.0),
          writes=self.rAR2(0, 33280))
        for tt in range(NTT):
            tk, q = tt // 4, tt % 4
            spar = tk % 2
            slab = S.QBUF[:, spar * 4096:(spar + 1) * 4096]
            sregs = self.rQB(spar * 4096, (spar + 1) * 4096)
            if q == 0:
                E("sync", lambda g, k=tk, sl=slab: g.dma_start(
                    sl.rearrange("p (cc n) -> p cc n", cc=NCC),
                    srcT[:, k * 512:(k + 1) * 512].rearrange(
                        "(cc p) n -> p cc n", p=128)),
                  reads=[self.sreg(srcT, tk)], writes=sregs, dma=True)
            for half in range(2):
                bank = 2 + half
                for cc in range(NCC):
                    E("tensor", lambda t, c=cc, h=half, b=bank, qq=q, sl=slab: t.matmul(
                        S.psA(b),
                        sl[:, c * 512 + qq * 128: c * 512 + qq * 128 + 128],
                        S.WBUF[:, c * 1024 + h * 512: c * 1024 + (h + 1) * 512],
                        start=(c == 0), stop=(c == NCC - 1)),
                      reads=["WBUF"] + sregs,
                      writes=[f"psA:{bank}"])
            for h in range(NH):
                bank = 2 + (h // 8)
                off = 1024 + (h % 8) * 64 + (h // 8) * 512
                pos = (tt * NH + h) * 65
                dst = S.AR2[:, pos:pos + 64]
                regs = self.rAR2(pos, pos + 64)
                if with_bias:
                    E("vector", lambda v, d=dst, o=off, hh=h: v.scalar_tensor_tensor(
                        d, S.PS_A[:, o:o + 64], 1.0, S.BVB[:, hh * 64:(hh + 1) * 64],
                        ALU.mult, ALU.add),
                      reads=[f"psA:{bank}", "MSK:0", "MSK:1"], writes=regs)
                else:
                    E("vector", lambda v, d=dst, o=off: v.tensor_copy(
                        d, S.PS_A[:, o:o + 64]),
                      reads=[f"psA:{bank}"], writes=regs)

    def phase_B(self):
        E = self.P.emit
        S = self
        E("sync", lambda g: g.dma_start(S.AR2[:, 0:2048], S.s_ropeo),
          reads=["s_ropeo"], writes=self.rAR2(0, 2048), dma=True)
        rope_off = lambda tk: ((tk - 6) * 512, 1024 + (tk - 6) * 512)
        self.qk_sweep(S.wq, (0, C), S.s_ln0T, S.s_qT, False, rope_off, (6, 7), "bq")
        self.qk_sweep(S.wk, (0, C), S.s_ln0T, None, True, rope_off, (6, 7), None)
        self.v_sweep(S.wv, (0, C), S.s_ln0T, True)

    def phase_E(self):
        E = self.P.emit
        S = self
        for cc in range(NCC):
            E("vector", lambda v, c=cc: v.memset(
                S.AR1[:, c * STRW: c * STRW + 2], 0.0),
              writes=self.rAR1(cc * STRW, cc * STRW + 2))
        for cc in range(NCC):
            for tk in range(NTK):
                rd = self.rAR1(cc * STRW + tk * 512, cc * STRW + 2 + (tk + 1) * 512)
                E("vector", lambda v, c=cc, k=tk: v.tensor_scalar_mul(
                    S.TOK[:, 0:512],
                    S.AR1[:, c * STRW + k * 512: c * STRW + k * 512 + 512],
                    S.col("w0", c)),
                  reads=rd + ["COLS"], writes=["TOK:0"])
                E("vector", lambda v, c=cc, k=tk: v.scalar_tensor_tensor(
                    S.TOK[:, 512:1024],
                    S.AR1[:, c * STRW + 1 + k * 512: c * STRW + 1 + k * 512 + 512],
                    S.col("w1", c), S.TOK[:, 0:512], ALU.mult, ALU.add),
                  reads=rd + ["COLS", "TOK:0"], writes=["TOK:0"])
                E("vector", lambda v, c=cc, k=tk: v.scalar_tensor_tensor(
                    S.TOK[:, 0:512], self.ar1(c, k * 512, 512),
                    S.col("w2", c), S.TOK[:, 512:1024], ALU.mult, ALU.add),
                  reads=rd + ["COLS", "TOK:0"], writes=["TOK:0"])
                E("gpsimd", lambda g, c=cc, k=tk: g.dma_start(
                    S.s_hcT[c * 128:(c + 1) * 128, k * 512:(k + 1) * 512],
                    S.TOK[:, 0:512]),
                  reads=["TOK:0"], writes=[self.sreg(S.s_hcT, tk)], dma=True)
        E("sync", lambda g: g.dma_start(S.AR2[:, 0:4096], S.c_ropei[:, 0:4096]),
          writes=self.rAR2(0, 4096), dma=True)
        E("sync", lambda g: g.dma_start(S.AR2[:, 4096:8192], S.c_ropei[:, 4096:8192]),
          writes=self.rAR2(4096, 8192), dma=True)
        rope_off = lambda tk: (tk * 512, 4096 + tk * 512)
        self.qk_sweep(S.attn_w, (0, C), S.s_hcT, S.s_qiT, False, rope_off,
                      tuple(range(8)), None)
        self.qk_sweep(S.attn_w, (C, 2 * C), S.s_hcT, None, True, rope_off,
                      tuple(range(8)), None)
        self.v_sweep(S.attn_w, (2 * C, 3 * C), S.s_hcT, False)


    def _make_conv_jobs(self):
        S = self
        jobs = []
        for e in range(NE):
            for g2 in range(16):
                def j(E, ee=e, gg=g2):
                    E("gpsimd", lambda g: g.dma_start(
                        S.s_wvbf[ee, gg].rearrange("p (gv cc j) -> p gv cc j",
                                                   gv=2, cc=NCC),
                        S.ewv[ee].rearrange(
                            "(cc p) (gv gr j) -> gr p gv cc j",
                            p=128, gv=2, j=256)[gg]),
                      writes=["s_wvbf"], dma=True, slot=3)
                jobs.append(j)
            for j4 in range(8):
                def j(E, ee=e, jj4=j4):
                    E("gpsimd", lambda g: g.dma_start(
                        S.s_pjbf[ee, jj4].rearrange("p (jj n) -> p jj n", jj=4),
                        S.eproj[ee, jj4 * 512:(jj4 + 1) * 512, :].rearrange(
                            "(jj p) n -> p jj n", p=128)),
                      writes=["s_pjbf"], dma=True, slot=3)
                jobs.append(j)
        return jobs

    def drain_conv_jobs(self, k):
        for _ in range(k):
            if self._conv_jobs:
                self._conv_jobs.pop(0)(self.P.emit)

    def attention(self, qT_dram, aoT_dram, causal):
        E = self.P.emit
        S = self
        if causal and not self._msk_loaded:
            self._msk_loaded = True
            E("sync", lambda g: g.dma_start(S.MSK[:, :], S.c_masks),
              writes=["MSK:0", "MSK:1"], dma=True)
        pending = None

        def emit_scores(h2, qc, kt, qregs, qbuf):
            par = kt % 2
            for hh in range(2):
                sb_ = 2 * par + hh
                kregs = self.rAR1_stripe(h2, kt * 128, 128)
                E("tensor", lambda t, p=h2, k=kt, b=sb_, r=hh, qb=qbuf,
                  q=qc: t.matmul(
                    S.psA(b),
                    S.AR1[r * 64:r * 64 + 64,
                          p * STRW + 2 + k * 128: p * STRW + 2 + k * 128 + 128],
                    qb[r * 64:r * 64 + 64, q * 512:(q + 1) * 512],
                    start=True, stop=True, tile_position=(r * 64, 0)),
                  reads=kregs + qregs, writes=[f"psA:{sb_}"])
            E("scalar", lambda s, pp=par: s.activation(
                S.PB[:, pp * 1024:(pp + 1) * 1024],
                S.PS_A[:, pp * 1024:(pp + 1) * 1024], AF.Exp, scale=SCALE),
              reads=[f"psA:{2 * par}", f"psA:{2 * par + 1}"],
              writes=[f"PB:{2 * par}", f"PB:{2 * par + 1}"])

        def emit_mask(qc, kt, n_kt):
            if not (causal and kt >= n_kt - 4):
                return
            bnd = kt - (n_kt - 4)
            for hh in range(2):
                po = (2 * (kt % 2) + hh) * 512
                preg = [f"PB:{2 * (kt % 2) + hh}"]
                E("vector", lambda v, o=po, bb=bnd: v.tensor_mul(
                    S.PB[:, o:o + 512], S.PB[:, o:o + 512],
                    S.MSK[:, bb * 512:(bb + 1) * 512]),
                  reads=preg + [f"MSK:{bnd // 2}"], writes=preg)

        def emit_pv(h2, qc, kt):
            for hh in range(2):
                h = 2 * h2 + hh
                po = (2 * (kt % 2) + hh) * 512
                preg = [f"PB:{2 * (kt % 2) + hh}"]
                pos = (kt * NH + h) * 65
                E("tensor", lambda t, o=po, hh_=hh, ps=pos,
                  first=(kt == 0): t.matmul(
                    S.PS_B[0:65, hh_ * 512:hh_ * 512 + 512],
                    S.AR2[:, ps:ps + 65], S.PB[:, o:o + 512],
                    start=first, stop=True),
                  reads=preg + self.rAR2(pos, pos + 65),
                  writes=[f"psB:{hh}"])

        def emit_stage(qc):
            p = qc % 2
            for hh in range(2):
                E("vector", lambda v, o=hh, pp=p: v.tensor_copy(
                    S.STCM[0:65, pp * 1024 + o * 512: pp * 1024 + (o + 1) * 512],
                    S.PS_B[0:65, o * 512:o * 512 + 512]),
                  reads=[f"psB:{hh}"],
                  writes=[f"STCM:{2 * p + hh}"])

        def make_evac(h2, qc):
            sao_par = qc % 2
            sao_reg = [f"SAO:{sao_par}"]
            p = qc % 2

            def emit_evac():
                for hh in range(2):
                    sreg_ = [f"STCM:{2 * p + hh}"]
                    E("vector", lambda v, o=hh, pp=p: v.reciprocal(
                        S.AR3[0:1, o * 512:(o + 1) * 512],
                        S.STCM[64:65, pp * 1024 + o * 512: pp * 1024 + (o + 1) * 512]),
                      reads=sreg_, writes=["AR3:0"])
                    E("vector", lambda v, o=hh: v.tensor_copy(
                        S.RROW[0:1, o * 512:(o + 1) * 512],
                        S.AR3[0:1, o * 512:(o + 1) * 512]),
                      reads=["AR3:0"], writes=["RROW"])
                    E("tensor", lambda t, o=hh: t.matmul(
                        S.PS_A[0:64, 2 * o * 512: 2 * o * 512 + 512],
                        S.ONES[0:1, 0:64],
                        S.RROW[0:1, o * 512:(o + 1) * 512], start=True, stop=True),
                      reads=["RROW", "ONES"], writes=[f"psA:{2 * hh}"])
                    E("vector", lambda v, o=hh: v.tensor_copy(
                        S.RECB[0:64, o * 512:(o + 1) * 512],
                        S.PS_A[0:64, 2 * o * 512: 2 * o * 512 + 512]),
                      reads=[f"psA:{2 * hh}"], writes=[f"RECB:{hh}"])
                    E("vector", lambda v, o=hh, sp=sao_par, pp=p: v.tensor_mul(
                        S.SAO[o * 64:(o + 1) * 64, sp * 512:(sp + 1) * 512],
                        S.STCM[0:64, pp * 1024 + o * 512: pp * 1024 + (o + 1) * 512],
                        S.RECB[0:64, o * 512:(o + 1) * 512]),
                      reads=sreg_ + [f"RECB:{hh}"], writes=sao_reg)
                E("gpsimd", lambda g, p2=h2, q=qc, sp=sao_par: g.dma_start(
                    aoT_dram[p2 * 128:(p2 + 1) * 128, q * 512:(q + 1) * 512],
                    S.SAO[:, sp * 512:(sp + 1) * 512]),
                  reads=sao_reg, writes=[self.sreg(aoT_dram, qc)], dma=True)
            return emit_evac

        for h2 in range(NH // 2):
            par = h2 % 2
            qbuf = S.QBUF[:, par * 4096:(par + 1) * 4096]
            qregs = self.rQB(par * 4096, (par + 1) * 4096)
            E("sync", lambda g, p=h2, qb=qbuf: g.dma_start(
                qb, qT_dram[p * 128:(p + 1) * 128, :]),
              reads=[self.sreg(qT_dram, k) for k in range(8)],
              writes=qregs, dma=True)
            for qc in range(NTK):
                n_kt = (4 * qc + 4) if causal else 32
                for kt in range(n_kt):
                    emit_scores(h2, qc, kt, qregs, qbuf)
                    emit_mask(qc, kt, n_kt)
                    if kt == 1 and pending is not None:
                        pending()
                        pending = None
                    if kt >= 1:
                        emit_pv(h2, qc, kt - 1)
                if pending is not None:
                    pending()
                    pending = None
                emit_pv(h2, qc, n_kt - 1)
                emit_stage(qc)
                pending = make_evac(h2, qc)
                self.drain_conv_jobs(1)
        pending()

    def proj_ln_phase(self, which):
        E = self.P.emit
        S = self
        is_d = which == "D"
        w_dram = S.wo if is_d else S.proj_w
        aoT = S.s_aoT if is_d else S.s_aoiT
        x_src = S.x_in if is_d else S.s_fin
        x1_dst = S.s_x1 if is_d else S.s_a
        E("gpsimd", lambda g: g.dma_start(
            S.WBUF[:, :].rearrange("p (cc n) -> p cc n", cc=NCC),
            w_dram.rearrange("(cc p) n -> p cc n", p=128)),
          writes=["WBUF"], dma=True)
        if not is_d:
            E("gpsimd", lambda g: g.dma_start(
                S.RWSB[:, :].rearrange("p (cc e) -> p cc e", cc=NCC),
                S.router_w.rearrange("(cc p) e -> p cc e", p=128)),
              writes=["RWSB"], dma=True)
        for tk in range(NTK):
            spar = tk % 2
            slab = S.QBUF[:, spar * 4096:(spar + 1) * 4096]
            sregs = self.rQB(spar * 4096, (spar + 1) * 4096)
            E("sync", lambda g, k=tk, sl=slab: g.dma_start(
                sl.rearrange("p (cc n) -> p cc n", cc=NCC),
                aoT[:, k * 512:(k + 1) * 512].rearrange(
                    "(cc p) n -> p cc n", p=128)),
              reads=[self.sreg(aoT, tk)], writes=sregs, dma=True)
            for ccp in range(NCC):
                bank = ccp % 2
                for cc in range(NCC):
                    E("tensor", lambda t, c=cc, cp=ccp, b=bank, sl=slab: t.matmul(
                        S.psA(b),
                        S.WBUF[:, c * 1024 + cp * 128: c * 1024 + (cp + 1) * 128],
                        sl[:, c * 512:(c + 1) * 512],
                        start=(c == 0), stop=(c == NCC - 1)),
                      reads=["WBUF"] + sregs,
                      writes=[f"psA:{bank}"])
                dst = S.STCM[:, ccp * 512:(ccp + 1) * 512]
                sreg = [f"STCM:{ccp}"]
                if is_d:
                    E("scalar", lambda s, b=bank, d=dst, c=ccp: s.activation(
                        d, S.psA(b), AF.Identity, bias=S.col("bo", c)),
                      reads=[f"psA:{bank}", "COLS"], writes=sreg)
                else:
                    E("scalar", lambda s, b=bank, d=dst: s.activation(
                        d, S.psA(b), AF.Copy),
                      reads=[f"psA:{bank}"], writes=sreg)
            for q in range(4):
                tt = tk * 4 + q
                parc = tt % 2
                for cc in range(NCC):
                    E("tensor", lambda t, c=cc, p=parc, qq=q: t.transpose(
                        S.PS_C[:, p * 1024 + c * 128: p * 1024 + (c + 1) * 128],
                        S.STCM[:, c * 512 + qq * 128: c * 512 + qq * 128 + 128],
                        S.IDENT[:, :]),
                      reads=[f"STCM:{cc}", "IDENT"], writes=[f"psC:{parc}"])
                E("vector", lambda v, p=parc: v.tensor_copy(
                    S.TOK[:, 0:1024], S.PS_C[:, p * 1024:(p + 1) * 1024]),
                  reads=[f"psC:{parc}"], writes=["TOK:0"])
                E("sync", lambda g, i=tt: g.dma_start(
                    S.AR3[:, 0:1024], x_src[i * 128:(i + 1) * 128, :]),
                  reads=[self.sreg(x_src, "t")] if not is_d else [],
                  writes=["AR3:0"], dma=True)
                if is_d:
                    self.ln_stats(S.TOK[:, 0:1024], ["TOK:0"], 0)
                    E("vector", lambda v: v.tensor_scalar(
                        S.AR3[:, 1024:2048], S.TOK[:, 0:1024],
                        S.STAT[:, 0:1], S.STAT[:, 1:2], ALU.subtract, ALU.mult),
                      reads=["TOK:0", "STAT"], writes=["AR3:1"])
                    E("vector", lambda v: v.tensor_add(
                        S.AR3[:, 1024:2048], S.AR3[:, 1024:2048], S.AR3[:, 0:1024]),
                      reads=["AR3:1", "AR3:0"], writes=["AR3:1"])
                else:
                    E("vector", lambda v: v.tensor_add(
                        S.AR3[:, 1024:2048], S.TOK[:, 0:1024], S.AR3[:, 0:1024]),
                      reads=["TOK:0", "AR3:0"], writes=["AR3:1"])
                E("gpsimd", lambda g, i=tt: g.dma_start(
                    x1_dst[i * 128:(i + 1) * 128, :], S.AR3[:, 1024:2048]),
                  reads=["AR3:1"], writes=[self.sreg(x1_dst, "t")], dma=True)
                self.ln_stats(S.AR3[:, 1024:2048], ["AR3:1"], 8)
                E("vector", lambda v: v.tensor_scalar(
                    S.AR3[:, 2048:3072], S.AR3[:, 1024:2048],
                    S.STAT[:, 8:9], S.STAT[:, 9:10], ALU.subtract, ALU.mult),
                  reads=["AR3:1", "STAT"], writes=["AR3:2"])
                if is_d:
                    E("gpsimd", lambda g, i=tt: g.dma_start(
                        S.s_fin[i * 128:(i + 1) * 128, :], S.AR3[:, 2048:3072]),
                      reads=["AR3:2"], writes=[self.sreg(S.s_fin, "t")], dma=True)
                E("vector", lambda v: v.tensor_copy(
                    S.TOK[:, 1024:2048], S.AR3[:, 2048:3072]),
                  reads=["AR3:2"], writes=["TOK:1"])
                for cc in range(NCC):
                    E("tensor", lambda t, c=cc, p=parc: t.transpose(
                        S.PS_C[:, p * 1024 + c * 128: p * 1024 + (c + 1) * 128],
                        S.TOK[:, 1024 + c * 128: 1024 + (c + 1) * 128],
                        S.IDENT[:, :]),
                      reads=["TOK:1", "IDENT"], writes=[f"psC:{parc}"])
                lnw = "ln1w" if is_d else "ln2w"
                lnb = "ln1b" if is_d else "ln2b"
                for cc in range(NCC):
                    if is_d:
                        dsth = self.ar1(cc, tt * 128, 128)
                        rr = self.rAR1_stripe(cc, tt * 128, 128)
                    else:
                        dsth = S.STCM[:, 4096 + cc * 128: 4096 + (cc + 1) * 128]
                        rr = [f"STCM:{8 + cc // 4}"]
                    E("vector", lambda v, c=cc, p=parc, d=dsth, rg=rr: v.tensor_scalar(
                        d, S.PS_C[:, p * 1024 + c * 128: p * 1024 + (c + 1) * 128],
                        S.col(lnw, c), S.col(lnb, c), ALU.mult, ALU.add),
                      reads=[f"psC:{parc}", "COLS"], writes=rr)
                if is_d and S.s_hT is not None:
                    for cc in range(NCC):
                        E("gpsimd", lambda g, c=cc, i=tt: g.dma_start(
                            S.s_hT[c * 128:(c + 1) * 128, i * 128:(i + 1) * 128],
                            self.ar1(c, i * 128, 128)),
                          reads=self.rAR1_stripe(cc, tt * 128, 128),
                          writes=["hTdbg"], dma=True)
                if not is_d:
                    for cc in range(NCC):
                        E("gpsimd", lambda g, c=cc, i=tt: g.dma_start(
                            S.s_h2T[c * 128:(c + 1) * 128, i * 128:(i + 1) * 128],
                            S.STCM[:, 4096 + c * 128: 4096 + (c + 1) * 128]),
                          reads=[f"STCM:{8 + cc // 4}"],
                          writes=[self.sreg(S.s_h2T, tt // 4)], dma=True)
                    for cc in range(NCC):
                        E("tensor", lambda t, c=cc: t.matmul(
                            S.PS_B[:, 0:4],
                            S.STCM[:, 4096 + c * 128: 4096 + (c + 1) * 128],
                            S.RWSB[:, c * 4:(c + 1) * 4],
                            start=(c == 0), stop=(c == NCC - 1)),
                          reads=[f"STCM:{8 + cc // 4}", "RWSB"], writes=["psB:0"])
                    E("vector", lambda v: v.reduce_max(
                        S.STAT[:, 16:17], S.PS_B[:, 0:4], AX.X),
                      reads=["psB:0"], writes=["STAT"])
                    E("vector", lambda v: v.tensor_single_scalar(
                        S.STAT[:, 20:24], S.PS_B[:, 0:4], S.STAT[:, 16:17],
                        ALU.is_ge),
                      reads=["psB:0", "STAT"], writes=["STAT"])
                    E("vector", lambda v, i=tt: v.tensor_copy(
                        S.MCOL[:, i * 4:i * 4 + 1], S.STAT[:, 20:21]),
                      reads=["STAT"], writes=["MCOL"])
                    E("vector", lambda v: v.tensor_scalar(
                        S.STAT[:, 17:18], S.STAT[:, 20:21], -1.0, 1.0,
                        ALU.mult, ALU.add),
                      reads=["STAT"], writes=["STAT"])
                    for e in range(1, NE):
                        E("vector", lambda v, i=tt, ee=e: v.tensor_mul(
                            S.MCOL[:, i * 4 + ee:i * 4 + ee + 1],
                            S.STAT[:, 20 + ee:21 + ee], S.STAT[:, 17:18]),
                          reads=["STAT"], writes=["MCOL"])
                        if e < NE - 1:
                            E("vector", lambda v, i=tt, ee=e: v.tensor_sub(
                                S.STAT[:, 17:18], S.STAT[:, 17:18],
                                S.MCOL[:, i * 4 + ee:i * 4 + ee + 1]),
                              reads=["STAT", "MCOL"], writes=["STAT"])

    def phase_H(self):
        E = self.P.emit
        S = self
        WV_OFF = (8192, 12288)
        PJ_OFF = 16384
        for tc in range(NTC):
            E("sync", lambda g, k=tc: g.dma_start(
                S.AR2[:, 0:8192].rearrange("p (cc n) -> p cc n", cc=NCC),
                S.s_h2T[:, k * 1024:(k + 1) * 1024].rearrange(
                    "(cc p) n -> p cc n", p=128)),
              reads=[self.sreg(S.s_h2T, tc * 2), self.sreg(S.s_h2T, tc * 2 + 1)],
              writes=self.rAR2(0, 8192), dma=True)
            for e in range(NE):
                for g2 in range(16):
                    woff = WV_OFF[g2 % 2]
                    wslot = S.AR2[:, woff:woff + 4096]
                    wregs = self.rAR2(woff, woff + 4096)
                    E("sync", lambda g, ee=e, gg=g2, w=wslot: g.dma_start(
                        w, S.s_wvbf[ee, gg]),
                      reads=["s_wvbf"], writes=wregs, dma=True)
                    for jh in range(2):
                        jc = 2 * g2 + jh
                        for half in range(2):
                            for cc in range(NCC):
                                E("tensor", lambda t, c=cc, h=half, w=wslot,
                                  j=jh: t.matmul(
                                    S.psA(h),
                                    w[:, c * 256 + j * 128: c * 256 + j * 128 + 128],
                                    S.AR2[:, c * 1024 + h * 512: c * 1024 + (h + 1) * 512],
                                    start=(c == 0), stop=(c == NCC - 1)),
                                  reads=wregs[:4] + self.rAR2(cc * 1024, (cc + 1) * 1024),
                                  writes=[f"psA:{half}"])
                            for cc in range(NCC):
                                E("tensor", lambda t, c=cc, h=half, w=wslot,
                                  j=jh: t.matmul(
                                    S.psA(2 + h),
                                    w[:, 2048 + c * 256 + j * 128: 2048 + c * 256 + j * 128 + 128],
                                    S.AR2[:, c * 1024 + h * 512: c * 1024 + (h + 1) * 512],
                                    start=(c == 0), stop=(c == NCC - 1)),
                                  reads=wregs[4:] + self.rAR2(cc * 1024, (cc + 1) * 1024),
                                  writes=[f"psA:{2 + half}"])
                        E("scalar", lambda s: s.activation(
                            S.SILU[:, :], S.PS_A[:, 0:1024], AF.Silu),
                          reads=["psA:0", "psA:1"], writes=["SILU"])
                        E("vector", lambda v, j=jc: v.tensor_mul(
                            S.AR1[:, j * 1024:(j + 1) * 1024], S.SILU[:, :],
                            S.PS_A[:, 1024:2048]),
                          reads=["SILU", "psA:2", "psA:3"],
                          writes=self.rAR1(jc * 1024, (jc + 1) * 1024))
                for grp, tts in enumerate(((0, 1, 2), (3, 4, 5), (6, 7))):
                    for jc in range(NJC):
                        if jc % 4 == 0:
                            poff4 = PJ_OFF + ((jc // 4) % 2) * 4096
                            pslot4 = S.AR2[:, poff4:poff4 + 4096]
                            E("sync", lambda g, ee=e, j4=jc // 4,
                              ps=pslot4: g.dma_start(ps, S.s_pjbf[ee, j4]),
                              reads=["s_pjbf"],
                              writes=self.rAR2(poff4, poff4 + 4096), dma=True)
                        poff = PJ_OFF + ((jc // 4) % 2) * 4096 + (jc % 4) * 1024
                        pslot = S.AR2[:, poff:poff + 1024]
                        pregs = self.rAR2(poff, poff + 1024)
                        for i, tt8 in enumerate(tts):
                            for half in range(2):
                                bank = 2 * i + half
                                bs = S.psA(bank) if bank < 4 else S.psB(bank - 4)
                                breg = (f"psA:{bank}" if bank < 4
                                        else f"psB:{bank - 4}")
                                E("tensor", lambda t, j=jc, t8=tt8, h=half, b=bs,
                                  ps=pslot: t.matmul(
                                    b,
                                    S.AR1[:, j * 1024 + t8 * 128: j * 1024 + t8 * 128 + 128],
                                    ps[:, h * 512:(h + 1) * 512],
                                    start=(j == 0), stop=(j == NJC - 1)),
                                  reads=pregs + self.rAR1(jc * 1024 + tt8 * 128,
                                                          jc * 1024 + tt8 * 128 + 128),
                                  writes=[breg])
                    for i, tt8 in enumerate(tts):
                        tile = tc * 8 + tt8
                        if i < 2:
                            src = S.PS_A[:, 2 * i * 512: 2 * i * 512 + 1024]
                            regs = [f"psA:{2 * i}", f"psA:{2 * i + 1}"]
                        else:
                            src = S.PS_B[:, 0:1024]
                            regs = ["psB:0", "psB:1"]
                        dst = S.QBUF[:, tt8 * 1024:(tt8 + 1) * 1024]
                        dregs = self.rQB(tt8 * 1024, (tt8 + 1) * 1024)
                        if e == 0:
                            E("vector", lambda v, s=src, d=dst, ti=tile: v.tensor_scalar_mul(
                                d, s, S.MCOL[:, ti * 4:ti * 4 + 1]),
                              reads=regs + ["MCOL"], writes=dregs)
                        else:
                            E("vector", lambda v, s=src, d=dst, ti=tile,
                              ee=e: v.scalar_tensor_tensor(
                                d, s, S.MCOL[:, ti * 4 + ee:ti * 4 + ee + 1], d,
                                ALU.mult, ALU.add),
                              reads=regs + ["MCOL"] + dregs, writes=dregs)
            for tt8 in range(8):
                tt = tc * 8 + tt8
                E("gpsimd", lambda g, i=tt: g.dma_start(
                    S.AR3[:, 0:1024], S.s_a[i * 128:(i + 1) * 128, :]),
                  reads=[self.sreg(S.s_a, "t")], writes=["AR3:0"], dma=True)
                E("gpsimd", lambda g, i=tt: g.dma_start(
                    S.AR3[:, 1024:2048], S.s_x1[i * 128:(i + 1) * 128, :]),
                  reads=[self.sreg(S.s_x1, "t")], writes=["AR3:1"], dma=True)
                E("vector", lambda v, t8=tt8: v.tensor_add(
                    S.AR3[:, 2048:3072], S.AR3[:, 0:1024],
                    S.QBUF[:, t8 * 1024:(t8 + 1) * 1024]),
                  reads=["AR3:0"] + self.rQB(tt8 * 1024, (tt8 + 1) * 1024),
                  writes=["AR3:2"])
                self.ln_stats(S.AR3[:, 2048:3072], ["AR3:2"], 24)
                E("vector", lambda v: v.tensor_scalar(
                    S.AR3[:, 0:1024], S.AR3[:, 2048:3072],
                    S.STAT[:, 24:25], S.STAT[:, 25:26], ALU.subtract, ALU.mult),
                  reads=["AR3:2", "STAT"], writes=["AR3:0"])
                E("vector", lambda v: v.tensor_add(
                    S.AR3[:, 0:1024], S.AR3[:, 0:1024], S.AR3[:, 1024:2048]),
                  reads=["AR3:0", "AR3:1"], writes=["AR3:0"])
                E("gpsimd", lambda g, i=tt: g.dma_start(
                    S.y_out[i * 128:(i + 1) * 128, :], S.AR3[:, 0:1024]),
                  reads=["AR3:0"], writes=["y_out"], dma=True)


_BUILD_CACHE = {}


def get_nc(serial=True, upto="H", dbg=()):
    key = (serial, upto, tuple(sorted(dbg)))
    if key not in _BUILD_CACHE:
        _BUILD_CACHE[key] = Build(serial=serial, upto=upto, dbg=dbg).build()
    return _BUILD_CACHE[key]


def make_inmaps(inputs, cores=8):
    ident, identf, perm, ones_row, rope_i, masks = host_constants()
    sv = pack_smallvecs(inputs)
    x = np.ascontiguousarray(np.asarray(inputs["L0_attn_hidden_states"], np.float32))
    f32c = lambda k: np.ascontiguousarray(np.asarray(inputs[k], np.float32))
    shared = dict(
        cos_o=f32c("L0_attn_cos"), sin_o=f32c("L0_attn_sin"),
        wq=f32c("L0_attn_wq"), wk=f32c("L0_attn_wk"), wv=f32c("L0_attn_wv"),
        wo=f32c("L0_attn_wo"), attn_w=f32c("L0_ffn_attn_w"),
        proj_w=f32c("L0_ffn_proj_w"), router_w=f32c("L0_ffn_router_w"),
        ewv=f32c("L0_ffn_expert_wv"), eproj=f32c("L0_ffn_expert_proj"),
        smallv=sv, c_ident=ident, c_identf=identf, c_perm=perm,
        c_ones=ones_row, c_ropei=rope_i, c_masks=masks,
    )
    return [dict(shared, x=np.ascontiguousarray(x[b % x.shape[0]]))
            for b in range(cores)]


def kernel(**inputs):
    nc = get_nc(serial=False, upto="H", dbg=())
    in_maps = make_inmaps(inputs, cores=8)
    res = run_bass_kernel_spmd(nc, in_maps, core_ids=list(range(8)))
    return np.stack([np.asarray(r["y"], np.float32) for r in res.results])


# version 10
# speedup vs baseline: 1.0848x; 1.0364x over previous
import numpy as np
import ml_dtypes
import concourse.bass as bass
import concourse.mybir as mybir
from concourse.bass_utils import run_bass_kernel_spmd

F32 = mybir.dt.float32
BF16 = mybir.dt.bfloat16
AF = mybir.ActivationFunctionType
ALU = mybir.AluOpType
AX = mybir.AxisListType
bfnp = ml_dtypes.bfloat16

T = 4096
C = 1024
NH = 16
HD = 64
NCC = 8
NTT = 32
NTK = 8
EPS = 1e-5
NE = 4
DFF = 4096
NJC = 32
NTC = 4
SCALE = 0.125
STRW = 4112

DMA_QUEUES = ("sync", "gpsimd")
N_SLOTS = 4
ENGS = ("tensor", "vector", "scalar", "gpsimd", "sync")


def _rgn(prefix, lo, hi, gran):
    return [f"{prefix}:{i}" for i in range(lo // gran, (hi + gran - 1) // gran)]


class Prog:

    def __init__(self, serial=False):
        self.items = []
        self.serial = serial

    def emit(self, eng, fn, reads=(), writes=(), dma=False, slot=None):
        self.items.append(dict(eng=eng, fn=fn, reads=tuple(reads),
                               writes=tuple(writes), dma=dma, slot=slot))
        return len(self.items) - 1

    def realize(self, eng_sems, dma_sems):
        n = len(self.items)
        deps = [set() for _ in range(n)]
        if self.serial:
            for i in range(1, n):
                deps[i].add(i - 1)
        else:
            last_w = {}
            readers = {}
            for i, it in enumerate(self.items):
                for r in it["reads"]:
                    if r in last_w:
                        deps[i].add(last_w[r])
                for w in it["writes"]:
                    if w in last_w:
                        deps[i].add(last_w[w])
                    for rd in readers.get(w, ()):
                        deps[i].add(rd)
                for r in it["reads"]:
                    readers.setdefault(r, []).append(i)
                for w in it["writes"]:
                    last_w[w] = i
                    readers[w] = []
        for i, it in enumerate(self.items):
            if it["eng"] == "tensor" and not it["dma"]:
                deps[i] = {j for j in deps[i]
                           if self.items[j]["dma"] or self.items[j]["eng"] != "tensor"}
        dma_idx = {}
        q_counts = {q: 0 for q in DMA_QUEUES}
        default_slots = {"sync": (0, 1, 2, 3), "gpsimd": (0, 1, 2)}
        slot_uses = {(q, s): 0 for q in DMA_QUEUES for s in range(N_SLOTS)}
        for i, it in enumerate(self.items):
            if it["dma"]:
                q = it["eng"]
                if it.get("slot") is not None:
                    s = it["slot"]
                else:
                    cands = default_slots[q]
                    s = cands[q_counts[q] % len(cands)]
                    q_counts[q] += 1
                slot_uses[(q, s)] += 1
                dma_idx[i] = (q, s, slot_uses[(q, s)])
        need_mark = [False] * n
        for i in range(n):
            for j in deps[i]:
                if not self.items[j]["dma"]:
                    need_mark[j] = True
        comp_seq = {}
        ctr = {e: 0 for e in ENGS}
        for i, it in enumerate(self.items):
            if not it["dma"] and need_mark[i]:
                ctr[it["eng"]] += 1
                comp_seq[i] = (it["eng"], ctr[it["eng"]])
        final_counts = dict(ctr)
        per_eng = {e: [] for e in ENGS}
        hwm = {e: {} for e in ENGS}

        def mk_closure(i):
            it = self.items[i]
            eng_name = it["eng"]
            waits = []
            for j in sorted(deps[i]):
                jt = self.items[j]
                if jt["dma"]:
                    q, s, use = dma_idx[j]
                    waits.append((("d", q, s), 16 * use))
                else:
                    e2, seq = comp_seq[j]
                    waits.append((("c", e2), seq))
            if it["dma"]:
                q, s, use = dma_idx[i]
                if use > 1:
                    waits.append((("d", q, s), 16 * (use - 1)))
            h = hwm[eng_name]
            real_waits = []
            for key, val in waits:
                if h.get(key, 0) >= val:
                    continue
                h[key] = val
                real_waits.append((key, val))
            if it["dma"]:
                q, s, use = dma_idx[i]
                inc = (("d", q, s), 16)
            elif need_mark[i]:
                inc = (("c", eng_name), 1)
            else:
                inc = None

            def closure(eng):
                for key, val in real_waits:
                    sem = dma_sems[key[1]][key[2]] if key[0] == "d" else eng_sems[key[1]]
                    eng.wait_ge(sem, val)
                ins = it["fn"](eng)
                if inc is not None:
                    key, amt = inc
                    sem = dma_sems[key[1]][key[2]] if key[0] == "d" else eng_sems[key[1]]
                    ins.then_inc(sem, amt)
            return closure

        for i in range(n):
            per_eng[self.items[i]["eng"]].append(mk_closure(i))

        def final_join(eng):
            for q in DMA_QUEUES:
                for s in range(N_SLOTS):
                    if slot_uses[(q, s)]:
                        eng.wait_ge(dma_sems[q][s], 16 * slot_uses[(q, s)])
            for e, cnt in final_counts.items():
                if cnt and e != "sync":
                    eng.wait_ge(eng_sems[e], cnt)
        per_eng["sync"].append(final_join)
        return per_eng


def host_constants():
    ident = np.eye(128, dtype=bfnp)
    identf = np.eye(128, dtype=np.float32)
    perm = np.zeros((128, 128), dtype=bfnp)
    for h in range(2):
        b = 64 * h
        for i in range(32):
            perm[b + i, b + 32 + i] = 1.0
            perm[b + 32 + i, b + i] = 1.0
    ones_row = np.ones((1, 128), dtype=bfnp)
    inv_freq = 1.0 / (10000.0 ** (np.arange(0, HD, 2, dtype=np.float32) / HD))
    t = np.arange(T, dtype=np.float32)
    emb = np.concatenate([np.outer(t, inv_freq), np.outer(t, inv_freq)], axis=-1)
    cosT = np.cos(emb).T.astype(np.float32)
    sinT = np.sin(emb).T.astype(np.float32)
    sinT[0:32] = -sinT[0:32]
    rope_i = np.zeros((128, 2 * T), dtype=bfnp)
    rope_i[0:64, 0:T] = cosT.astype(bfnp)
    rope_i[64:128, 0:T] = cosT.astype(bfnp)
    rope_i[0:64, T:] = sinT.astype(bfnp)
    rope_i[64:128, T:] = sinT.astype(bfnp)
    masks = np.zeros((128, 4 * 512), dtype=bfnp)
    r = np.arange(128)[:, None]
    j = np.arange(512)[None, :]
    for b in range(4):
        masks[:, b * 512:(b + 1) * 512] = (j >= 128 * b + r).astype(bfnp)
    bmap = np.zeros((16, 1024), dtype=bfnp)
    for c in range(8):
        bmap[2 * c, c * 128: c * 128 + 64] = 1.0
        bmap[2 * c + 1, c * 128 + 64: c * 128 + 128] = 1.0
    return ident, identf, perm, ones_row, rope_i, masks, bmap


SV = {k: i for i, k in enumerate(
    ["bq", "bv", "bo", "ln1w", "ln1b", "ln2w", "ln2b", "w0", "w1", "w2"])}


def pack_smallvecs(ins):
    names = ["L0_attn_bq", "L0_attn_bv", "L0_attn_bo", "L0_ffn_ln1_w",
             "L0_ffn_ln1_b", "L0_ffn_ln2_w", "L0_ffn_ln2_b"]
    rows = [np.asarray(ins[n], np.float32).reshape(C) for n in names]
    cw = np.asarray(ins["L0_ffn_conv_w"], np.float32).reshape(C, 3)
    rows += [np.ascontiguousarray(cw[:, i]) for i in range(3)]
    return np.stack(rows)


class Build:
    def __init__(self, serial=True, upto="H", dbg=()):
        self.serial = serial
        self.upto = upto
        self.dbg = set(dbg)
        self.P = Prog(serial=serial)

    def rAR1(self, lo, hi):
        return _rgn("AR1", lo, hi, 512)

    def rAR1_stripe(self, cc, lo, w):
        a = cc * STRW + 2 + lo
        return self.rAR1(a, a + w)

    def rAR2(self, lo, hi):
        return _rgn("AR2", lo, hi, 512)

    def rQB(self, lo, hi):
        return _rgn("QB", lo, hi, 512)

    def rAR3(self, slot):
        return [f"AR3:{slot}"]

    def ar1(self, cc, lo, w):
        return self.AR1[:, cc * STRW + 2 + lo: cc * STRW + 2 + lo + w]

    def psA(self, b, w=512):
        return self.PS_A[:, b * 512: b * 512 + w]

    def psB(self, b, w=512):
        return self.PS_B[:, b * 512: b * 512 + w]

    def col(self, vname, cc):
        i = SV[vname] * 8 + cc
        return self.COLS[:, i:i + 1]

    def sreg(self, ap, sub):
        return f"{ap.tensor.name}:{sub}"

    def build(self):
        nc = bass.Bass("TRN2", target_bir_lowering=False)
        self.nc = nc
        PH = "ABCDEFGH"
        on = {p: PH.index(p) <= PH.index(self.upto) for p in PH}

        din = lambda name, shape, dt: nc.dram_tensor(name, shape, dt, kind="ExternalInput").ap()

        def scratch(name, shape, dt):
            kind = "ExternalOutput" if name in self.dbg else "Internal"
            return nc.dram_tensor(name, shape, dt, kind=kind).ap()

        self.x_in = din("x", [T, C], F32)
        self.cos_o = din("cos_o", [1024, HD], F32)
        self.sin_o = din("sin_o", [1024, HD], F32)
        self.wq = din("wq", [C, C], F32)
        self.wk = din("wk", [C, C], F32)
        self.wv = din("wv", [C, C], F32)
        self.wo = din("wo", [C, C], F32)
        self.attn_w = din("attn_w", [C, 3 * C], F32)
        self.proj_w = din("proj_w", [C, C], F32)
        self.router_w = din("router_w", [C, NE], F32)
        self.ewv = din("ewv", [NE, C, 2 * DFF], F32)
        self.eproj = din("eproj", [NE, DFF, C], F32)
        self.smallv = din("smallv", [10, C], F32)
        self.c_ident = din("c_ident", [128, 128], BF16)
        self.c_identf = din("c_identf", [128, 128], F32)
        self.c_perm = din("c_perm", [128, 128], BF16)
        self.c_ones = din("c_ones", [1, 128], BF16)
        self.c_ropei = din("c_ropei", [128, 2 * T], BF16)
        self.c_masks = din("c_masks", [128, 4 * 512], BF16)
        self.c_bmap = din("c_bmap", [16, 1024], BF16)

        self.y_out = nc.dram_tensor("y", [T, C], F32, kind="ExternalOutput").ap()

        self.s_ln0T = scratch("s_ln0T", [C, T], BF16)
        self.s_qT = scratch("s_qT", [C, T], BF16)
        self.s_aoT = scratch("s_aoT", [C, T], BF16)
        self.s_x1 = scratch("s_x1", [T, C], F32)
        self.s_fin = scratch("s_fin", [T, C], F32)
        self.s_hcT = scratch("s_hcT", [C, T], BF16)
        self.s_qiT = scratch("s_qiT", [C, T], BF16)
        self.s_aoiT = scratch("s_aoiT", [C, T], BF16)
        self.s_a = scratch("s_a", [T, C], F32)
        self.s_h2T = scratch("s_h2T", [C, T], BF16)
        self.s_den = scratch("s_den", [NH, T], BF16)
        self.s_deni = scratch("s_deni", [NH, T], BF16)
        self.s_ropeo = scratch("s_ropeo", [128, 2048], BF16)
        self.s_wvbf = scratch("s_wvbf", [NE, 16, 128, 4096], BF16)
        self.s_pjbf = scratch("s_pjbf", [NE, 8, 128, 4096], BF16)
        self.s_hT = scratch("s_hT", [C, T], BF16) if "s_hT" in self.dbg else None

        from contextlib import ExitStack
        ctx = self.ctx = ExitStack()
        sb = lambda name, shape, dt: ctx.enter_context(nc.sbuf_tensor(name, shape, dt))
        pst = lambda name, shape, dt: ctx.enter_context(nc.psum_tensor(name, shape, dt))

        self.AR1 = sb("AR1", [128, NCC * STRW], BF16)
        self.AR2 = sb("AR2", [128, 33280], BF16)
        self.AR3 = sb("AR3", [128, 3072], F32)
        self.WBUF = sb("WBUF", [128, 8192], BF16)
        self.QBUF = sb("QBUF", [128, 8192], BF16)
        self.MSK = sb("MSK", [128, 2048], BF16)
        self.BVB = self.MSK
        self._msk_loaded = False
        self._conv_jobs = self._make_conv_jobs()
        self.IDENT = sb("IDENT", [128, 128], BF16)
        self.IDENTF = sb("IDENTF", [128, 128], F32)
        self.PERM = sb("PERM", [128, 128], BF16)
        self.ONES = sb("ONES", [1, 128], BF16)
        self.COLS = sb("COLS", [128, 80], F32)
        self.PB = sb("PB", [128, 4 * 512], BF16)
        self.BMAP = sb("BMAP", [16, 1024], BF16)
        self.DENL = sb("DENL", [16, 512], BF16)
        self.DENR = sb("DENR", [16, 512], F32)
        self.DENB = sb("DENB", [16, 512], BF16)
        self.RT = sb("RT", [128, 512], BF16)
        self.TOK = sb("TOK", [128, 2 * C], BF16)
        self.STAT = sb("STAT", [128, 32], F32)
        self.MCOL = sb("MCOL", [128, 128], F32)
        self.SVTMP = sb("SVTMP", [80, 128], F32)
        self.ROPO = sb("ROPO", [128, 64], BF16)
        self.SILU = sb("SILU", [128, C], BF16)
        self.JUNK = self.SILU
        self.RWSB = sb("RWSB", [128, 32], BF16)
        self.STCM = sb("STCM", [128, 5120], BF16)

        self.PS_A = pst("PS_A", [128, 2048], F32)
        self.PS_B = pst("PS_B", [128, 1024], F32)
        self.PS_C = pst("PS_C", [128, 2048], BF16)

        self.setup_consts()
        if on["A"]:
            self.phase_A()
        if on["B"]:
            self.phase_B()
        if on["C"]:
            self.attention(self.s_qT, self.s_aoT, self.s_den, causal=False)
        if on["D"]:
            self.proj_ln_phase(which="D")
        if on["E"]:
            self.phase_E()
        if on["F"]:
            self.attention(self.s_qiT, self.s_aoiT, self.s_deni, causal=True)
        if on["G"]:
            self.proj_ln_phase(which="G")
        if on["H"]:
            self.phase_H()

        eng_sems = {e: ctx.enter_context(nc.semaphore(f"se_{e}")) for e in ENGS}
        dma_sems = {q: [ctx.enter_context(nc.semaphore(f"sd_{q}{s}"))
                        for s in range(N_SLOTS)] for q in DMA_QUEUES}
        block = ctx.enter_context(nc.Block())
        per_eng = self.P.realize(eng_sems, dma_sems)

        @block.tensor
        def _(eng):
            for cl in per_eng["tensor"]:
                cl(eng)

        @block.vector
        def _(eng):
            for cl in per_eng["vector"]:
                cl(eng)

        @block.scalar
        def _(eng):
            for cl in per_eng["scalar"]:
                cl(eng)

        @block.gpsimd
        def _(eng):
            for cl in per_eng["gpsimd"]:
                cl(eng)

        @block.sync
        def _(eng):
            for cl in per_eng["sync"]:
                cl(eng)

        ctx.close()
        return nc

    def setup_consts(self):
        E = self.P.emit
        S = self
        E("sync", lambda g: g.dma_start(S.IDENT[:, :], S.c_ident), writes=["IDENT"], dma=True)
        E("sync", lambda g: g.dma_start(S.IDENTF[:, :], S.c_identf), writes=["IDENTF"], dma=True)
        E("sync", lambda g: g.dma_start(S.PERM[:, :], S.c_perm), writes=["PERM"], dma=True)
        E("sync", lambda g: g.dma_start(S.ONES[:, :], S.c_ones), writes=["ONES"], dma=True)
        E("sync", lambda g: g.dma_start(S.BMAP[:, :], S.c_bmap), writes=["BMAP"], dma=True)
        E("sync", lambda g: g.dma_start(
            S.SVTMP[:, :], S.smallv.rearrange("v (cc p) -> (v cc) p", p=128)),
          writes=["SVTMP"], dma=True)
        E("tensor", lambda t: t.transpose(S.PS_A[:, 0:80], S.SVTMP[:, :], S.IDENTF[0:80, 0:80]),
          reads=["SVTMP", "IDENTF"], writes=["psA:0"])
        E("vector", lambda v: v.tensor_copy(S.COLS[:, :], S.PS_A[:, 0:80]),
          reads=["psA:0"], writes=["COLS"])
        E("sync", lambda g: g.dma_start(S.AR3[0:1, 0:1024],
                                        S.smallv[SV["bv"]:SV["bv"] + 1, :]),
          writes=["AR3:0"], dma=True)
        E("vector", lambda v: v.tensor_copy(S.TOK[0:1, 0:1024], S.AR3[0:1, 0:1024]),
          reads=["AR3:0"], writes=["TOK:0"])
        for h in range(2):
            E("tensor", lambda t, hh=h: t.matmul(
                S.psA(hh), S.ONES[0:1, 0:128], S.TOK[0:1, hh * 512:(hh + 1) * 512],
                start=True, stop=True),
              reads=["TOK:0", "ONES"], writes=[f"psA:{h}"])
        E("vector", lambda v: v.tensor_copy(S.BVB[:, 0:1024], S.PS_A[:, 0:1024]),
          reads=["psA:0", "psA:1"], writes=["MSK:0", "MSK:1"])
        for w, src in ((0, S.cos_o), (1, S.sin_o)):
            for i in range(8):
                E("gpsimd", lambda g, s=src, ii=i: g.dma_start(
                    S.ROPO[:, 0:64], s[ii * 128:(ii + 1) * 128, :]),
                  writes=["ROPO"], dma=True)
                E("tensor", lambda t: t.transpose(S.PS_C[0:64, 0:128],
                                                  S.ROPO[:, 0:64], S.IDENT[:, :]),
                  reads=["ROPO", "IDENT"], writes=["psC:0"])
                E("vector", lambda v, ww=w, ii=i: v.tensor_copy(
                    S.TOK[0:64, ww * 1024 + ii * 128: ww * 1024 + (ii + 1) * 128],
                    S.PS_C[0:64, 0:128]),
                  reads=["psC:0"], writes=[f"TOK:{w}"])
        E("vector", lambda v: v.tensor_scalar_mul(S.TOK[0:32, 1024:2048],
                                                  S.TOK[0:32, 1024:2048], -1.0),
          reads=["TOK:1"], writes=["TOK:1"])
        for r in range(2):
            E("gpsimd", lambda g, rr=r: g.dma_start(
                S.s_ropeo[rr * 64:(rr + 1) * 64, :], S.TOK[0:64, 0:2048]),
              reads=["TOK:0", "TOK:1"], writes=["s_ropeo"], dma=True)

    def ln_stats(self, src_ap, src_regions, o):
        E = self.P.emit
        S = self
        E("vector", lambda v: v.reduce_sum(S.STAT[:, o + 2:o + 3], src_ap, AX.X),
          reads=list(src_regions), writes=["STAT"])
        E("scalar", lambda s: s.activation(S.JUNK[:, :], src_ap, AF.Square,
                                           accum_out=S.STAT[:, o + 3:o + 4]),
          reads=list(src_regions), writes=["SILU", "STAT"])
        E("vector", lambda v: v.tensor_scalar_mul(S.STAT[:, o:o + 1],
                                                  S.STAT[:, o + 2:o + 3], 1.0 / C),
          reads=["STAT"], writes=["STAT"])
        E("vector", lambda v: v.tensor_scalar_mul(S.STAT[:, o + 4:o + 5],
                                                  S.STAT[:, o + 3:o + 4], 1.0 / C),
          reads=["STAT"], writes=["STAT"])
        E("vector", lambda v: v.scalar_tensor_tensor(
            S.STAT[:, o + 5:o + 6], S.STAT[:, o:o + 1], S.STAT[:, o:o + 1],
            S.STAT[:, o + 4:o + 5], ALU.mult, ALU.subtract),
          reads=["STAT"], writes=["STAT"])
        E("vector", lambda v: v.tensor_scalar(S.STAT[:, o + 5:o + 6],
                                              S.STAT[:, o + 5:o + 6], -1.0, EPS,
                                              ALU.mult, ALU.add),
          reads=["STAT"], writes=["STAT"])
        E("scalar", lambda s: s.activation(S.STAT[:, o + 6:o + 7],
                                           S.STAT[:, o + 5:o + 6], AF.Sqrt),
          reads=["STAT"], writes=["STAT"])
        E("vector", lambda v: v.reciprocal(S.STAT[:, o + 1:o + 2],
                                           S.STAT[:, o + 6:o + 7]),
          reads=["STAT"], writes=["STAT"])

    def phase_A(self):
        E = self.P.emit
        S = self
        xs = S.AR3[:, 0:1024]
        for tt in range(NTT):
            par = tt % 2
            E("sync", lambda g, i=tt: g.dma_start(xs, S.x_in[i * 128:(i + 1) * 128, :]),
              writes=["AR3:0"], dma=True)
            self.ln_stats(xs, ["AR3:0"], 0)
            E("vector", lambda v: v.tensor_scalar(
                S.TOK[:, 0:1024], xs, S.STAT[:, 0:1], S.STAT[:, 1:2],
                ALU.subtract, ALU.mult),
              reads=["AR3:0", "STAT"], writes=["TOK:0"])
            for cc in range(NCC):
                E("tensor", lambda t, c=cc, p=par: t.transpose(
                    S.PS_C[:, p * 1024 + c * 128: p * 1024 + (c + 1) * 128],
                    S.TOK[:, c * 128:(c + 1) * 128], S.IDENT[:, :]),
                  reads=["TOK:0", "IDENT"], writes=[f"psC:{par}"])
            q = tt % 4
            for cc in range(NCC):
                E("vector", lambda v, c=cc, p=par, qq=q: v.tensor_copy(
                    S.QBUF[:, c * 512 + qq * 128: c * 512 + qq * 128 + 128],
                    S.PS_C[:, p * 1024 + c * 128: p * 1024 + (c + 1) * 128]),
                  reads=[f"psC:{par}"], writes=self.rQB(cc * 512, cc * 512 + 512))
            if q == 3:
                tk = tt // 4
                E("gpsimd", lambda g, k=tk: g.dma_start(
                    S.s_ln0T[:, k * 512:(k + 1) * 512].rearrange(
                        "(cc p) n -> p cc n", p=128),
                    S.QBUF[:, 0:4096].rearrange("p (cc n) -> p cc n", cc=NCC)),
                  reads=self.rQB(0, 4096),
                  writes=[self.sreg(S.s_ln0T, tk)], dma=True)

    def qk_sweep(self, w_dram, w_cols, srcT, dstT, to_ar1, rope_tbl_off,
                 rope_tks, bias_name):
        E = self.P.emit
        S = self
        lo, hi = w_cols
        E("gpsimd", lambda g: g.dma_start(
            S.WBUF[:, :].rearrange("p (cc n) -> p cc n", cc=NCC),
            w_dram[:, lo:hi].rearrange("(cc p) n -> p cc n", p=128)),
          writes=["WBUF"], dma=True)
        for tk in range(NTK):
            spar = tk % 2
            slab = S.QBUF[:, spar * 4096:(spar + 1) * 4096]
            sregs = self.rQB(spar * 4096, (spar + 1) * 4096)
            E("sync", lambda g, k=tk, sl=slab: g.dma_start(
                sl.rearrange("p (cc n) -> p cc n", cc=NCC),
                srcT[:, k * 512:(k + 1) * 512].rearrange(
                    "(cc p) n -> p cc n", p=128)),
              reads=[self.sreg(srcT, tk)], writes=sregs, dma=True)
            for ccp in range(NCC):
                bank = ccp % 2
                for cc in range(NCC):
                    E("tensor", lambda t, c=cc, cp=ccp, b=bank, sl=slab: t.matmul(
                        S.psA(b),
                        S.WBUF[:, c * 1024 + cp * 128: c * 1024 + (cp + 1) * 128],
                        sl[:, c * 512:(c + 1) * 512],
                        start=(c == 0), stop=(c == NCC - 1)),
                      reads=["WBUF"] + sregs,
                      writes=[f"psA:{bank}"])
                if to_ar1:
                    out_ap = self.ar1(ccp, tk * 512, 512)
                    out_reg = self.rAR1_stripe(ccp, tk * 512, 512)
                else:
                    po = (tk % 2) * 512
                    out_ap = S.PB[:, po:po + 512]
                    out_reg = [f"PB:{tk % 2}"]
                if bias_name is not None:
                    E("scalar", lambda s, b=bank, o=out_ap, c=ccp: s.activation(
                        o, S.psA(b), AF.Identity, bias=S.col(bias_name, c)),
                      reads=[f"psA:{bank}", "COLS"], writes=out_reg)
                else:
                    E("scalar", lambda s, b=bank, o=out_ap: s.activation(
                        o, S.psA(b), AF.Copy),
                      reads=[f"psA:{bank}"], writes=out_reg)
                if rope_tbl_off is not None and tk in rope_tks:
                    co, so = rope_tbl_off(tk)
                    pb2o = 1024 + (tk % 2) * 512
                    pb2 = S.PB[:, pb2o:pb2o + 512]
                    pb2r = [f"PB:{2 + tk % 2}"]
                    bank2 = ccp % 2
                    E("tensor", lambda t, o=out_ap, b=bank2: t.matmul(
                        S.psB(b), S.PERM[:, :], o, start=True, stop=True),
                      reads=out_reg + ["PERM"], writes=[f"psB:{bank2}"])
                    E("vector", lambda v, b=bank2, p=pb2, s=so: v.tensor_mul(
                        p, S.psB(b), S.AR2[:, s:s + 512]),
                      reads=[f"psB:{bank2}"] + self.rAR2(so, so + 512),
                      writes=pb2r)
                    E("vector", lambda v, o=out_ap, c=co: v.tensor_mul(
                        S.RT[:, 0:512], o, S.AR2[:, c:c + 512]),
                      reads=out_reg + self.rAR2(co, co + 512), writes=["RT"])
                    E("vector", lambda v, o=out_ap, p=pb2: v.tensor_add(
                        o, S.RT[:, 0:512], p),
                      reads=["RT"] + pb2r, writes=out_reg)
                if not to_ar1:
                    E("gpsimd", lambda g, o=out_ap, c=ccp, k=tk: g.dma_start(
                        dstT[c * 128:(c + 1) * 128, k * 512:(k + 1) * 512], o),
                      reads=out_reg, writes=[self.sreg(dstT, tk)], dma=True)

    def v_sweep(self, w_dram, w_cols, srcT, with_bias):
        E = self.P.emit
        S = self
        lo, hi = w_cols
        E("gpsimd", lambda g: g.dma_start(
            S.WBUF[:, :].rearrange("p (cc n) -> p cc n", cc=NCC),
            w_dram[:, lo:hi].rearrange("(cc p) n -> p cc n", p=128)),
          writes=["WBUF"], dma=True)
        E("vector", lambda v: v.memset(S.AR2[:, 64:33280:65], 1.0),
          writes=self.rAR2(0, 33280))
        for tt in range(NTT):
            tk, q = tt // 4, tt % 4
            spar = tk % 2
            slab = S.QBUF[:, spar * 4096:(spar + 1) * 4096]
            sregs = self.rQB(spar * 4096, (spar + 1) * 4096)
            if q == 0:
                E("sync", lambda g, k=tk, sl=slab: g.dma_start(
                    sl.rearrange("p (cc n) -> p cc n", cc=NCC),
                    srcT[:, k * 512:(k + 1) * 512].rearrange(
                        "(cc p) n -> p cc n", p=128)),
                  reads=[self.sreg(srcT, tk)], writes=sregs, dma=True)
            for half in range(2):
                bank = 2 + half
                for cc in range(NCC):
                    E("tensor", lambda t, c=cc, h=half, b=bank, qq=q, sl=slab: t.matmul(
                        S.psA(b),
                        sl[:, c * 512 + qq * 128: c * 512 + qq * 128 + 128],
                        S.WBUF[:, c * 1024 + h * 512: c * 1024 + (h + 1) * 512],
                        start=(c == 0), stop=(c == NCC - 1)),
                      reads=["WBUF"] + sregs,
                      writes=[f"psA:{bank}"])
            for h in range(NH):
                bank = 2 + (h // 8)
                off = 1024 + (h % 8) * 64 + (h // 8) * 512
                pos = (tt * NH + h) * 65
                dst = S.AR2[:, pos:pos + 64]
                regs = self.rAR2(pos, pos + 64)
                if with_bias:
                    E("vector", lambda v, d=dst, o=off, hh=h: v.scalar_tensor_tensor(
                        d, S.PS_A[:, o:o + 64], 1.0, S.BVB[:, hh * 64:(hh + 1) * 64],
                        ALU.mult, ALU.add),
                      reads=[f"psA:{bank}", "MSK:0", "MSK:1"], writes=regs)
                else:
                    E("vector", lambda v, d=dst, o=off: v.tensor_copy(
                        d, S.PS_A[:, o:o + 64]),
                      reads=[f"psA:{bank}"], writes=regs)

    def phase_B(self):
        E = self.P.emit
        S = self
        E("sync", lambda g: g.dma_start(S.AR2[:, 0:2048], S.s_ropeo),
          reads=["s_ropeo"], writes=self.rAR2(0, 2048), dma=True)
        rope_off = lambda tk: ((tk - 6) * 512, 1024 + (tk - 6) * 512)
        self.qk_sweep(S.wq, (0, C), S.s_ln0T, S.s_qT, False, rope_off, (6, 7), "bq")
        self.qk_sweep(S.wk, (0, C), S.s_ln0T, None, True, rope_off, (6, 7), None)
        self.v_sweep(S.wv, (0, C), S.s_ln0T, True)

    def phase_E(self):
        E = self.P.emit
        S = self
        for cc in range(NCC):
            E("vector", lambda v, c=cc: v.memset(
                S.AR1[:, c * STRW: c * STRW + 2], 0.0),
              writes=self.rAR1(cc * STRW, cc * STRW + 2))
        for cc in range(NCC):
            for tk in range(NTK):
                rd = self.rAR1(cc * STRW + tk * 512, cc * STRW + 2 + (tk + 1) * 512)
                E("vector", lambda v, c=cc, k=tk: v.tensor_scalar_mul(
                    S.TOK[:, 0:512],
                    S.AR1[:, c * STRW + k * 512: c * STRW + k * 512 + 512],
                    S.col("w0", c)),
                  reads=rd + ["COLS"], writes=["TOK:0"])
                E("vector", lambda v, c=cc, k=tk: v.scalar_tensor_tensor(
                    S.TOK[:, 512:1024],
                    S.AR1[:, c * STRW + 1 + k * 512: c * STRW + 1 + k * 512 + 512],
                    S.col("w1", c), S.TOK[:, 0:512], ALU.mult, ALU.add),
                  reads=rd + ["COLS", "TOK:0"], writes=["TOK:0"])
                E("vector", lambda v, c=cc, k=tk: v.scalar_tensor_tensor(
                    S.TOK[:, 0:512], self.ar1(c, k * 512, 512),
                    S.col("w2", c), S.TOK[:, 512:1024], ALU.mult, ALU.add),
                  reads=rd + ["COLS", "TOK:0"], writes=["TOK:0"])
                E("gpsimd", lambda g, c=cc, k=tk: g.dma_start(
                    S.s_hcT[c * 128:(c + 1) * 128, k * 512:(k + 1) * 512],
                    S.TOK[:, 0:512]),
                  reads=["TOK:0"], writes=[self.sreg(S.s_hcT, tk)], dma=True)
        E("sync", lambda g: g.dma_start(S.AR2[:, 0:4096], S.c_ropei[:, 0:4096]),
          writes=self.rAR2(0, 4096), dma=True)
        E("sync", lambda g: g.dma_start(S.AR2[:, 4096:8192], S.c_ropei[:, 4096:8192]),
          writes=self.rAR2(4096, 8192), dma=True)
        rope_off = lambda tk: (tk * 512, 4096 + tk * 512)
        self.qk_sweep(S.attn_w, (0, C), S.s_hcT, S.s_qiT, False, rope_off,
                      tuple(range(8)), None)
        self.qk_sweep(S.attn_w, (C, 2 * C), S.s_hcT, None, True, rope_off,
                      tuple(range(8)), None)
        self.v_sweep(S.attn_w, (2 * C, 3 * C), S.s_hcT, False)


    def _make_conv_jobs(self):
        S = self
        jobs = []
        for e in range(NE):
            for g2 in range(16):
                def j(E, ee=e, gg=g2):
                    E("gpsimd", lambda g: g.dma_start(
                        S.s_wvbf[ee, gg].rearrange("p (gv cc j) -> p gv cc j",
                                                   gv=2, cc=NCC),
                        S.ewv[ee].rearrange(
                            "(cc p) (gv gr j) -> gr p gv cc j",
                            p=128, gv=2, j=256)[gg]),
                      writes=["s_wvbf"], dma=True, slot=3)
                jobs.append(j)
            for j4 in range(8):
                def j(E, ee=e, jj4=j4):
                    E("gpsimd", lambda g: g.dma_start(
                        S.s_pjbf[ee, jj4].rearrange("p (jj n) -> p jj n", jj=4),
                        S.eproj[ee, jj4 * 512:(jj4 + 1) * 512, :].rearrange(
                            "(jj p) n -> p jj n", p=128)),
                      writes=["s_pjbf"], dma=True, slot=3)
                jobs.append(j)
        return jobs

    def drain_conv_jobs(self, k):
        for _ in range(k):
            if self._conv_jobs:
                self._conv_jobs.pop(0)(self.P.emit)

    def attention(self, qT_dram, aoT_dram, den_dram, causal):
        E = self.P.emit
        S = self
        if causal and not self._msk_loaded:
            self._msk_loaded = True
            E("sync", lambda g: g.dma_start(S.MSK[:, :], S.c_masks),
              writes=["MSK:0", "MSK:1"], dma=True)

        def emit_scores(h2, qc, kt, qregs, qbuf):
            par = kt % 2
            for hh in range(2):
                sb_ = 2 * par + hh
                kregs = self.rAR1_stripe(h2, kt * 128, 128)
                E("tensor", lambda t, p=h2, k=kt, b=sb_, r=hh, qb=qbuf,
                  q=qc: t.matmul(
                    S.psA(b),
                    S.AR1[r * 64:r * 64 + 64,
                          p * STRW + 2 + k * 128: p * STRW + 2 + k * 128 + 128],
                    qb[r * 64:r * 64 + 64, q * 512:(q + 1) * 512],
                    start=True, stop=True, tile_position=(r * 64, 0)),
                  reads=kregs + qregs, writes=[f"psA:{sb_}"])
            E("scalar", lambda s, pp=par: s.activation(
                S.PB[:, pp * 1024:(pp + 1) * 1024],
                S.PS_A[:, pp * 1024:(pp + 1) * 1024], AF.Exp, scale=SCALE),
              reads=[f"psA:{2 * par}", f"psA:{2 * par + 1}"],
              writes=[f"PB:{2 * par}", f"PB:{2 * par + 1}"])

        def emit_mask(qc, kt, n_kt):
            if not (causal and kt >= n_kt - 4):
                return
            bnd = kt - (n_kt - 4)
            for hh in range(2):
                po = (2 * (kt % 2) + hh) * 512
                preg = [f"PB:{2 * (kt % 2) + hh}"]
                E("vector", lambda v, o=po, bb=bnd: v.tensor_mul(
                    S.PB[:, o:o + 512], S.PB[:, o:o + 512],
                    S.MSK[:, bb * 512:(bb + 1) * 512]),
                  reads=preg + [f"MSK:{bnd // 2}"], writes=preg)

        def emit_pv(h2, qc, kt):
            for hh in range(2):
                h = 2 * h2 + hh
                po = (2 * (kt % 2) + hh) * 512
                preg = [f"PB:{2 * (kt % 2) + hh}"]
                pos = (kt * NH + h) * 65
                E("tensor", lambda t, o=po, hh_=hh, ps=pos,
                  first=(kt == 0): t.matmul(
                    S.PS_B[0:65, hh_ * 512:hh_ * 512 + 512],
                    S.AR2[:, ps:ps + 65], S.PB[:, o:o + 512],
                    start=first, stop=True),
                  reads=preg + self.rAR2(pos, pos + 65),
                  writes=[f"psB:{hh}"])

        def emit_stage(qc):
            p = qc % 2
            for hh in range(2):
                E("vector", lambda v, o=hh, pp=p: v.tensor_copy(
                    S.STCM[0:65, pp * 1024 + o * 512: pp * 1024 + (o + 1) * 512],
                    S.PS_B[0:65, o * 512:o * 512 + 512]),
                  reads=[f"psB:{hh}"],
                  writes=[f"STCM:{2 * p + hh}"])

        def emit_evac(h2, qc):
            p = qc % 2
            sregs = [f"STCM:{2 * p}", f"STCM:{2 * p + 1}"]
            E("gpsimd", lambda g, p2=h2, pp=p, q=qc: g.dma_start(
                aoT_dram[p2 * 128:(p2 + 1) * 128,
                         q * 512:(q + 1) * 512].rearrange(
                    "(hh pr) t -> pr hh t", hh=2),
                S.STCM[0:64, pp * 1024:(pp + 1) * 1024].rearrange(
                    "pr (hh t) -> pr hh t", hh=2)),
              reads=sregs, writes=[self.sreg(aoT_dram, qc)], dma=True)
            for hh in range(2):
                E("gpsimd", lambda g, p2=h2, pp=p, q=qc, o=hh: g.dma_start(
                    den_dram[2 * p2 + o:2 * p2 + o + 1, q * 512:(q + 1) * 512],
                    S.STCM[64:65, pp * 1024 + o * 512: pp * 1024 + o * 512 + 512]),
                  reads=[f"STCM:{2 * p + hh}"],
                  writes=[self.sreg(den_dram, "d")], dma=True)

        for h2 in range(NH // 2):
            par = h2 % 2
            qbuf = S.QBUF[:, par * 4096:(par + 1) * 4096]
            qregs = self.rQB(par * 4096, (par + 1) * 4096)
            E("sync", lambda g, p=h2, qb=qbuf: g.dma_start(
                qb, qT_dram[p * 128:(p + 1) * 128, :]),
              reads=[self.sreg(qT_dram, k) for k in range(8)],
              writes=qregs, dma=True)
            for qc in range(NTK):
                n_kt = (4 * qc + 4) if causal else 32
                for kt in range(n_kt):
                    emit_scores(h2, qc, kt, qregs, qbuf)
                    emit_mask(qc, kt, n_kt)
                    if kt >= 1:
                        emit_pv(h2, qc, kt - 1)
                emit_pv(h2, qc, n_kt - 1)
                emit_stage(qc)
                emit_evac(h2, qc)
                self.drain_conv_jobs(1)

    def proj_ln_phase(self, which):
        E = self.P.emit
        S = self
        is_d = which == "D"
        w_dram = S.wo if is_d else S.proj_w
        aoT = S.s_aoT if is_d else S.s_aoiT
        x_src = S.x_in if is_d else S.s_fin
        x1_dst = S.s_x1 if is_d else S.s_a
        E("gpsimd", lambda g: g.dma_start(
            S.WBUF[:, :].rearrange("p (cc n) -> p cc n", cc=NCC),
            w_dram.rearrange("(cc p) n -> p cc n", p=128)),
          writes=["WBUF"], dma=True)
        if not is_d:
            E("gpsimd", lambda g: g.dma_start(
                S.RWSB[:, :].rearrange("p (cc e) -> p cc e", cc=NCC),
                S.router_w.rearrange("(cc p) e -> p cc e", p=128)),
              writes=["RWSB"], dma=True)
        den = S.s_den if is_d else S.s_deni
        for tk in range(NTK):
            spar = tk % 2
            slab = S.QBUF[:, spar * 4096:(spar + 1) * 4096]
            sregs = self.rQB(spar * 4096, (spar + 1) * 4096)
            E("sync", lambda g, k=tk, sl=slab: g.dma_start(
                sl.rearrange("p (cc n) -> p cc n", cc=NCC),
                aoT[:, k * 512:(k + 1) * 512].rearrange(
                    "(cc p) n -> p cc n", p=128)),
              reads=[self.sreg(aoT, tk)], writes=sregs, dma=True)
            E("sync", lambda g, k=tk: g.dma_start(
                S.DENL[:, :], den[:, k * 512:(k + 1) * 512]),
              reads=[self.sreg(den, "d")], writes=["DENL"], dma=True)
            E("vector", lambda v: v.reciprocal(S.DENR[:, :], S.DENL[:, :]),
              reads=["DENL"], writes=["DENR"])
            E("vector", lambda v: v.tensor_copy(S.DENB[:, :], S.DENR[:, :]),
              reads=["DENR"], writes=["DENB"])
            for cc in range(NCC):
                bb = cc % 2
                E("tensor", lambda t, c=cc, b=bb: t.matmul(
                    S.psB(b), S.BMAP[:, c * 128:(c + 1) * 128], S.DENB[:, :],
                    start=True, stop=True),
                  reads=["BMAP", "DENB"], writes=[f"psB:{bb}"])
                E("vector", lambda v, c=cc, b=bb, sl=slab: v.tensor_mul(
                    sl[:, c * 512:(c + 1) * 512], sl[:, c * 512:(c + 1) * 512],
                    S.psB(b)),
                  reads=self.rQB(spar * 4096 + cc * 512,
                                 spar * 4096 + (cc + 1) * 512) + [f"psB:{bb}"],
                  writes=self.rQB(spar * 4096 + cc * 512,
                                  spar * 4096 + (cc + 1) * 512))
            for ccp in range(NCC):
                bank = ccp % 2
                for cc in range(NCC):
                    E("tensor", lambda t, c=cc, cp=ccp, b=bank, sl=slab: t.matmul(
                        S.psA(b),
                        S.WBUF[:, c * 1024 + cp * 128: c * 1024 + (cp + 1) * 128],
                        sl[:, c * 512:(c + 1) * 512],
                        start=(c == 0), stop=(c == NCC - 1)),
                      reads=["WBUF"] + sregs,
                      writes=[f"psA:{bank}"])
                dst = S.STCM[:, ccp * 512:(ccp + 1) * 512]
                sreg = [f"STCM:{ccp}"]
                if is_d:
                    E("scalar", lambda s, b=bank, d=dst, c=ccp: s.activation(
                        d, S.psA(b), AF.Identity, bias=S.col("bo", c)),
                      reads=[f"psA:{bank}", "COLS"], writes=sreg)
                else:
                    E("scalar", lambda s, b=bank, d=dst: s.activation(
                        d, S.psA(b), AF.Copy),
                      reads=[f"psA:{bank}"], writes=sreg)
            for q in range(4):
                tt = tk * 4 + q
                parc = tt % 2
                for cc in range(NCC):
                    E("tensor", lambda t, c=cc, p=parc, qq=q: t.transpose(
                        S.PS_C[:, p * 1024 + c * 128: p * 1024 + (c + 1) * 128],
                        S.STCM[:, c * 512 + qq * 128: c * 512 + qq * 128 + 128],
                        S.IDENT[:, :]),
                      reads=[f"STCM:{cc}", "IDENT"], writes=[f"psC:{parc}"])
                E("vector", lambda v, p=parc: v.tensor_copy(
                    S.TOK[:, 0:1024], S.PS_C[:, p * 1024:(p + 1) * 1024]),
                  reads=[f"psC:{parc}"], writes=["TOK:0"])
                E("sync", lambda g, i=tt: g.dma_start(
                    S.AR3[:, 0:1024], x_src[i * 128:(i + 1) * 128, :]),
                  reads=[self.sreg(x_src, "t")] if not is_d else [],
                  writes=["AR3:0"], dma=True)
                if is_d:
                    self.ln_stats(S.TOK[:, 0:1024], ["TOK:0"], 0)
                    E("vector", lambda v: v.tensor_scalar(
                        S.AR3[:, 1024:2048], S.TOK[:, 0:1024],
                        S.STAT[:, 0:1], S.STAT[:, 1:2], ALU.subtract, ALU.mult),
                      reads=["TOK:0", "STAT"], writes=["AR3:1"])
                    E("vector", lambda v: v.tensor_add(
                        S.AR3[:, 1024:2048], S.AR3[:, 1024:2048], S.AR3[:, 0:1024]),
                      reads=["AR3:1", "AR3:0"], writes=["AR3:1"])
                else:
                    E("vector", lambda v: v.tensor_add(
                        S.AR3[:, 1024:2048], S.TOK[:, 0:1024], S.AR3[:, 0:1024]),
                      reads=["TOK:0", "AR3:0"], writes=["AR3:1"])
                E("gpsimd", lambda g, i=tt: g.dma_start(
                    x1_dst[i * 128:(i + 1) * 128, :], S.AR3[:, 1024:2048]),
                  reads=["AR3:1"], writes=[self.sreg(x1_dst, "t")], dma=True)
                self.ln_stats(S.AR3[:, 1024:2048], ["AR3:1"], 8)
                E("vector", lambda v: v.tensor_scalar(
                    S.AR3[:, 2048:3072], S.AR3[:, 1024:2048],
                    S.STAT[:, 8:9], S.STAT[:, 9:10], ALU.subtract, ALU.mult),
                  reads=["AR3:1", "STAT"], writes=["AR3:2"])
                if is_d:
                    E("gpsimd", lambda g, i=tt: g.dma_start(
                        S.s_fin[i * 128:(i + 1) * 128, :], S.AR3[:, 2048:3072]),
                      reads=["AR3:2"], writes=[self.sreg(S.s_fin, "t")], dma=True)
                E("vector", lambda v: v.tensor_copy(
                    S.TOK[:, 1024:2048], S.AR3[:, 2048:3072]),
                  reads=["AR3:2"], writes=["TOK:1"])
                for cc in range(NCC):
                    E("tensor", lambda t, c=cc, p=parc: t.transpose(
                        S.PS_C[:, p * 1024 + c * 128: p * 1024 + (c + 1) * 128],
                        S.TOK[:, 1024 + c * 128: 1024 + (c + 1) * 128],
                        S.IDENT[:, :]),
                      reads=["TOK:1", "IDENT"], writes=[f"psC:{parc}"])
                lnw = "ln1w" if is_d else "ln2w"
                lnb = "ln1b" if is_d else "ln2b"
                for cc in range(NCC):
                    if is_d:
                        dsth = self.ar1(cc, tt * 128, 128)
                        rr = self.rAR1_stripe(cc, tt * 128, 128)
                    else:
                        dsth = S.STCM[:, 4096 + cc * 128: 4096 + (cc + 1) * 128]
                        rr = [f"STCM:{8 + cc // 4}"]
                    E("vector", lambda v, c=cc, p=parc, d=dsth, rg=rr: v.tensor_scalar(
                        d, S.PS_C[:, p * 1024 + c * 128: p * 1024 + (c + 1) * 128],
                        S.col(lnw, c), S.col(lnb, c), ALU.mult, ALU.add),
                      reads=[f"psC:{parc}", "COLS"], writes=rr)
                if is_d and S.s_hT is not None:
                    for cc in range(NCC):
                        E("gpsimd", lambda g, c=cc, i=tt: g.dma_start(
                            S.s_hT[c * 128:(c + 1) * 128, i * 128:(i + 1) * 128],
                            self.ar1(c, i * 128, 128)),
                          reads=self.rAR1_stripe(cc, tt * 128, 128),
                          writes=["hTdbg"], dma=True)
                if not is_d:
                    for cc in range(NCC):
                        E("gpsimd", lambda g, c=cc, i=tt: g.dma_start(
                            S.s_h2T[c * 128:(c + 1) * 128, i * 128:(i + 1) * 128],
                            S.STCM[:, 4096 + c * 128: 4096 + (c + 1) * 128]),
                          reads=[f"STCM:{8 + cc // 4}"],
                          writes=[self.sreg(S.s_h2T, tt // 4)], dma=True)
                    for cc in range(NCC):
                        E("tensor", lambda t, c=cc: t.matmul(
                            S.PS_B[:, 0:4],
                            S.STCM[:, 4096 + c * 128: 4096 + (c + 1) * 128],
                            S.RWSB[:, c * 4:(c + 1) * 4],
                            start=(c == 0), stop=(c == NCC - 1)),
                          reads=[f"STCM:{8 + cc // 4}", "RWSB"], writes=["psB:0"])
                    E("vector", lambda v: v.reduce_max(
                        S.STAT[:, 16:17], S.PS_B[:, 0:4], AX.X),
                      reads=["psB:0"], writes=["STAT"])
                    E("vector", lambda v: v.tensor_single_scalar(
                        S.STAT[:, 20:24], S.PS_B[:, 0:4], S.STAT[:, 16:17],
                        ALU.is_ge),
                      reads=["psB:0", "STAT"], writes=["STAT"])
                    E("vector", lambda v, i=tt: v.tensor_copy(
                        S.MCOL[:, i * 4:i * 4 + 1], S.STAT[:, 20:21]),
                      reads=["STAT"], writes=["MCOL"])
                    E("vector", lambda v: v.tensor_scalar(
                        S.STAT[:, 17:18], S.STAT[:, 20:21], -1.0, 1.0,
                        ALU.mult, ALU.add),
                      reads=["STAT"], writes=["STAT"])
                    for e in range(1, NE):
                        E("vector", lambda v, i=tt, ee=e: v.tensor_mul(
                            S.MCOL[:, i * 4 + ee:i * 4 + ee + 1],
                            S.STAT[:, 20 + ee:21 + ee], S.STAT[:, 17:18]),
                          reads=["STAT"], writes=["MCOL"])
                        if e < NE - 1:
                            E("vector", lambda v, i=tt, ee=e: v.tensor_sub(
                                S.STAT[:, 17:18], S.STAT[:, 17:18],
                                S.MCOL[:, i * 4 + ee:i * 4 + ee + 1]),
                              reads=["STAT", "MCOL"], writes=["STAT"])

    def phase_H(self):
        E = self.P.emit
        S = self
        WV_OFF = (8192, 12288)
        PJ_OFF = 16384
        for tc in range(NTC):
            E("sync", lambda g, k=tc: g.dma_start(
                S.AR2[:, 0:8192].rearrange("p (cc n) -> p cc n", cc=NCC),
                S.s_h2T[:, k * 1024:(k + 1) * 1024].rearrange(
                    "(cc p) n -> p cc n", p=128)),
              reads=[self.sreg(S.s_h2T, tc * 2), self.sreg(S.s_h2T, tc * 2 + 1)],
              writes=self.rAR2(0, 8192), dma=True)
            for e in range(NE):
                for g2 in range(16):
                    woff = WV_OFF[g2 % 2]
                    wslot = S.AR2[:, woff:woff + 4096]
                    wregs = self.rAR2(woff, woff + 4096)
                    E("sync", lambda g, ee=e, gg=g2, w=wslot: g.dma_start(
                        w, S.s_wvbf[ee, gg]),
                      reads=["s_wvbf"], writes=wregs, dma=True)
                    for jh in range(2):
                        jc = 2 * g2 + jh
                        for half in range(2):
                            for cc in range(NCC):
                                E("tensor", lambda t, c=cc, h=half, w=wslot,
                                  j=jh: t.matmul(
                                    S.psA(h),
                                    w[:, c * 256 + j * 128: c * 256 + j * 128 + 128],
                                    S.AR2[:, c * 1024 + h * 512: c * 1024 + (h + 1) * 512],
                                    start=(c == 0), stop=(c == NCC - 1)),
                                  reads=wregs[:4] + self.rAR2(cc * 1024, (cc + 1) * 1024),
                                  writes=[f"psA:{half}"])
                            for cc in range(NCC):
                                E("tensor", lambda t, c=cc, h=half, w=wslot,
                                  j=jh: t.matmul(
                                    S.psA(2 + h),
                                    w[:, 2048 + c * 256 + j * 128: 2048 + c * 256 + j * 128 + 128],
                                    S.AR2[:, c * 1024 + h * 512: c * 1024 + (h + 1) * 512],
                                    start=(c == 0), stop=(c == NCC - 1)),
                                  reads=wregs[4:] + self.rAR2(cc * 1024, (cc + 1) * 1024),
                                  writes=[f"psA:{2 + half}"])
                        E("scalar", lambda s: s.activation(
                            S.SILU[:, :], S.PS_A[:, 0:1024], AF.Silu),
                          reads=["psA:0", "psA:1"], writes=["SILU"])
                        E("vector", lambda v, j=jc: v.tensor_mul(
                            S.AR1[:, j * 1024:(j + 1) * 1024], S.SILU[:, :],
                            S.PS_A[:, 1024:2048]),
                          reads=["SILU", "psA:2", "psA:3"],
                          writes=self.rAR1(jc * 1024, (jc + 1) * 1024))
                for grp, tts in enumerate(((0, 1, 2), (3, 4, 5), (6, 7))):
                    for jc in range(NJC):
                        if jc % 4 == 0:
                            poff4 = PJ_OFF + ((jc // 4) % 2) * 4096
                            pslot4 = S.AR2[:, poff4:poff4 + 4096]
                            E("sync", lambda g, ee=e, j4=jc // 4,
                              ps=pslot4: g.dma_start(ps, S.s_pjbf[ee, j4]),
                              reads=["s_pjbf"],
                              writes=self.rAR2(poff4, poff4 + 4096), dma=True)
                        poff = PJ_OFF + ((jc // 4) % 2) * 4096 + (jc % 4) * 1024
                        pslot = S.AR2[:, poff:poff + 1024]
                        pregs = self.rAR2(poff, poff + 1024)
                        for i, tt8 in enumerate(tts):
                            for half in range(2):
                                bank = 2 * i + half
                                bs = S.psA(bank) if bank < 4 else S.psB(bank - 4)
                                breg = (f"psA:{bank}" if bank < 4
                                        else f"psB:{bank - 4}")
                                E("tensor", lambda t, j=jc, t8=tt8, h=half, b=bs,
                                  ps=pslot: t.matmul(
                                    b,
                                    S.AR1[:, j * 1024 + t8 * 128: j * 1024 + t8 * 128 + 128],
                                    ps[:, h * 512:(h + 1) * 512],
                                    start=(j == 0), stop=(j == NJC - 1)),
                                  reads=pregs + self.rAR1(jc * 1024 + tt8 * 128,
                                                          jc * 1024 + tt8 * 128 + 128),
                                  writes=[breg])
                    for i, tt8 in enumerate(tts):
                        tile = tc * 8 + tt8
                        if i < 2:
                            src = S.PS_A[:, 2 * i * 512: 2 * i * 512 + 1024]
                            regs = [f"psA:{2 * i}", f"psA:{2 * i + 1}"]
                        else:
                            src = S.PS_B[:, 0:1024]
                            regs = ["psB:0", "psB:1"]
                        dst = S.QBUF[:, tt8 * 1024:(tt8 + 1) * 1024]
                        dregs = self.rQB(tt8 * 1024, (tt8 + 1) * 1024)
                        if e == 0:
                            E("vector", lambda v, s=src, d=dst, ti=tile: v.tensor_scalar_mul(
                                d, s, S.MCOL[:, ti * 4:ti * 4 + 1]),
                              reads=regs + ["MCOL"], writes=dregs)
                        else:
                            E("vector", lambda v, s=src, d=dst, ti=tile,
                              ee=e: v.scalar_tensor_tensor(
                                d, s, S.MCOL[:, ti * 4 + ee:ti * 4 + ee + 1], d,
                                ALU.mult, ALU.add),
                              reads=regs + ["MCOL"] + dregs, writes=dregs)
            for tt8 in range(8):
                tt = tc * 8 + tt8
                E("gpsimd", lambda g, i=tt: g.dma_start(
                    S.AR3[:, 0:1024], S.s_a[i * 128:(i + 1) * 128, :]),
                  reads=[self.sreg(S.s_a, "t")], writes=["AR3:0"], dma=True)
                E("gpsimd", lambda g, i=tt: g.dma_start(
                    S.AR3[:, 1024:2048], S.s_x1[i * 128:(i + 1) * 128, :]),
                  reads=[self.sreg(S.s_x1, "t")], writes=["AR3:1"], dma=True)
                E("vector", lambda v, t8=tt8: v.tensor_add(
                    S.AR3[:, 2048:3072], S.AR3[:, 0:1024],
                    S.QBUF[:, t8 * 1024:(t8 + 1) * 1024]),
                  reads=["AR3:0"] + self.rQB(tt8 * 1024, (tt8 + 1) * 1024),
                  writes=["AR3:2"])
                self.ln_stats(S.AR3[:, 2048:3072], ["AR3:2"], 24)
                E("vector", lambda v: v.tensor_scalar(
                    S.AR3[:, 0:1024], S.AR3[:, 2048:3072],
                    S.STAT[:, 24:25], S.STAT[:, 25:26], ALU.subtract, ALU.mult),
                  reads=["AR3:2", "STAT"], writes=["AR3:0"])
                E("vector", lambda v: v.tensor_add(
                    S.AR3[:, 0:1024], S.AR3[:, 0:1024], S.AR3[:, 1024:2048]),
                  reads=["AR3:0", "AR3:1"], writes=["AR3:0"])
                E("gpsimd", lambda g, i=tt: g.dma_start(
                    S.y_out[i * 128:(i + 1) * 128, :], S.AR3[:, 0:1024]),
                  reads=["AR3:0"], writes=["y_out"], dma=True)


_BUILD_CACHE = {}


def get_nc(serial=True, upto="H", dbg=()):
    key = (serial, upto, tuple(sorted(dbg)))
    if key not in _BUILD_CACHE:
        _BUILD_CACHE[key] = Build(serial=serial, upto=upto, dbg=dbg).build()
    return _BUILD_CACHE[key]


def make_inmaps(inputs, cores=8):
    ident, identf, perm, ones_row, rope_i, masks, bmap = host_constants()
    sv = pack_smallvecs(inputs)
    x = np.ascontiguousarray(np.asarray(inputs["L0_attn_hidden_states"], np.float32))
    f32c = lambda k: np.ascontiguousarray(np.asarray(inputs[k], np.float32))
    shared = dict(
        cos_o=f32c("L0_attn_cos"), sin_o=f32c("L0_attn_sin"),
        wq=f32c("L0_attn_wq"), wk=f32c("L0_attn_wk"), wv=f32c("L0_attn_wv"),
        wo=f32c("L0_attn_wo"), attn_w=f32c("L0_ffn_attn_w"),
        proj_w=f32c("L0_ffn_proj_w"), router_w=f32c("L0_ffn_router_w"),
        ewv=f32c("L0_ffn_expert_wv"), eproj=f32c("L0_ffn_expert_proj"),
        smallv=sv, c_ident=ident, c_identf=identf, c_perm=perm,
        c_ones=ones_row, c_ropei=rope_i, c_masks=masks, c_bmap=bmap,
    )
    return [dict(shared, x=np.ascontiguousarray(x[b % x.shape[0]]))
            for b in range(cores)]


def kernel(**inputs):
    nc = get_nc(serial=False, upto="H", dbg=())
    in_maps = make_inmaps(inputs, cores=8)
    res = run_bass_kernel_spmd(nc, in_maps, core_ids=list(range(8)))
    return np.stack([np.asarray(r["y"], np.float32) for r in res.results])


# version 19
# speedup vs baseline: 1.2085x; 1.1141x over previous
import numpy as np
import ml_dtypes
import concourse.bass as bass
import concourse.mybir as mybir
from concourse.bass_utils import run_bass_kernel_spmd

F32 = mybir.dt.float32
BF16 = mybir.dt.bfloat16
AF = mybir.ActivationFunctionType
ALU = mybir.AluOpType
AX = mybir.AxisListType
bfnp = ml_dtypes.bfloat16

T = 4096
C = 1024
NH = 16
HD = 64
NCC = 8
NTT = 32
NTK = 8
EPS = 1e-5
NE = 4
DFF = 4096
NJC = 32
NTC = 4
SCALE = 0.125
STRW = 4112

DMA_QUEUES = ("sync", "gpsimd")
N_SLOTS = 4
ENGS = ("tensor", "vector", "scalar", "gpsimd", "sync")


def _rgn(prefix, lo, hi, gran):
    return [f"{prefix}:{i}" for i in range(lo // gran, (hi + gran - 1) // gran)]


class Prog:

    def __init__(self, serial=False):
        self.items = []
        self.serial = serial

    def emit(self, eng, fn, reads=(), writes=(), dma=False, slot=None):
        self.items.append(dict(eng=eng, fn=fn, reads=tuple(reads),
                               writes=tuple(writes), dma=dma, slot=slot))
        return len(self.items) - 1

    def realize(self, eng_sems, dma_sems):
        n = len(self.items)
        deps = [set() for _ in range(n)]
        if self.serial:
            for i in range(1, n):
                deps[i].add(i - 1)
        else:
            last_w = {}
            readers = {}
            for i, it in enumerate(self.items):
                for r in it["reads"]:
                    if r in last_w:
                        deps[i].add(last_w[r])
                for w in it["writes"]:
                    if w in last_w:
                        deps[i].add(last_w[w])
                    for rd in readers.get(w, ()):
                        deps[i].add(rd)
                for r in it["reads"]:
                    readers.setdefault(r, []).append(i)
                for w in it["writes"]:
                    last_w[w] = i
                    readers[w] = []
        for i, it in enumerate(self.items):
            if it["eng"] == "tensor" and not it["dma"]:
                deps[i] = {j for j in deps[i]
                           if self.items[j]["dma"] or self.items[j]["eng"] != "tensor"}
        dma_idx = {}
        q_counts = {q: 0 for q in DMA_QUEUES}
        default_slots = {"sync": (0, 1, 2, 3), "gpsimd": (0, 1, 2)}
        slot_uses = {(q, s): 0 for q in DMA_QUEUES for s in range(N_SLOTS)}
        for i, it in enumerate(self.items):
            if it["dma"]:
                q = it["eng"]
                if it.get("slot") is not None:
                    s = it["slot"]
                else:
                    cands = default_slots[q]
                    s = cands[q_counts[q] % len(cands)]
                    q_counts[q] += 1
                slot_uses[(q, s)] += 1
                dma_idx[i] = (q, s, slot_uses[(q, s)])
        need_mark = [False] * n
        for i in range(n):
            for j in deps[i]:
                if not self.items[j]["dma"]:
                    need_mark[j] = True
        comp_seq = {}
        ctr = {e: 0 for e in ENGS}
        for i, it in enumerate(self.items):
            if not it["dma"] and need_mark[i]:
                ctr[it["eng"]] += 1
                comp_seq[i] = (it["eng"], ctr[it["eng"]])
        final_counts = dict(ctr)
        per_eng = {e: [] for e in ENGS}
        hwm = {e: {} for e in ENGS}

        def mk_closure(i):
            it = self.items[i]
            eng_name = it["eng"]
            waits = []
            for j in sorted(deps[i]):
                jt = self.items[j]
                if jt["dma"]:
                    q, s, use = dma_idx[j]
                    waits.append((("d", q, s), 16 * use))
                else:
                    e2, seq = comp_seq[j]
                    waits.append((("c", e2), seq))
            if it["dma"]:
                q, s, use = dma_idx[i]
                if use > 1:
                    waits.append((("d", q, s), 16 * (use - 1)))
            h = hwm[eng_name]
            real_waits = []
            for key, val in waits:
                if h.get(key, 0) >= val:
                    continue
                h[key] = val
                real_waits.append((key, val))
            if it["dma"]:
                q, s, use = dma_idx[i]
                inc = (("d", q, s), 16)
            elif need_mark[i]:
                inc = (("c", eng_name), 1)
            else:
                inc = None

            def closure(eng):
                for key, val in real_waits:
                    sem = dma_sems[key[1]][key[2]] if key[0] == "d" else eng_sems[key[1]]
                    eng.wait_ge(sem, val)
                ins = it["fn"](eng)
                if inc is not None:
                    key, amt = inc
                    sem = dma_sems[key[1]][key[2]] if key[0] == "d" else eng_sems[key[1]]
                    ins.then_inc(sem, amt)
            return closure

        for i in range(n):
            per_eng[self.items[i]["eng"]].append(mk_closure(i))

        def final_join(eng):
            for q in DMA_QUEUES:
                for s in range(N_SLOTS):
                    if slot_uses[(q, s)]:
                        eng.wait_ge(dma_sems[q][s], 16 * slot_uses[(q, s)])
            for e, cnt in final_counts.items():
                if cnt and e != "sync":
                    eng.wait_ge(eng_sems[e], cnt)
        per_eng["sync"].append(final_join)
        return per_eng


def host_constants():
    ident = np.eye(128, dtype=bfnp)
    identf = np.eye(128, dtype=np.float32)
    perm = np.zeros((128, 128), dtype=bfnp)
    for h in range(2):
        b = 64 * h
        for i in range(32):
            perm[b + i, b + 32 + i] = 1.0
            perm[b + 32 + i, b + i] = 1.0
    ones_row = np.ones((1, 128), dtype=bfnp)
    inv_freq = 1.0 / (10000.0 ** (np.arange(0, HD, 2, dtype=np.float32) / HD))
    t = np.arange(T, dtype=np.float32)
    emb = np.concatenate([np.outer(t, inv_freq), np.outer(t, inv_freq)], axis=-1)
    cosT = np.cos(emb).T.astype(np.float32)
    sinT = np.sin(emb).T.astype(np.float32)
    sinT[0:32] = -sinT[0:32]
    rope_i = np.zeros((128, 2 * T), dtype=bfnp)
    rope_i[0:64, 0:T] = cosT.astype(bfnp)
    rope_i[64:128, 0:T] = cosT.astype(bfnp)
    rope_i[0:64, T:] = sinT.astype(bfnp)
    rope_i[64:128, T:] = sinT.astype(bfnp)
    masks = np.zeros((128, 4 * 512), dtype=bfnp)
    r = np.arange(128)[:, None]
    j = np.arange(512)[None, :]
    for b in range(4):
        masks[:, b * 512:(b + 1) * 512] = (j >= 128 * b + r).astype(bfnp)
    bmap = np.zeros((16, 1024), dtype=bfnp)
    for c in range(8):
        bmap[2 * c, c * 128: c * 128 + 64] = 1.0
        bmap[2 * c + 1, c * 128 + 64: c * 128 + 128] = 1.0
    ii = np.arange(128)
    tris = (ii[:, None] < ii[None, :]).astype(bfnp)
    ti, ei = ii[:, None] // 4, ii[:, None] % 4
    tj, ej = ii[None, :] // 4, ii[None, :] % 4
    dcmb = (((ei == ej) & (ti < tj)) | (ei < ej)).astype(bfnp)
    selc = (ii[:, None] % 4 < np.arange(4)[None, :])[:128].astype(bfnp)
    selc = np.ascontiguousarray(selc[:, :4])
    onesc = np.ones((128, 1), dtype=bfnp)
    onesfc = np.ones((1, 128), dtype=np.float32)
    iotap = np.arange(128, dtype=np.float32).reshape(128, 1)
    iotas4 = np.zeros((128, 128), dtype=np.float32)
    for tt in range(32):
        iotas4[:, tt * 4:(tt + 1) * 4] = (tt * 128 + ii)[:, None]
    pos8 = (np.arange(8, dtype=np.float32) * 512).reshape(1, 8)
    routec = dict(c_tris=tris, c_dcmb=dcmb, c_selc=selc, c_onesc=onesc,
                  c_onesfc=onesfc, c_iotap=iotap, c_iotas4=iotas4,
                  c_pos8=pos8)
    return ident, identf, perm, ones_row, rope_i, masks, bmap, routec


SV = {k: i for i, k in enumerate(
    ["bq", "bv", "bo", "ln1w", "ln1b", "ln2w", "ln2b", "w0", "w1", "w2"])}


def pack_smallvecs(ins):
    names = ["L0_attn_bq", "L0_attn_bv", "L0_attn_bo", "L0_ffn_ln1_w",
             "L0_ffn_ln1_b", "L0_ffn_ln2_w", "L0_ffn_ln2_b"]
    rows = [np.asarray(ins[n], np.float32).reshape(C) for n in names]
    cw = np.asarray(ins["L0_ffn_conv_w"], np.float32).reshape(C, 3)
    rows += [np.ascontiguousarray(cw[:, i]) for i in range(3)]
    return np.stack(rows)


class Build:
    def __init__(self, serial=True, upto="H", dbg=()):
        self.serial = serial
        self.upto = upto
        self.dbg = set(dbg)
        self.P = Prog(serial=serial)

    def rAR1(self, lo, hi):
        return _rgn("AR1", lo, hi, 512)

    def rAR1_stripe(self, cc, lo, w):
        a = cc * STRW + 2 + lo
        return self.rAR1(a, a + w)

    def rAR2(self, lo, hi):
        return _rgn("AR2", lo, hi, 512)

    def rQB(self, lo, hi):
        return _rgn("QB", lo, hi, 512)

    def rAR3(self, slot):
        return [f"AR3:{slot}"]

    def ar1(self, cc, lo, w):
        return self.AR1[:, cc * STRW + 2 + lo: cc * STRW + 2 + lo + w]

    def psA(self, b, w=512):
        return self.PS_A[:, b * 512: b * 512 + w]

    def psB(self, b, w=512):
        return self.PS_B[:, b * 512: b * 512 + w]

    def col(self, vname, cc):
        i = SV[vname] * 8 + cc
        return self.COLS[:, i:i + 1]

    def sreg(self, ap, sub):
        return f"{ap.tensor.name}:{sub}"

    def build(self):
        nc = bass.Bass("TRN2", target_bir_lowering=False)
        self.nc = nc
        PH = "ABCDEFGH"
        on = {p: PH.index(p) <= PH.index(self.upto) for p in PH}

        din = lambda name, shape, dt: nc.dram_tensor(name, shape, dt, kind="ExternalInput").ap()

        def scratch(name, shape, dt):
            kind = "ExternalOutput" if name in self.dbg else "Internal"
            return nc.dram_tensor(name, shape, dt, kind=kind).ap()

        self.x_in = din("x", [T, C], F32)
        self.cos_o = din("cos_o", [1024, HD], F32)
        self.sin_o = din("sin_o", [1024, HD], F32)
        self.wq = din("wq", [C, C], F32)
        self.wk = din("wk", [C, C], F32)
        self.wv = din("wv", [C, C], F32)
        self.wo = din("wo", [C, C], F32)
        self.attn_w = din("attn_w", [C, 3 * C], F32)
        self.proj_w = din("proj_w", [C, C], F32)
        self.router_w = din("router_w", [C, NE], F32)
        self.ewv = din("ewv", [NE, C, 2 * DFF], F32)
        self.eproj = din("eproj", [NE, DFF, C], F32)
        self.smallv = din("smallv", [10, C], F32)
        self.c_ident = din("c_ident", [128, 128], BF16)
        self.c_identf = din("c_identf", [128, 128], F32)
        self.c_perm = din("c_perm", [128, 128], BF16)
        self.c_ones = din("c_ones", [1, 128], BF16)
        self.c_ropei = din("c_ropei", [128, 2 * T], BF16)
        self.c_masks = din("c_masks", [128, 4 * 512], BF16)
        self.c_bmap = din("c_bmap", [16, 1024], BF16)
        self.c_tris = din("c_tris", [128, 128], BF16)
        self.c_dcmb = din("c_dcmb", [128, 128], BF16)
        self.c_selc = din("c_selc", [128, 4], BF16)
        self.c_onesc = din("c_onesc", [128, 1], BF16)
        self.c_onesfc = din("c_onesfc", [1, 128], F32)
        self.c_iotap = din("c_iotap", [128, 1], F32)
        self.c_iotas4 = din("c_iotas4", [128, 128], F32)
        self.c_pos8 = din("c_pos8", [1, 8], F32)

        self.y_out = nc.dram_tensor("y", [T, C], F32, kind="ExternalOutput").ap()

        self.s_ln0T = scratch("s_ln0T", [C, T], BF16)
        self.s_qT = scratch("s_qT", [C, T], BF16)
        self.s_aoT = scratch("s_aoT", [C, T], BF16)
        self.s_x1 = scratch("s_x1", [T, C], F32)
        self.s_fin = scratch("s_fin", [T, C], F32)
        self.s_hcT = scratch("s_hcT", [C, T], BF16)
        self.s_qiT = scratch("s_qiT", [C, T], BF16)
        self.s_aoiT = scratch("s_aoiT", [C, T], BF16)
        self.s_a = scratch("s_a", [T, C], F32)
        self.s_h2 = scratch("s_h2", [T, C], BF16)
        self.s_hs = scratch("s_hs", [T, C], BF16)
        self.s_g = scratch("s_g", [T, 4], F32)
        self.out_moe = scratch("out_moe", [T, C], BF16)
        self.s_den = scratch("s_den", [NH, T], BF16)
        self.s_deni = scratch("s_deni", [NH, T], BF16)
        self.s_dbg = (scratch("s_dbg", [128, 640], F32)
                      if "s_dbg" in self.dbg else None)
        self.s_ropeo = scratch("s_ropeo", [128, 2048], BF16)
        self.s_wvbf = scratch("s_wvbf", [NE, 16, 128, 4096], BF16)
        self.s_pjbf = scratch("s_pjbf", [NE, 8, 128, 4096], BF16)
        self.s_hT = scratch("s_hT", [C, T], BF16) if "s_hT" in self.dbg else None

        from contextlib import ExitStack
        ctx = self.ctx = ExitStack()
        sb = lambda name, shape, dt: ctx.enter_context(nc.sbuf_tensor(name, shape, dt))
        pst = lambda name, shape, dt: ctx.enter_context(nc.psum_tensor(name, shape, dt))

        self.AR1 = sb("AR1", [128, NCC * STRW], BF16)
        self.AR2 = sb("AR2", [128, 33280], BF16)
        self.AR3 = sb("AR3", [128, 3072], F32)
        self.WBUF = sb("WBUF", [128, 8192], BF16)
        self.QBUF = sb("QBUF", [128, 8192], BF16)
        self.MSK = sb("MSK", [128, 2048], BF16)
        self.BVB = self.MSK
        self._msk_loaded = False
        self._conv_jobs = self._make_conv_jobs()
        self.IDENT = sb("IDENT", [128, 128], BF16)
        self.IDENTF = sb("IDENTF", [128, 128], F32)
        self.PERM = sb("PERM", [128, 128], BF16)
        self.ONES = sb("ONES", [1, 128], BF16)
        self.COLS = sb("COLS", [128, 80], F32)
        self.PB = sb("PB", [128, 4 * 512], BF16)
        self.BMAP = sb("BMAP", [16, 1024], BF16)
        self.DENL = sb("DENL", [16, 512], BF16)
        self.DENR = sb("DENR", [16, 512], F32)
        self.DENB = sb("DENB", [16, 512], BF16)
        self.RT = sb("RT", [128, 512], BF16)
        self.TOK = sb("TOK", [128, 2 * C], BF16)
        self.STAT = sb("STAT", [128, 32], F32)
        self.MCOL = sb("MCOL", [128, 128], F32)
        self.SVTMP = sb("SVTMP", [80, 128], F32)
        self.ROPO = sb("ROPO", [128, 64], BF16)
        self.SILU = sb("SILU", [128, C], BF16)
        self.JUNK = self.SILU
        self.RWSB = sb("RWSB", [128, 32], BF16)
        self.STCM = sb("STCM", [128, 5120], BF16)
        self.MCOLB = self.TOK[:, 512:640]
        self.TRIS = self.TOK[:, 256:384]
        self.DCMB = self.TOK[:, 384:512]
        self.SELC = sb("SELC", [128, 4], BF16)
        self.ONESC = sb("ONESC", [128, 1], BF16)
        self.ONESFC = self.AR3[0:1, 2688:2816]
        self.IOTAP = sb("IOTAP", [128, 1], F32)
        self.IOTAS4 = self.AR3[:, 2560:2688]
        self.CNTC = sb("CNTC", [128, 1], BF16)
        self.MIX = self.AR3[0:1, 2816:2880]
        self.MCH = sb("MCH", [128, 20], F32)
        self.ESF = sb("ESF", [128, 32], F32)
        self.IXI = sb("IXI", [128, 32], mybir.dt.int32)
        self.IXW = sb("IXW", [128, 32], mybir.dt.int32)
        self.IXP = sb("IXP", [128, 16], mybir.dt.int32)
        self.IXH = sb("IXH", [128, 8], mybir.dt.int32)
        self.SIDI = sb("SIDI", [128, 8], mybir.dt.int32)
        self.IXF = self.AR3[:, 2880:2896]
        self.GQ = self.AR3[:, 2896:2912]

        self.PS_A = pst("PS_A", [128, 2048], F32)
        self.PS_B = pst("PS_B", [128, 1024], F32)
        self.PS_C = pst("PS_C", [128, 2048], BF16)

        self.setup_consts()
        if on["A"]:
            self.phase_A()
        if on["B"]:
            self.phase_B()
        if on["C"]:
            self.attention(self.s_qT, self.s_aoT, self.s_den, causal=False)
        if on["D"]:
            self.proj_ln_phase(which="D")
        if on["E"]:
            self.phase_E()
        if on["F"]:
            self.attention(self.s_qiT, self.s_aoiT, self.s_deni, causal=True)
        if on["G"]:
            self.proj_ln_phase(which="G")
        if on["H"]:
            self.phase_H()

        eng_sems = {e: ctx.enter_context(nc.semaphore(f"se_{e}")) for e in ENGS}
        dma_sems = {q: [ctx.enter_context(nc.semaphore(f"sd_{q}{s}"))
                        for s in range(N_SLOTS)] for q in DMA_QUEUES}
        block = ctx.enter_context(nc.Block())
        per_eng = self.P.realize(eng_sems, dma_sems)

        @block.tensor
        def _(eng):
            for cl in per_eng["tensor"]:
                cl(eng)

        @block.vector
        def _(eng):
            for cl in per_eng["vector"]:
                cl(eng)

        @block.scalar
        def _(eng):
            for cl in per_eng["scalar"]:
                cl(eng)

        @block.gpsimd
        def _(eng):
            for cl in per_eng["gpsimd"]:
                cl(eng)

        @block.sync
        def _(eng):
            for cl in per_eng["sync"]:
                cl(eng)

        ctx.close()
        return nc

    def setup_consts(self):
        E = self.P.emit
        S = self
        E("sync", lambda g: g.dma_start(S.IDENT[:, :], S.c_ident), writes=["IDENT"], dma=True)
        E("sync", lambda g: g.dma_start(S.IDENTF[:, :], S.c_identf), writes=["IDENTF"], dma=True)
        E("sync", lambda g: g.dma_start(S.PERM[:, :], S.c_perm), writes=["PERM"], dma=True)
        E("sync", lambda g: g.dma_start(S.ONES[:, :], S.c_ones), writes=["ONES"], dma=True)
        E("sync", lambda g: g.dma_start(S.BMAP[:, :], S.c_bmap), writes=["BMAP"], dma=True)
        E("sync", lambda g: g.dma_start(
            S.SVTMP[:, :], S.smallv.rearrange("v (cc p) -> (v cc) p", p=128)),
          writes=["SVTMP"], dma=True)
        E("tensor", lambda t: t.transpose(S.PS_A[:, 0:80], S.SVTMP[:, :], S.IDENTF[0:80, 0:80]),
          reads=["SVTMP", "IDENTF"], writes=["psA:0"])
        E("vector", lambda v: v.tensor_copy(S.COLS[:, :], S.PS_A[:, 0:80]),
          reads=["psA:0"], writes=["COLS"])
        E("sync", lambda g: g.dma_start(S.AR3[0:1, 0:1024],
                                        S.smallv[SV["bv"]:SV["bv"] + 1, :]),
          writes=["AR3:0"], dma=True)
        E("vector", lambda v: v.tensor_copy(S.TOK[0:1, 0:1024], S.AR3[0:1, 0:1024]),
          reads=["AR3:0"], writes=["TOK:0"])
        for h in range(2):
            E("tensor", lambda t, hh=h: t.matmul(
                S.psA(hh), S.ONES[0:1, 0:128], S.TOK[0:1, hh * 512:(hh + 1) * 512],
                start=True, stop=True),
              reads=["TOK:0", "ONES"], writes=[f"psA:{h}"])
        E("vector", lambda v: v.tensor_copy(S.BVB[:, 0:1024], S.PS_A[:, 0:1024]),
          reads=["psA:0", "psA:1"], writes=["MSK:0", "MSK:1"])
        for w, src in ((0, S.cos_o), (1, S.sin_o)):
            for i in range(8):
                E("gpsimd", lambda g, s=src, ii=i: g.dma_start(
                    S.ROPO[:, 0:64], s[ii * 128:(ii + 1) * 128, :]),
                  writes=["ROPO"], dma=True)
                E("tensor", lambda t: t.transpose(S.PS_C[0:64, 0:128],
                                                  S.ROPO[:, 0:64], S.IDENT[:, :]),
                  reads=["ROPO", "IDENT"], writes=["psC:0"])
                E("vector", lambda v, ww=w, ii=i: v.tensor_copy(
                    S.TOK[0:64, ww * 1024 + ii * 128: ww * 1024 + (ii + 1) * 128],
                    S.PS_C[0:64, 0:128]),
                  reads=["psC:0"], writes=[f"TOK:{w}"])
        E("vector", lambda v: v.tensor_scalar_mul(S.TOK[0:32, 1024:2048],
                                                  S.TOK[0:32, 1024:2048], -1.0),
          reads=["TOK:1"], writes=["TOK:1"])
        for r in range(2):
            E("gpsimd", lambda g, rr=r: g.dma_start(
                S.s_ropeo[rr * 64:(rr + 1) * 64, :], S.TOK[0:64, 0:2048]),
              reads=["TOK:0", "TOK:1"], writes=["s_ropeo"], dma=True)

    def ln_stats(self, src_ap, src_regions, o):
        E = self.P.emit
        S = self
        E("vector", lambda v: v.reduce_sum(S.STAT[:, o + 2:o + 3], src_ap, AX.X),
          reads=list(src_regions), writes=["STAT"])
        E("scalar", lambda s: s.activation(S.JUNK[:, :], src_ap, AF.Square,
                                           accum_out=S.STAT[:, o + 3:o + 4]),
          reads=list(src_regions), writes=["SILU", "STAT"])
        E("vector", lambda v: v.tensor_scalar_mul(S.STAT[:, o:o + 1],
                                                  S.STAT[:, o + 2:o + 3], 1.0 / C),
          reads=["STAT"], writes=["STAT"])
        E("vector", lambda v: v.tensor_scalar_mul(S.STAT[:, o + 4:o + 5],
                                                  S.STAT[:, o + 3:o + 4], 1.0 / C),
          reads=["STAT"], writes=["STAT"])
        E("vector", lambda v: v.scalar_tensor_tensor(
            S.STAT[:, o + 5:o + 6], S.STAT[:, o:o + 1], S.STAT[:, o:o + 1],
            S.STAT[:, o + 4:o + 5], ALU.mult, ALU.subtract),
          reads=["STAT"], writes=["STAT"])
        E("vector", lambda v: v.tensor_scalar(S.STAT[:, o + 5:o + 6],
                                              S.STAT[:, o + 5:o + 6], -1.0, EPS,
                                              ALU.mult, ALU.add),
          reads=["STAT"], writes=["STAT"])
        E("scalar", lambda s: s.activation(S.STAT[:, o + 6:o + 7],
                                           S.STAT[:, o + 5:o + 6], AF.Sqrt),
          reads=["STAT"], writes=["STAT"])
        E("vector", lambda v: v.reciprocal(S.STAT[:, o + 1:o + 2],
                                           S.STAT[:, o + 6:o + 7]),
          reads=["STAT"], writes=["STAT"])

    def phase_A(self):
        E = self.P.emit
        S = self
        xs = S.AR3[:, 0:1024]
        for tt in range(NTT):
            par = tt % 2
            E("sync", lambda g, i=tt: g.dma_start(xs, S.x_in[i * 128:(i + 1) * 128, :]),
              writes=["AR3:0"], dma=True)
            self.ln_stats(xs, ["AR3:0"], 0)
            E("vector", lambda v: v.tensor_scalar(
                S.TOK[:, 0:1024], xs, S.STAT[:, 0:1], S.STAT[:, 1:2],
                ALU.subtract, ALU.mult),
              reads=["AR3:0", "STAT"], writes=["TOK:0"])
            for cc in range(NCC):
                E("tensor", lambda t, c=cc, p=par: t.transpose(
                    S.PS_C[:, p * 1024 + c * 128: p * 1024 + (c + 1) * 128],
                    S.TOK[:, c * 128:(c + 1) * 128], S.IDENT[:, :]),
                  reads=["TOK:0", "IDENT"], writes=[f"psC:{par}"])
            q = tt % 4
            for cc in range(NCC):
                E("vector", lambda v, c=cc, p=par, qq=q: v.tensor_copy(
                    S.QBUF[:, c * 512 + qq * 128: c * 512 + qq * 128 + 128],
                    S.PS_C[:, p * 1024 + c * 128: p * 1024 + (c + 1) * 128]),
                  reads=[f"psC:{par}"], writes=self.rQB(cc * 512, cc * 512 + 512))
            if q == 3:
                tk = tt // 4
                E("gpsimd", lambda g, k=tk: g.dma_start(
                    S.s_ln0T[:, k * 512:(k + 1) * 512].rearrange(
                        "(cc p) n -> p cc n", p=128),
                    S.QBUF[:, 0:4096].rearrange("p (cc n) -> p cc n", cc=NCC)),
                  reads=self.rQB(0, 4096),
                  writes=[self.sreg(S.s_ln0T, tk)], dma=True)

    def qk_sweep(self, w_dram, w_cols, srcT, dstT, to_ar1, rope_tbl_off,
                 rope_tks, bias_name):
        E = self.P.emit
        S = self
        lo, hi = w_cols
        E("gpsimd", lambda g: g.dma_start(
            S.WBUF[:, :].rearrange("p (cc n) -> p cc n", cc=NCC),
            w_dram[:, lo:hi].rearrange("(cc p) n -> p cc n", p=128)),
          writes=["WBUF"], dma=True)
        for tk in range(NTK):
            spar = tk % 2
            slab = S.QBUF[:, spar * 4096:(spar + 1) * 4096]
            sregs = self.rQB(spar * 4096, (spar + 1) * 4096)
            E("sync", lambda g, k=tk, sl=slab: g.dma_start(
                sl.rearrange("p (cc n) -> p cc n", cc=NCC),
                srcT[:, k * 512:(k + 1) * 512].rearrange(
                    "(cc p) n -> p cc n", p=128)),
              reads=[self.sreg(srcT, tk)], writes=sregs, dma=True)
            for ccp in range(NCC):
                bank = ccp % 2
                for cc in range(NCC):
                    E("tensor", lambda t, c=cc, cp=ccp, b=bank, sl=slab: t.matmul(
                        S.psA(b),
                        S.WBUF[:, c * 1024 + cp * 128: c * 1024 + (cp + 1) * 128],
                        sl[:, c * 512:(c + 1) * 512],
                        start=(c == 0), stop=(c == NCC - 1)),
                      reads=["WBUF"] + sregs,
                      writes=[f"psA:{bank}"])
                if to_ar1:
                    out_ap = self.ar1(ccp, tk * 512, 512)
                    out_reg = self.rAR1_stripe(ccp, tk * 512, 512)
                else:
                    po = (tk % 2) * 512
                    out_ap = S.PB[:, po:po + 512]
                    out_reg = [f"PB:{tk % 2}"]
                if bias_name is not None:
                    E("scalar", lambda s, b=bank, o=out_ap, c=ccp: s.activation(
                        o, S.psA(b), AF.Identity, bias=S.col(bias_name, c)),
                      reads=[f"psA:{bank}", "COLS"], writes=out_reg)
                else:
                    E("scalar", lambda s, b=bank, o=out_ap: s.activation(
                        o, S.psA(b), AF.Copy),
                      reads=[f"psA:{bank}"], writes=out_reg)
                if rope_tbl_off is not None and tk in rope_tks:
                    co, so = rope_tbl_off(tk)
                    pb2o = 1024 + (tk % 2) * 512
                    pb2 = S.PB[:, pb2o:pb2o + 512]
                    pb2r = [f"PB:{2 + tk % 2}"]
                    bank2 = ccp % 2
                    E("tensor", lambda t, o=out_ap, b=bank2: t.matmul(
                        S.psB(b), S.PERM[:, :], o, start=True, stop=True),
                      reads=out_reg + ["PERM"], writes=[f"psB:{bank2}"])
                    E("vector", lambda v, b=bank2, p=pb2, s=so: v.tensor_mul(
                        p, S.psB(b), S.AR2[:, s:s + 512]),
                      reads=[f"psB:{bank2}"] + self.rAR2(so, so + 512),
                      writes=pb2r)
                    E("vector", lambda v, o=out_ap, c=co: v.tensor_mul(
                        S.RT[:, 0:512], o, S.AR2[:, c:c + 512]),
                      reads=out_reg + self.rAR2(co, co + 512), writes=["RT"])
                    E("vector", lambda v, o=out_ap, p=pb2: v.tensor_add(
                        o, S.RT[:, 0:512], p),
                      reads=["RT"] + pb2r, writes=out_reg)
                if not to_ar1:
                    E("gpsimd", lambda g, o=out_ap, c=ccp, k=tk: g.dma_start(
                        dstT[c * 128:(c + 1) * 128, k * 512:(k + 1) * 512], o),
                      reads=out_reg, writes=[self.sreg(dstT, tk)], dma=True)

    def v_sweep(self, w_dram, w_cols, srcT, with_bias):
        E = self.P.emit
        S = self
        lo, hi = w_cols
        E("gpsimd", lambda g: g.dma_start(
            S.WBUF[:, :].rearrange("p (cc n) -> p cc n", cc=NCC),
            w_dram[:, lo:hi].rearrange("(cc p) n -> p cc n", p=128)),
          writes=["WBUF"], dma=True)
        E("vector", lambda v: v.memset(S.AR2[:, 64:33280:65], 1.0),
          writes=self.rAR2(0, 33280))
        for tt in range(NTT):
            tk, q = tt // 4, tt % 4
            spar = tk % 2
            slab = S.QBUF[:, spar * 4096:(spar + 1) * 4096]
            sregs = self.rQB(spar * 4096, (spar + 1) * 4096)
            if q == 0:
                E("sync", lambda g, k=tk, sl=slab: g.dma_start(
                    sl.rearrange("p (cc n) -> p cc n", cc=NCC),
                    srcT[:, k * 512:(k + 1) * 512].rearrange(
                        "(cc p) n -> p cc n", p=128)),
                  reads=[self.sreg(srcT, tk)], writes=sregs, dma=True)
            for half in range(2):
                bank = 2 + half
                for cc in range(NCC):
                    E("tensor", lambda t, c=cc, h=half, b=bank, qq=q, sl=slab: t.matmul(
                        S.psA(b),
                        sl[:, c * 512 + qq * 128: c * 512 + qq * 128 + 128],
                        S.WBUF[:, c * 1024 + h * 512: c * 1024 + (h + 1) * 512],
                        start=(c == 0), stop=(c == NCC - 1)),
                      reads=["WBUF"] + sregs,
                      writes=[f"psA:{bank}"])
            for h in range(NH):
                bank = 2 + (h // 8)
                off = 1024 + (h % 8) * 64 + (h // 8) * 512
                pos = (tt * NH + h) * 65
                dst = S.AR2[:, pos:pos + 64]
                regs = self.rAR2(pos, pos + 64)
                if with_bias:
                    E("vector", lambda v, d=dst, o=off, hh=h: v.scalar_tensor_tensor(
                        d, S.PS_A[:, o:o + 64], 1.0, S.BVB[:, hh * 64:(hh + 1) * 64],
                        ALU.mult, ALU.add),
                      reads=[f"psA:{bank}", "MSK:0", "MSK:1"], writes=regs)
                else:
                    E("vector", lambda v, d=dst, o=off: v.tensor_copy(
                        d, S.PS_A[:, o:o + 64]),
                      reads=[f"psA:{bank}"], writes=regs)

    def phase_B(self):
        E = self.P.emit
        S = self
        E("sync", lambda g: g.dma_start(S.AR2[:, 0:2048], S.s_ropeo),
          reads=["s_ropeo"], writes=self.rAR2(0, 2048), dma=True)
        rope_off = lambda tk: ((tk - 6) * 512, 1024 + (tk - 6) * 512)
        self.qk_sweep(S.wq, (0, C), S.s_ln0T, S.s_qT, False, rope_off, (6, 7), "bq")
        self.qk_sweep(S.wk, (0, C), S.s_ln0T, None, True, rope_off, (6, 7), None)
        self.v_sweep(S.wv, (0, C), S.s_ln0T, True)

    def phase_E(self):
        E = self.P.emit
        S = self
        for cc in range(NCC):
            E("vector", lambda v, c=cc: v.memset(
                S.AR1[:, c * STRW: c * STRW + 2], 0.0),
              writes=self.rAR1(cc * STRW, cc * STRW + 2))
        for cc in range(NCC):
            for tk in range(NTK):
                rd = self.rAR1(cc * STRW + tk * 512, cc * STRW + 2 + (tk + 1) * 512)
                E("vector", lambda v, c=cc, k=tk: v.tensor_scalar_mul(
                    S.TOK[:, 0:512],
                    S.AR1[:, c * STRW + k * 512: c * STRW + k * 512 + 512],
                    S.col("w0", c)),
                  reads=rd + ["COLS"], writes=["TOK:0"])
                E("vector", lambda v, c=cc, k=tk: v.scalar_tensor_tensor(
                    S.TOK[:, 512:1024],
                    S.AR1[:, c * STRW + 1 + k * 512: c * STRW + 1 + k * 512 + 512],
                    S.col("w1", c), S.TOK[:, 0:512], ALU.mult, ALU.add),
                  reads=rd + ["COLS", "TOK:0"], writes=["TOK:0"])
                E("vector", lambda v, c=cc, k=tk: v.scalar_tensor_tensor(
                    S.TOK[:, 0:512], self.ar1(c, k * 512, 512),
                    S.col("w2", c), S.TOK[:, 512:1024], ALU.mult, ALU.add),
                  reads=rd + ["COLS", "TOK:0"], writes=["TOK:0"])
                E("gpsimd", lambda g, c=cc, k=tk: g.dma_start(
                    S.s_hcT[c * 128:(c + 1) * 128, k * 512:(k + 1) * 512],
                    S.TOK[:, 0:512]),
                  reads=["TOK:0"], writes=[self.sreg(S.s_hcT, tk)], dma=True)
        E("sync", lambda g: g.dma_start(S.AR2[:, 0:4096], S.c_ropei[:, 0:4096]),
          writes=self.rAR2(0, 4096), dma=True)
        E("sync", lambda g: g.dma_start(S.AR2[:, 4096:8192], S.c_ropei[:, 4096:8192]),
          writes=self.rAR2(4096, 8192), dma=True)
        rope_off = lambda tk: (tk * 512, 4096 + tk * 512)
        self.qk_sweep(S.attn_w, (0, C), S.s_hcT, S.s_qiT, False, rope_off,
                      tuple(range(8)), None)
        self.qk_sweep(S.attn_w, (C, 2 * C), S.s_hcT, None, True, rope_off,
                      tuple(range(8)), None)
        self.v_sweep(S.attn_w, (2 * C, 3 * C), S.s_hcT, False)


    def _make_conv_jobs(self):
        S = self
        jobs = []
        for e in range(NE):
            for g2 in range(16):
                def j(E, ee=e, gg=g2):
                    E("gpsimd", lambda g: g.dma_start(
                        S.s_wvbf[ee, gg].rearrange("p (gv cc j) -> p gv cc j",
                                                   gv=2, cc=NCC),
                        S.ewv[ee].rearrange(
                            "(cc p) (gv gr j) -> gr p gv cc j",
                            p=128, gv=2, j=256)[gg]),
                      writes=["s_wvbf"], dma=True, slot=3)
                jobs.append(j)
            for j4 in range(8):
                def j(E, ee=e, jj4=j4):
                    E("gpsimd", lambda g: g.dma_start(
                        S.s_pjbf[ee, jj4].rearrange("p (jj n) -> p jj n", jj=4),
                        S.eproj[ee, jj4 * 512:(jj4 + 1) * 512, :].rearrange(
                            "(jj p) n -> p jj n", p=128)),
                      writes=["s_pjbf"], dma=True, slot=3)
                jobs.append(j)
        return jobs

    def drain_conv_jobs(self, k):
        for _ in range(k):
            if self._conv_jobs:
                self._conv_jobs.pop(0)(self.P.emit)

    def attention(self, qT_dram, aoT_dram, den_dram, causal):
        E = self.P.emit
        S = self
        if causal and not self._msk_loaded:
            self._msk_loaded = True
            E("sync", lambda g: g.dma_start(S.MSK[:, :], S.c_masks),
              writes=["MSK:0", "MSK:1"], dma=True)

        def emit_scores(h2, qc, kt, qregs, qbuf):
            par = kt % 2
            for hh in range(2):
                sb_ = 2 * par + hh
                kregs = self.rAR1_stripe(h2, kt * 128, 128)
                E("tensor", lambda t, p=h2, k=kt, b=sb_, r=hh, qb=qbuf,
                  q=qc: t.matmul(
                    S.psA(b),
                    S.AR1[r * 64:r * 64 + 64,
                          p * STRW + 2 + k * 128: p * STRW + 2 + k * 128 + 128],
                    qb[r * 64:r * 64 + 64, q * 512:(q + 1) * 512],
                    start=True, stop=True, tile_position=(r * 64, 0)),
                  reads=kregs + qregs, writes=[f"psA:{sb_}"])
            E("scalar", lambda s, pp=par: s.activation(
                S.PB[:, pp * 1024:(pp + 1) * 1024],
                S.PS_A[:, pp * 1024:(pp + 1) * 1024], AF.Exp, scale=SCALE),
              reads=[f"psA:{2 * par}", f"psA:{2 * par + 1}"],
              writes=[f"PB:{2 * par}", f"PB:{2 * par + 1}"])

        def emit_mask(qc, kt, n_kt):
            if not (causal and kt >= n_kt - 4):
                return
            bnd = kt - (n_kt - 4)
            for hh in range(2):
                po = (2 * (kt % 2) + hh) * 512
                preg = [f"PB:{2 * (kt % 2) + hh}"]
                E("vector", lambda v, o=po, bb=bnd: v.tensor_mul(
                    S.PB[:, o:o + 512], S.PB[:, o:o + 512],
                    S.MSK[:, bb * 512:(bb + 1) * 512]),
                  reads=preg + [f"MSK:{bnd // 2}"], writes=preg)

        def emit_pv(h2, qc, kt):
            for hh in range(2):
                h = 2 * h2 + hh
                po = (2 * (kt % 2) + hh) * 512
                preg = [f"PB:{2 * (kt % 2) + hh}"]
                pos = (kt * NH + h) * 65
                E("tensor", lambda t, o=po, hh_=hh, ps=pos,
                  first=(kt == 0): t.matmul(
                    S.PS_B[0:65, hh_ * 512:hh_ * 512 + 512],
                    S.AR2[:, ps:ps + 65], S.PB[:, o:o + 512],
                    start=first, stop=True),
                  reads=preg + self.rAR2(pos, pos + 65),
                  writes=[f"psB:{hh}"])

        def emit_stage(qc):
            p = qc % 2
            for hh in range(2):
                E("vector", lambda v, o=hh, pp=p: v.tensor_copy(
                    S.STCM[0:65, pp * 1024 + o * 512: pp * 1024 + (o + 1) * 512],
                    S.PS_B[0:65, o * 512:o * 512 + 512]),
                  reads=[f"psB:{hh}"],
                  writes=[f"STCM:{2 * p + hh}"])

        def emit_evac(h2, qc):
            p = qc % 2
            sregs = [f"STCM:{2 * p}", f"STCM:{2 * p + 1}"]
            E("gpsimd", lambda g, p2=h2, pp=p, q=qc: g.dma_start(
                aoT_dram[p2 * 128:(p2 + 1) * 128,
                         q * 512:(q + 1) * 512].rearrange(
                    "(hh pr) t -> pr hh t", hh=2),
                S.STCM[0:64, pp * 1024:(pp + 1) * 1024].rearrange(
                    "pr (hh t) -> pr hh t", hh=2)),
              reads=sregs, writes=[self.sreg(aoT_dram, qc)], dma=True)
            for hh in range(2):
                E("gpsimd", lambda g, p2=h2, pp=p, q=qc, o=hh: g.dma_start(
                    den_dram[2 * p2 + o:2 * p2 + o + 1, q * 512:(q + 1) * 512],
                    S.STCM[64:65, pp * 1024 + o * 512: pp * 1024 + o * 512 + 512]),
                  reads=[f"STCM:{2 * p + hh}"],
                  writes=[self.sreg(den_dram, "d")], dma=True)

        for h2 in range(NH // 2):
            par = h2 % 2
            qbuf = S.QBUF[:, par * 4096:(par + 1) * 4096]
            qregs = self.rQB(par * 4096, (par + 1) * 4096)
            E("sync", lambda g, p=h2, qb=qbuf: g.dma_start(
                qb, qT_dram[p * 128:(p + 1) * 128, :]),
              reads=[self.sreg(qT_dram, k) for k in range(8)],
              writes=qregs, dma=True)
            for qc in range(NTK):
                n_kt = (4 * qc + 4) if causal else 32
                for kt in range(n_kt):
                    emit_scores(h2, qc, kt, qregs, qbuf)
                    emit_mask(qc, kt, n_kt)
                    if kt >= 1:
                        emit_pv(h2, qc, kt - 1)
                emit_pv(h2, qc, n_kt - 1)
                emit_stage(qc)
                emit_evac(h2, qc)
                self.drain_conv_jobs(1)

    def proj_ln_phase(self, which):
        E = self.P.emit
        S = self
        is_d = which == "D"
        w_dram = S.wo if is_d else S.proj_w
        aoT = S.s_aoT if is_d else S.s_aoiT
        x_src = S.x_in if is_d else S.s_fin
        x1_dst = S.s_x1 if is_d else S.s_a
        E("gpsimd", lambda g: g.dma_start(
            S.WBUF[:, :].rearrange("p (cc n) -> p cc n", cc=NCC),
            w_dram.rearrange("(cc p) n -> p cc n", p=128)),
          writes=["WBUF"], dma=True)
        if not is_d:
            E("gpsimd", lambda g: g.dma_start(
                S.RWSB[:, :].rearrange("p (cc e) -> p cc e", cc=NCC),
                S.router_w.rearrange("(cc p) e -> p cc e", p=128)),
              writes=["RWSB"], dma=True)
            for vi, (svk, mo) in enumerate((("ln2w", 0), ("ln2b", 1))):
                E("sync", lambda g, k=svk: g.dma_start(
                    S.AR3[0:1, 0:1024], S.smallv[SV[k]:SV[k] + 1, :]),
                  writes=["AR3:0"], dma=True)
                E("vector", lambda v: v.tensor_copy(S.TOK[0:1, 0:1024],
                                                    S.AR3[0:1, 0:1024]),
                  reads=["AR3:0"], writes=["TOK:0"])
                for hf in range(2):
                    E("tensor", lambda t, h=hf: t.matmul(
                        S.psA(h), S.ONES[0:1, 0:128],
                        S.TOK[0:1, h * 512:(h + 1) * 512],
                        start=True, stop=True),
                      reads=["TOK:0", "ONES"], writes=[f"psA:{hf}"])
                E("vector", lambda v, m=mo: v.tensor_copy(
                    S.MSK[:, m * 1024:(m + 1) * 1024], S.PS_A[:, 0:1024]),
                  reads=["psA:0", "psA:1"], writes=[f"MSK:{mo}"])
        den = S.s_den if is_d else S.s_deni
        for tk in range(NTK):
            spar = tk % 2
            slab = S.QBUF[:, spar * 4096:(spar + 1) * 4096]
            sregs = self.rQB(spar * 4096, (spar + 1) * 4096)
            E("sync", lambda g, k=tk, sl=slab: g.dma_start(
                sl.rearrange("p (cc n) -> p cc n", cc=NCC),
                aoT[:, k * 512:(k + 1) * 512].rearrange(
                    "(cc p) n -> p cc n", p=128)),
              reads=[self.sreg(aoT, tk)], writes=sregs, dma=True)
            E("sync", lambda g, k=tk: g.dma_start(
                S.DENL[:, :], den[:, k * 512:(k + 1) * 512]),
              reads=[self.sreg(den, "d")], writes=["DENL"], dma=True)
            E("vector", lambda v: v.reciprocal(S.DENR[:, :], S.DENL[:, :]),
              reads=["DENL"], writes=["DENR"])
            E("vector", lambda v: v.tensor_copy(S.DENB[:, :], S.DENR[:, :]),
              reads=["DENR"], writes=["DENB"])
            for cc in range(NCC):
                bb = cc % 2
                E("tensor", lambda t, c=cc, b=bb: t.matmul(
                    S.psB(b), S.BMAP[:, c * 128:(c + 1) * 128], S.DENB[:, :],
                    start=True, stop=True),
                  reads=["BMAP", "DENB"], writes=[f"psB:{bb}"])
                E("vector", lambda v, c=cc, b=bb, sl=slab: v.tensor_mul(
                    sl[:, c * 512:(c + 1) * 512], sl[:, c * 512:(c + 1) * 512],
                    S.psB(b)),
                  reads=self.rQB(spar * 4096 + cc * 512,
                                 spar * 4096 + (cc + 1) * 512) + [f"psB:{bb}"],
                  writes=self.rQB(spar * 4096 + cc * 512,
                                  spar * 4096 + (cc + 1) * 512))
            for ccp in range(NCC):
                bank = ccp % 2
                for cc in range(NCC):
                    E("tensor", lambda t, c=cc, cp=ccp, b=bank, sl=slab: t.matmul(
                        S.psA(b),
                        S.WBUF[:, c * 1024 + cp * 128: c * 1024 + (cp + 1) * 128],
                        sl[:, c * 512:(c + 1) * 512],
                        start=(c == 0), stop=(c == NCC - 1)),
                      reads=["WBUF"] + sregs,
                      writes=[f"psA:{bank}"])
                dst = S.STCM[:, ccp * 512:(ccp + 1) * 512]
                sreg = [f"STCM:{ccp}"]
                if is_d:
                    E("scalar", lambda s, b=bank, d=dst, c=ccp: s.activation(
                        d, S.psA(b), AF.Identity, bias=S.col("bo", c)),
                      reads=[f"psA:{bank}", "COLS"], writes=sreg)
                else:
                    E("scalar", lambda s, b=bank, d=dst: s.activation(
                        d, S.psA(b), AF.Copy),
                      reads=[f"psA:{bank}"], writes=sreg)
            for q in range(4):
                tt = tk * 4 + q
                parc = tt % 2
                for cc in range(NCC):
                    E("tensor", lambda t, c=cc, p=parc, qq=q: t.transpose(
                        S.PS_C[:, p * 1024 + c * 128: p * 1024 + (c + 1) * 128],
                        S.STCM[:, c * 512 + qq * 128: c * 512 + qq * 128 + 128],
                        S.IDENT[:, :]),
                      reads=[f"STCM:{cc}", "IDENT"], writes=[f"psC:{parc}"])
                E("vector", lambda v, p=parc: v.tensor_copy(
                    S.TOK[:, 0:1024], S.PS_C[:, p * 1024:(p + 1) * 1024]),
                  reads=[f"psC:{parc}"], writes=["TOK:0"])
                E("sync", lambda g, i=tt: g.dma_start(
                    S.AR3[:, 0:1024], x_src[i * 128:(i + 1) * 128, :]),
                  reads=[self.sreg(x_src, "t")] if not is_d else [],
                  writes=["AR3:0"], dma=True)
                if is_d:
                    self.ln_stats(S.TOK[:, 0:1024], ["TOK:0"], 0)
                    E("vector", lambda v: v.tensor_scalar(
                        S.AR3[:, 1024:2048], S.TOK[:, 0:1024],
                        S.STAT[:, 0:1], S.STAT[:, 1:2], ALU.subtract, ALU.mult),
                      reads=["TOK:0", "STAT"], writes=["AR3:1"])
                    E("vector", lambda v: v.tensor_add(
                        S.AR3[:, 1024:2048], S.AR3[:, 1024:2048], S.AR3[:, 0:1024]),
                      reads=["AR3:1", "AR3:0"], writes=["AR3:1"])
                else:
                    E("vector", lambda v: v.tensor_add(
                        S.AR3[:, 1024:2048], S.TOK[:, 0:1024], S.AR3[:, 0:1024]),
                      reads=["TOK:0", "AR3:0"], writes=["AR3:1"])
                E("gpsimd", lambda g, i=tt: g.dma_start(
                    x1_dst[i * 128:(i + 1) * 128, :], S.AR3[:, 1024:2048]),
                  reads=["AR3:1"], writes=[self.sreg(x1_dst, "t")], dma=True)
                self.ln_stats(S.AR3[:, 1024:2048], ["AR3:1"], 8)
                E("vector", lambda v: v.tensor_scalar(
                    S.AR3[:, 2048:3072], S.AR3[:, 1024:2048],
                    S.STAT[:, 8:9], S.STAT[:, 9:10], ALU.subtract, ALU.mult),
                  reads=["AR3:1", "STAT"], writes=["AR3:2"])
                if is_d:
                    E("gpsimd", lambda g, i=tt: g.dma_start(
                        S.s_fin[i * 128:(i + 1) * 128, :], S.AR3[:, 2048:3072]),
                      reads=["AR3:2"], writes=[self.sreg(S.s_fin, "t")], dma=True)
                E("vector", lambda v: v.tensor_copy(
                    S.TOK[:, 1024:2048], S.AR3[:, 2048:3072]),
                  reads=["AR3:2"], writes=["TOK:1"])
                for cc in range(NCC):
                    E("tensor", lambda t, c=cc, p=parc: t.transpose(
                        S.PS_C[:, p * 1024 + c * 128: p * 1024 + (c + 1) * 128],
                        S.TOK[:, 1024 + c * 128: 1024 + (c + 1) * 128],
                        S.IDENT[:, :]),
                      reads=["TOK:1", "IDENT"], writes=[f"psC:{parc}"])
                lnw = "ln1w" if is_d else "ln2w"
                lnb = "ln1b" if is_d else "ln2b"
                for cc in range(NCC):
                    if is_d:
                        dsth = self.ar1(cc, tt * 128, 128)
                        rr = self.rAR1_stripe(cc, tt * 128, 128)
                    else:
                        dsth = S.STCM[:, 4096 + cc * 128: 4096 + (cc + 1) * 128]
                        rr = [f"STCM:{8 + cc // 4}"]
                    E("vector", lambda v, c=cc, p=parc, d=dsth, rg=rr: v.tensor_scalar(
                        d, S.PS_C[:, p * 1024 + c * 128: p * 1024 + (c + 1) * 128],
                        S.col(lnw, c), S.col(lnb, c), ALU.mult, ALU.add),
                      reads=[f"psC:{parc}", "COLS"], writes=rr)
                if is_d and S.s_hT is not None:
                    for cc in range(NCC):
                        E("gpsimd", lambda g, c=cc, i=tt: g.dma_start(
                            S.s_hT[c * 128:(c + 1) * 128, i * 128:(i + 1) * 128],
                            self.ar1(c, i * 128, 128)),
                          reads=self.rAR1_stripe(cc, tt * 128, 128),
                          writes=["hTdbg"], dma=True)
                if not is_d:
                    E("vector", lambda v: v.tensor_mul(
                        S.AR3[:, 0:1024], S.TOK[:, 1024:2048],
                        S.MSK[:, 0:1024]),
                      reads=["TOK:1", "MSK:0"], writes=["AR3:0"])
                    E("vector", lambda v: v.tensor_add(
                        S.AR3[:, 0:1024], S.AR3[:, 0:1024],
                        S.MSK[:, 1024:2048]),
                      reads=["AR3:0", "MSK:1"], writes=["AR3:0"])
                    E("gpsimd", lambda g, i=tt: g.dma_start(
                        S.s_h2[i * 128:(i + 1) * 128, :],
                        S.AR3[:, 0:1024]),
                      reads=["AR3:0"],
                      writes=[self.sreg(S.s_h2, tt)], dma=True)
                    for cc in range(NCC):
                        E("tensor", lambda t, c=cc: t.matmul(
                            S.PS_B[:, 0:4],
                            S.STCM[:, 4096 + c * 128: 4096 + (c + 1) * 128],
                            S.RWSB[:, c * 4:(c + 1) * 4],
                            start=(c == 0), stop=(c == NCC - 1)),
                          reads=[f"STCM:{8 + cc // 4}", "RWSB"], writes=["psB:0"])
                    E("vector", lambda v: v.reduce_max(
                        S.STAT[:, 16:17], S.PS_B[:, 0:4], AX.X),
                      reads=["psB:0"], writes=["STAT"])
                    E("vector", lambda v: v.tensor_single_scalar(
                        S.STAT[:, 20:24], S.PS_B[:, 0:4], S.STAT[:, 16:17],
                        ALU.is_ge),
                      reads=["psB:0", "STAT"], writes=["STAT"])
                    E("vector", lambda v, i=tt: v.tensor_copy(
                        S.MCOL[:, i * 4:i * 4 + 1], S.STAT[:, 20:21]),
                      reads=["STAT"], writes=["MCOL"])
                    E("vector", lambda v: v.tensor_scalar(
                        S.STAT[:, 17:18], S.STAT[:, 20:21], -1.0, 1.0,
                        ALU.mult, ALU.add),
                      reads=["STAT"], writes=["STAT"])
                    for e in range(1, NE):
                        E("vector", lambda v, i=tt, ee=e: v.tensor_mul(
                            S.MCOL[:, i * 4 + ee:i * 4 + ee + 1],
                            S.STAT[:, 20 + ee:21 + ee], S.STAT[:, 17:18]),
                          reads=["STAT"], writes=["MCOL"])
                        if e < NE - 1:
                            E("vector", lambda v, i=tt, ee=e: v.tensor_sub(
                                S.STAT[:, 17:18], S.STAT[:, 17:18],
                                S.MCOL[:, i * 4 + ee:i * 4 + ee + 1]),
                              reads=["STAT", "MCOL"], writes=["STAT"])

    def moe_machinery(self):
        E = self.P.emit
        S = self
        for dst, src, reg in ((S.TRIS, S.c_tris, "TOK:0"),
                              (S.DCMB, S.c_dcmb, "TOK:0"),
                              (S.SELC, S.c_selc, "SELC"),
                              (S.ONESC, S.c_onesc, "ONESC"),
                              (S.ONESFC, S.c_onesfc, "AR3:2"),
                              (S.IOTAP, S.c_iotap, "IOTAP"),
                              (S.IOTAS4, S.c_iotas4, "AR3:2")):
            E("sync", lambda g, d=dst, s=src: g.dma_start(d[:, :], s),
              writes=[reg], dma=True)
        E("sync", lambda g: g.dma_start(S.MIX[0:1, 32:40], S.c_pos8),
          writes=["AR3:2"], dma=True)
        E("vector", lambda v: v.tensor_copy(S.MCOLB[:, :], S.MCOL[:, :]),
          reads=["MCOL"], writes=["TOK:0"])
        E("tensor", lambda t: t.matmul(S.PS_B[:, 0:1], S.MCOLB[:, :],
                                       S.ONESC[:, :], start=True, stop=True),
          reads=["TOK:0", "ONESC"], writes=["psB:0"])
        E("vector", lambda v: v.tensor_copy(S.CNTC[:, :], S.PS_B[:, 0:1]),
          reads=["psB:0"], writes=["CNTC"])
        E("tensor", lambda t: t.matmul(S.PS_A[:, 0:128], S.TRIS[:, :],
                                       S.MCOLB[:, :], start=True, stop=True),
          reads=["TOK:0", "TOK:0"], writes=["psA:0"])
        E("vector", lambda v: v.tensor_copy(S.TOK[:, 0:128], S.PS_A[:, 0:128]),
          reads=["psA:0"], writes=["TOK:0"])
        E("tensor", lambda t: t.matmul(S.PS_B[:, 512:513], S.DCMB[:, :],
                                       S.CNTC[:, :], start=True, stop=True),
          reads=["TOK:0", "CNTC"], writes=["psB:1"])
        E("vector", lambda v: v.tensor_copy(S.MCH[:, 0:1], S.PS_B[:, 512:513]),
          reads=["psB:1"], writes=["MCH:0"])
        E("tensor", lambda t: t.matmul(S.PS_B[0:1, 4:8], S.CNTC[:, :],
                                       S.SELC[:, :], start=True, stop=True),
          reads=["CNTC", "SELC"], writes=["psB:0"])
        E("vector", lambda v: v.tensor_copy(S.MIX[0:1, 0:4], S.PS_B[0:1, 4:8]),
          reads=["psB:0"], writes=["AR3:2"])
        E("tensor", lambda t: t.transpose(S.PS_B[0:1, 640:768], S.MCH[:, 0:1],
                                          S.IDENTF[:, :]),
          reads=["MCH:0", "IDENTF"], writes=["psB:1"])
        E("vector", lambda v: v.tensor_copy(S.AR3[0:1, 0:128],
                                            S.PS_B[0:1, 640:768]),
          reads=["psB:1"], writes=["AR3:0"])
        E("tensor", lambda t: t.matmul(S.PS_A[:, 512:640], S.ONESFC[:, :],
                                       S.AR3[0:1, 0:128], start=True, stop=True),
          reads=["AR3:2", "AR3:0"], writes=["psA:1"])
        E("vector", lambda v: v.tensor_add(S.AR3[:, 1024:1152], S.TOK[:, 0:128],
                                           S.PS_A[:, 512:640]),
          reads=["TOK:0", "psA:1"], writes=["AR3:1"])
        E("vector", lambda v: v.tensor_mul(S.AR3[:, 2048:2176], S.MCOL[:, :],
                                           S.AR3[:, 1024:1152]),
          reads=["MCOL", "AR3:1"], writes=["AR3:2"])
        E("vector", lambda v: v.tensor_add(S.AR3[:, 1536:1568],
                                           S.AR3[:, 2048:2176:4],
                                           S.AR3[:, 2049:2176:4]),
          reads=["AR3:2"], writes=["AR3:1"])
        for off in (2050, 2051):
            E("vector", lambda v, o=off: v.tensor_add(
                S.AR3[:, 1536:1568], S.AR3[:, 1536:1568], S.AR3[:, o:2176:4]),
              reads=["AR3:1", "AR3:2"], writes=["AR3:1"])
        E("vector", lambda v: v.tensor_copy(S.IXI[:, :], S.AR3[:, 1536:1568]),
          reads=["AR3:1"], writes=["IXI"])
        for tt in range(NTT):
            par = tt % 2
            E("sync", lambda g, i=tt, p=par: g.dma_start(
                S.QBUF[:, p * 1024:(p + 1) * 1024],
                S.s_h2[i * 128:(i + 1) * 128, :]),
              reads=[self.sreg(S.s_h2, tt)],
              writes=self.rQB(par * 1024, (par + 1) * 1024), dma=True)
            E("gpsimd", lambda g, i=tt, p=par: g.indirect_dma_start(
                S.s_hs, bass.IndirectOffsetOnAxis(ap=S.IXI[:, i:i + 1], axis=0),
                S.QBUF[:, p * 1024:(p + 1) * 1024], None),
              reads=["IXI"] + self.rQB(par * 1024, (par + 1) * 1024),
              writes=[self.sreg(S.s_hs, "w")], dma=True)
            E("gpsimd", lambda g, i=tt: g.indirect_dma_start(
                S.s_g, bass.IndirectOffsetOnAxis(ap=S.IXI[:, i:i + 1], axis=0),
                S.IOTAS4[:, i * 4:(i + 1) * 4], None),
              reads=["IXI", "AR3:2"], writes=[self.sreg(S.s_g, "w")], dma=True)
        E("vector", lambda v: v.tensor_single_scalar(
            S.MIX[0:1, 40:48], S.MIX[0:1, 32:40], S.MIX[0:1, 1:2], ALU.is_ge),
          reads=["AR3:2"], writes=["AR3:2"])
        for e in (2, 3):
            E("vector", lambda v, ee=e: v.tensor_single_scalar(
                S.MIX[0:1, 48:56], S.MIX[0:1, 32:40], S.MIX[0:1, ee:ee + 1],
                ALU.is_ge),
              reads=["AR3:2"], writes=["AR3:2"])
            E("vector", lambda v: v.tensor_add(
                S.MIX[0:1, 40:48], S.MIX[0:1, 40:48], S.MIX[0:1, 48:56]),
              reads=["AR3:2"], writes=["AR3:2"])
        E("vector", lambda v: v.tensor_scalar(
            S.MIX[0:1, 16:20], S.MIX[0:1, 0:4], 512.0, 0.0, ALU.is_ge, ALU.add),
          reads=["AR3:2"], writes=["AR3:2"])
        for k in range(2, 8):
            E("vector", lambda v, kk=k: v.tensor_scalar(
                S.MIX[0:1, 24:28], S.MIX[0:1, 0:4], 512.0 * kk, 0.0,
                ALU.is_ge, ALU.add),
              reads=["AR3:2"], writes=["AR3:2"])
            E("vector", lambda v: v.tensor_add(
                S.MIX[0:1, 16:20], S.MIX[0:1, 16:20], S.MIX[0:1, 24:28]),
              reads=["AR3:2"], writes=["AR3:2"])
        E("vector", lambda v: v.scalar_tensor_tensor(
            S.MIX[0:1, 24:28], S.MIX[0:1, 16:20], -512.0, S.MIX[0:1, 0:4],
            ALU.mult, ALU.add),
          reads=["AR3:2"], writes=["AR3:2"])
        E("vector", lambda v: v.tensor_scalar(
            S.MIX[0:1, 20:24], S.MIX[0:1, 24:28], 0.0, 0.0, ALU.is_gt, ALU.add),
          reads=["AR3:2"], writes=["AR3:2"])
        E("tensor", lambda t: t.matmul(S.PS_A[:, 656:659], S.ONESFC[:, :],
                                       S.MIX[0:1, 1:4], start=True, stop=True),
          reads=["AR3:2", "AR3:2"], writes=["psA:1"])
        E("vector", lambda v: v.tensor_copy(S.MCH[:, 1:4], S.PS_A[:, 656:659]),
          reads=["psA:1"], writes=["MCH:0"])
        E("tensor", lambda t: t.matmul(S.PS_A[:, 640:648], S.ONESFC[:, :],
                                       S.MIX[0:1, 40:48], start=True, stop=True),
          reads=["AR3:2", "AR3:2"], writes=["psA:1"])
        E("vector", lambda v: v.tensor_copy(S.MCH[:, 4:12], S.PS_A[:, 640:648]),
          reads=["psA:1"], writes=["MCH:1"])
        E("tensor", lambda t: t.matmul(S.PS_A[:, 648:652], S.ONESFC[:, :],
                                       S.MIX[0:1, 16:20], start=True, stop=True),
          reads=["AR3:2", "AR3:2"], writes=["psA:1"])
        E("vector", lambda v: v.tensor_copy(S.MCH[:, 12:16], S.PS_A[:, 648:652]),
          reads=["psA:1"], writes=["MCH:2"])
        E("tensor", lambda t: t.matmul(S.PS_A[:, 652:656], S.ONESFC[:, :],
                                       S.MIX[0:1, 20:24], start=True, stop=True),
          reads=["AR3:2", "AR3:2"], writes=["psA:1"])
        E("vector", lambda v: v.tensor_copy(S.MCH[:, 16:20], S.PS_A[:, 652:656]),
          reads=["psA:1"], writes=["MCH:3"])
        E("vector", lambda v: v.tensor_single_scalar(
            S.ESF[:, :], S.IOTAS4[:, 0:128:4], S.MCH[:, 1:2], ALU.is_ge),
          reads=["AR3:2", "MCH:0"], writes=["ESF"])
        for e in (2, 3):
            E("vector", lambda v, ee=e: v.tensor_single_scalar(
                S.AR3[:, 0:32], S.IOTAS4[:, 0:128:4], S.MCH[:, ee:ee + 1],
                ALU.is_ge),
              reads=["AR3:2", "MCH:0"], writes=["AR3:0"])
            E("vector", lambda v: v.tensor_add(S.ESF[:, :], S.ESF[:, :],
                                               S.AR3[:, 0:32]),
              reads=["ESF", "AR3:0"], writes=["ESF"])
        if self.s_dbg is not None:
            E("gpsimd", lambda g: g.dma_start(S.s_dbg[:, 0:128], S.MCOL[:, :]),
              reads=["MCOL"], writes=["s_dbg"], dma=True)
            E("gpsimd", lambda g: g.dma_start(S.s_dbg[0:1, 418:482], S.MIX[0:1, :]),
              reads=["AR3:2"], writes=["s_dbg"], dma=True)
            E("gpsimd", lambda g: g.dma_start(S.s_dbg[:, 482:514], S.ESF[:, 0:32]),
              reads=["ESF"], writes=["s_dbg"], dma=True)
            E("gpsimd", lambda g: g.dma_start(S.s_dbg[:, 610:630], S.MCH[:, :]),
              reads=["MCH:0", "MCH:1", "MCH:2", "MCH:3"], writes=["s_dbg"], dma=True)

    def moe_pass(self, pidx):
        E = self.P.emit
        S = self
        BIG = 1.0e6
        WV_OFF = (8192, 12288)
        PJ_OFF = 16384
        primary = pidx < 8
        est = pidx - 7
        mp = pidx % 2
        o = mp * 8
        ixfr = ["AR3:2"]

        def fcol(c):
            return S.IXF[:, o + c:o + c + 1]

        if primary:
            epc = S.MCH[:, 4 + pidx:5 + pidx]
            E("vector", lambda v: v.scalar_tensor_tensor(
                fcol(0), epc, 2048.0, S.IOTAP[:, :], ALU.mult, ALU.add),
              reads=["MCH:1", "IOTAP"], writes=ixfr)
            E("vector", lambda v: v.scalar_tensor_tensor(
                fcol(1), epc, 1024.0, S.IOTAP[:, :], ALU.mult, ALU.add),
              reads=["MCH:1", "IOTAP"], writes=ixfr)
            for g2 in range(16):
                E("vector", lambda v, gg=g2: v.tensor_scalar_add(
                    S.IXW[:, mp * 16 + gg:mp * 16 + gg + 1], fcol(0), 128.0 * gg),
                  reads=ixfr, writes=[f"IXW:{mp}"])
            for j4 in range(8):
                E("vector", lambda v, jj=j4: v.tensor_scalar_add(
                    S.IXP[:, mp * 8 + jj:mp * 8 + jj + 1], fcol(1), 128.0 * jj),
                  reads=ixfr, writes=[f"IXP:{mp}"])
        else:
            E("vector", lambda v: v.scalar_tensor_tensor(
                fcol(0), S.MCH[:, 12 + est:13 + est], 512.0, S.IOTAP[:, :],
                ALU.mult, ALU.add),
              reads=["MCH:2", "IOTAP"], writes=ixfr)
            for q in range(4):
                E("vector", lambda v, qq=q: v.tensor_scalar_add(
                    fcol(2), fcol(0), 128.0 * qq),
                  reads=ixfr, writes=ixfr)
                E("vector", lambda v, qq=q: v.tensor_copy(
                    S.IXH[:, mp * 4 + qq:mp * 4 + qq + 1], fcol(2)),
                  reads=ixfr, writes=[f"IXH:{mp}"])

        for q in range(4):
            gsl = q % 2
            gt = S.QBUF[:, gsl * 1024:(gsl + 1) * 1024]
            gregs = self.rQB(gsl * 1024, (gsl + 1) * 1024)
            gq = S.GQ[:, q * 4:q * 4 + 4]
            gqr = ["AR3:2"]
            if primary:
                row0 = pidx * 512 + q * 128
                E("sync", lambda g, r=row0, d=gt: g.dma_start(
                    d, S.s_hs[r:r + 128, :]),
                  reads=[self.sreg(S.s_hs, "w")], writes=gregs, dma=True)
                E("sync", lambda g, r=row0, d=gq: g.dma_start(
                    d, S.s_g[r:r + 128, :]),
                  reads=[self.sreg(S.s_g, "w")], writes=gqr, dma=True)
            else:
                idxc = S.IXH[:, mp * 4 + q:mp * 4 + q + 1]
                E("gpsimd", lambda g, d=gt, ic=idxc: g.indirect_dma_start(
                    d, None, S.s_hs,
                    bass.IndirectOffsetOnAxis(ap=ic, axis=0)),
                  reads=[f"IXH:{mp}", self.sreg(S.s_hs, "w")],
                  writes=gregs, dma=True)
                E("gpsimd", lambda g, d=gq, ic=idxc: g.indirect_dma_start(
                    d, None, S.s_g,
                    bass.IndirectOffsetOnAxis(ap=ic, axis=0)),
                  reads=[f"IXH:{mp}", self.sreg(S.s_g, "w")],
                  writes=gqr, dma=True)
            if primary:
                esq = S.ESF[:, pidx * 4 + q:pidx * 4 + q + 1]
                E("vector", lambda v, e_=esq: v.tensor_single_scalar(
                    fcol(5), e_, S.MCH[:, 4 + pidx:5 + pidx], ALU.is_equal),
                  reads=["ESF", "MCH:1"], writes=ixfr)
            else:
                E("vector", lambda v, qq=q: v.tensor_scalar_add(
                    fcol(2), fcol(0), 128.0 * qq),
                  reads=ixfr, writes=ixfr)
                E("vector", lambda v: v.tensor_single_scalar(
                    fcol(3), fcol(2), S.MCH[:, 1:2], ALU.is_ge),
                  reads=ixfr + ["MCH:0"], writes=ixfr)
                for ee in (2, 3):
                    E("vector", lambda v, e2=ee: v.tensor_single_scalar(
                        fcol(4), fcol(2), S.MCH[:, e2:e2 + 1], ALU.is_ge),
                      reads=ixfr + ["MCH:0"], writes=ixfr)
                    E("vector", lambda v: v.tensor_add(fcol(3), fcol(3), fcol(4)),
                      reads=ixfr, writes=ixfr)
                E("vector", lambda v: v.tensor_single_scalar(
                    fcol(5), fcol(3), float(est), ALU.is_equal),
                  reads=ixfr, writes=ixfr)
                E("vector", lambda v: v.tensor_mul(
                    fcol(5), fcol(5), S.MCH[:, 16 + est:17 + est]),
                  reads=ixfr + ["MCH:3"], writes=ixfr)
            E("vector", lambda v: v.tensor_scalar(
                fcol(6), fcol(5), -BIG, BIG, ALU.mult, ALU.add),
              reads=ixfr, writes=ixfr)
            E("vector", lambda v, g_=gq: v.tensor_add(
                fcol(6), fcol(6), g_[:, 0:1]),
              reads=ixfr + gqr, writes=ixfr)
            E("vector", lambda v, qq=q: v.tensor_copy(
                S.SIDI[:, mp * 4 + qq:mp * 4 + qq + 1], fcol(6)),
              reads=ixfr, writes=[f"SIDI:{mp}"])
            for cc in range(NCC):
                E("tensor", lambda t, c=cc, qq=q, d=gt: t.transpose(
                    S.PS_C[:, (qq % 2) * 1024 + c * 128:(qq % 2) * 1024 + (c + 1) * 128],
                    d[:, c * 128:(c + 1) * 128], S.IDENT[:, :]),
                  reads=gregs + ["IDENT"], writes=[f"psC:{q % 2}"])
            for cc in range(NCC):
                E("vector", lambda v, c=cc, qq=q: v.tensor_copy(
                    S.AR2[:, c * 512 + qq * 128: c * 512 + qq * 128 + 128],
                    S.PS_C[:, (qq % 2) * 1024 + c * 128:(qq % 2) * 1024 + (c + 1) * 128]),
                  reads=[f"psC:{q % 2}"],
                  writes=self.rAR2(cc * 512, (cc + 1) * 512))

        for g2 in range(16):
            woff = WV_OFF[g2 % 2]
            wslot = S.AR2[:, woff:woff + 4096]
            wregs = self.rAR2(woff, woff + 4096)
            if primary:
                E("gpsimd", lambda g, gg=g2, w=wslot: g.indirect_dma_start(
                    w, None, S.s_wvbf.rearrange("e g p j -> (e g p) j"),
                    bass.IndirectOffsetOnAxis(
                        ap=S.IXW[:, mp * 16 + gg:mp * 16 + gg + 1], axis=0)),
                  reads=[f"IXW:{mp}", "s_wvbf"], writes=wregs, dma=True)
            else:
                E("sync", lambda g, gg=g2, w=wslot: g.dma_start(
                    w, S.s_wvbf[est, gg]),
                  reads=["s_wvbf"], writes=wregs, dma=True)
            for jh in range(2):
                jc = 2 * g2 + jh
                gb, vb = jc % 2, 2 + jc % 2
                for cc in range(NCC):
                    E("tensor", lambda t, c=cc, w=wslot, j=jh, b=gb: t.matmul(
                        S.psA(b),
                        w[:, c * 256 + j * 128: c * 256 + j * 128 + 128],
                        S.AR2[:, c * 512:(c + 1) * 512],
                        start=(c == 0), stop=(c == NCC - 1)),
                      reads=wregs[:4] + self.rAR2(cc * 512, (cc + 1) * 512),
                      writes=[f"psA:{gb}"])
                for cc in range(NCC):
                    E("tensor", lambda t, c=cc, w=wslot, j=jh, b=vb: t.matmul(
                        S.psA(b),
                        w[:, 2048 + c * 256 + j * 128: 2048 + c * 256 + j * 128 + 128],
                        S.AR2[:, c * 512:(c + 1) * 512],
                        start=(c == 0), stop=(c == NCC - 1)),
                      reads=wregs[4:] + self.rAR2(cc * 512, (cc + 1) * 512),
                      writes=[f"psA:{vb}"])
                E("scalar", lambda s, b=gb, j=jc: s.activation(
                    S.SILU[:, (j % 2) * 512:(j % 2) * 512 + 512], S.psA(b),
                    AF.Silu),
                  reads=[f"psA:{gb}"], writes=["SILU"])
                E("vector", lambda v, j=jc, b=vb: v.tensor_mul(
                    S.AR1[:, j * 512:(j + 1) * 512],
                    S.SILU[:, (j % 2) * 512:(j % 2) * 512 + 512], S.psA(b)),
                  reads=["SILU", f"psA:{vb}"],
                  writes=self.rAR1(jc * 512, (jc + 1) * 512))

        for grp, tqs in enumerate(((0, 1, 2), (3,))):
            for jc in range(NJC):
                if jc % 4 == 0:
                    j4 = jc // 4
                    poff4 = PJ_OFF + (j4 % 2) * 4096
                    pslot4 = S.AR2[:, poff4:poff4 + 4096]
                    if primary:
                        E("gpsimd", lambda g, jj=j4, ps=pslot4: g.indirect_dma_start(
                            ps, None, S.s_pjbf.rearrange("e j p n -> (e j p) n"),
                            bass.IndirectOffsetOnAxis(
                                ap=S.IXP[:, mp * 8 + jj:mp * 8 + jj + 1], axis=0)),
                          reads=[f"IXP:{mp}", "s_pjbf"],
                          writes=self.rAR2(poff4, poff4 + 4096), dma=True)
                    else:
                        E("sync", lambda g, jj=j4, ps=pslot4: g.dma_start(
                            ps, S.s_pjbf[est, jj]),
                          reads=["s_pjbf"],
                          writes=self.rAR2(poff4, poff4 + 4096), dma=True)
                poff = PJ_OFF + ((jc // 4) % 2) * 4096 + (jc % 4) * 1024
                pslot = S.AR2[:, poff:poff + 1024]
                pregs = self.rAR2(poff, poff + 1024)
                for i, tq in enumerate(tqs):
                    for half in range(2):
                        bank = 2 * i + half
                        bs = S.psA(bank) if bank < 4 else S.psB(bank - 4)
                        breg = (f"psA:{bank}" if bank < 4 else f"psB:{bank - 4}")
                        E("tensor", lambda t, j=jc, tq_=tq, h=half, b=bs,
                          ps=pslot: t.matmul(
                            b,
                            S.AR1[:, j * 512 + tq_ * 128: j * 512 + tq_ * 128 + 128],
                            ps[:, h * 512:(h + 1) * 512],
                            start=(j == 0), stop=(j == NJC - 1)),
                          reads=pregs + self.rAR1(jc * 512 + tq * 128,
                                                  jc * 512 + tq * 128 + 128),
                          writes=[breg])
            for i, tq in enumerate(tqs):
                if i < 2:
                    src = S.PS_A[:, 2 * i * 512: 2 * i * 512 + 1024]
                    regs = [f"psA:{2 * i}", f"psA:{2 * i + 1}"]
                else:
                    src = S.PS_B[:, 0:1024]
                    regs = ["psB:0", "psB:1"]
                yt = S.QBUF[:, 4096 + tq * 1024: 4096 + (tq + 1) * 1024]
                ytregs = self.rQB(4096 + tq * 1024, 4096 + (tq + 1) * 1024)
                E("vector", lambda v, s=src, d=yt: v.tensor_copy(d, s),
                  reads=regs, writes=ytregs)
                E("gpsimd", lambda g, tq_=tq, d=yt: g.indirect_dma_start(
                    S.out_moe,
                    bass.IndirectOffsetOnAxis(
                        ap=S.SIDI[:, mp * 4 + tq_:mp * 4 + tq_ + 1], axis=0),
                    d, None, bounds_check=T - 1, oob_is_err=False),
                  reads=[f"SIDI:{mp}"] + ytregs,
                  writes=[self.sreg(S.out_moe, "w")], dma=True)

    def phase_H(self):
        self.moe_machinery()
        for pidx in range(11):
            self.moe_pass(pidx)
        E = self.P.emit
        S = self
        for tt in range(NTT):
            par = tt % 2
            om = S.TOK[:, par * 1024:(par + 1) * 1024]
            E("gpsimd", lambda g, i=tt, m=om: g.dma_start(
                m, S.out_moe[i * 128:(i + 1) * 128, :]),
              reads=[self.sreg(S.out_moe, "w")], writes=[f"TOK:{par}"], dma=True)
            E("gpsimd", lambda g, i=tt: g.dma_start(
                S.AR3[:, 0:1024], S.s_a[i * 128:(i + 1) * 128, :]),
              reads=[self.sreg(S.s_a, "t")], writes=["AR3:0"], dma=True)
            E("gpsimd", lambda g, i=tt: g.dma_start(
                S.AR3[:, 1024:2048], S.s_x1[i * 128:(i + 1) * 128, :]),
              reads=[self.sreg(S.s_x1, "t")], writes=["AR3:1"], dma=True)
            E("vector", lambda v, m=om, p=par: v.tensor_add(
                S.AR3[:, 2048:3072], S.AR3[:, 0:1024], m),
              reads=["AR3:0", f"TOK:{par}"], writes=["AR3:2"])
            self.ln_stats(S.AR3[:, 2048:3072], ["AR3:2"], 24)
            E("vector", lambda v: v.tensor_scalar(
                S.AR3[:, 0:1024], S.AR3[:, 2048:3072],
                S.STAT[:, 24:25], S.STAT[:, 25:26], ALU.subtract, ALU.mult),
              reads=["AR3:2", "STAT"], writes=["AR3:0"])
            E("vector", lambda v: v.tensor_add(
                S.AR3[:, 0:1024], S.AR3[:, 0:1024], S.AR3[:, 1024:2048]),
              reads=["AR3:0", "AR3:1"], writes=["AR3:0"])
            E("gpsimd", lambda g, i=tt: g.dma_start(
                S.y_out[i * 128:(i + 1) * 128, :], S.AR3[:, 0:1024]),
              reads=["AR3:0"], writes=["y_out"], dma=True)


_BUILD_CACHE = {}


def get_nc(serial=True, upto="H", dbg=()):
    key = (serial, upto, tuple(sorted(dbg)))
    if key not in _BUILD_CACHE:
        _BUILD_CACHE[key] = Build(serial=serial, upto=upto, dbg=dbg).build()
    return _BUILD_CACHE[key]


def make_inmaps(inputs, cores=8):
    ident, identf, perm, ones_row, rope_i, masks, bmap, routec = host_constants()
    sv = pack_smallvecs(inputs)
    x = np.ascontiguousarray(np.asarray(inputs["L0_attn_hidden_states"], np.float32))
    f32c = lambda k: np.ascontiguousarray(np.asarray(inputs[k], np.float32))
    shared = dict(
        cos_o=f32c("L0_attn_cos"), sin_o=f32c("L0_attn_sin"),
        wq=f32c("L0_attn_wq"), wk=f32c("L0_attn_wk"), wv=f32c("L0_attn_wv"),
        wo=f32c("L0_attn_wo"), attn_w=f32c("L0_ffn_attn_w"),
        proj_w=f32c("L0_ffn_proj_w"), router_w=f32c("L0_ffn_router_w"),
        ewv=f32c("L0_ffn_expert_wv"), eproj=f32c("L0_ffn_expert_proj"),
        smallv=sv, c_ident=ident, c_identf=identf, c_perm=perm,
        c_ones=ones_row, c_ropei=rope_i, c_masks=masks, c_bmap=bmap,
        **routec,
    )
    return [dict(shared, x=np.ascontiguousarray(x[b % x.shape[0]]))
            for b in range(cores)]


def kernel(**inputs):
    nc = get_nc(serial=False, upto="H", dbg=())
    in_maps = make_inmaps(inputs, cores=8)
    res = run_bass_kernel_spmd(nc, in_maps, core_ids=list(range(8)))
    return np.stack([np.asarray(r["y"], np.float32) for r in res.results])


# version 20
# speedup vs baseline: 1.4195x; 1.1746x over previous
import numpy as np
import ml_dtypes
import concourse.bass as bass
import concourse.mybir as mybir
from concourse.bass_utils import run_bass_kernel_spmd

F32 = mybir.dt.float32
BF16 = mybir.dt.bfloat16
AF = mybir.ActivationFunctionType
ALU = mybir.AluOpType
AX = mybir.AxisListType
bfnp = ml_dtypes.bfloat16

T = 4096
C = 1024
NH = 16
HD = 64
NCC = 8
NTT = 32
NTK = 8
EPS = 1e-5
NE = 4
DFF = 4096
NJC = 32
NTC = 4
SCALE = 0.125
STRW = 4112

DMA_QUEUES = ("sync", "gpsimd")
N_SLOTS = 4
ENGS = ("tensor", "vector", "scalar", "gpsimd", "sync")


def _rgn(prefix, lo, hi, gran):
    return [f"{prefix}:{i}" for i in range(lo // gran, (hi + gran - 1) // gran)]


class Prog:

    def __init__(self, serial=False):
        self.items = []
        self.serial = serial

    def emit(self, eng, fn, reads=(), writes=(), dma=False, slot=None):
        self.items.append(dict(eng=eng, fn=fn, reads=tuple(reads),
                               writes=tuple(writes), dma=dma, slot=slot))
        return len(self.items) - 1

    def realize(self, eng_sems, dma_sems):
        n = len(self.items)
        deps = [set() for _ in range(n)]
        if self.serial:
            for i in range(1, n):
                deps[i].add(i - 1)
        else:
            last_w = {}
            readers = {}
            for i, it in enumerate(self.items):
                for r in it["reads"]:
                    if r in last_w:
                        deps[i].add(last_w[r])
                for w in it["writes"]:
                    if w in last_w:
                        deps[i].add(last_w[w])
                    for rd in readers.get(w, ()):
                        deps[i].add(rd)
                for r in it["reads"]:
                    readers.setdefault(r, []).append(i)
                for w in it["writes"]:
                    last_w[w] = i
                    readers[w] = []
        for i, it in enumerate(self.items):
            if it["eng"] == "tensor" and not it["dma"]:
                deps[i] = {j for j in deps[i]
                           if self.items[j]["dma"] or self.items[j]["eng"] != "tensor"}
        dma_idx = {}
        q_counts = {q: 0 for q in DMA_QUEUES}
        default_slots = {"sync": (0, 1, 2, 3), "gpsimd": (0, 1, 2)}
        slot_uses = {(q, s): 0 for q in DMA_QUEUES for s in range(N_SLOTS)}
        for i, it in enumerate(self.items):
            if it["dma"]:
                q = it["eng"]
                if it.get("slot") is not None:
                    s = it["slot"]
                else:
                    cands = default_slots[q]
                    s = cands[q_counts[q] % len(cands)]
                    q_counts[q] += 1
                slot_uses[(q, s)] += 1
                dma_idx[i] = (q, s, slot_uses[(q, s)])
        need_mark = [False] * n
        for i in range(n):
            for j in deps[i]:
                if not self.items[j]["dma"]:
                    need_mark[j] = True
        comp_seq = {}
        ctr = {e: 0 for e in ENGS}
        for i, it in enumerate(self.items):
            if not it["dma"] and need_mark[i]:
                ctr[it["eng"]] += 1
                comp_seq[i] = (it["eng"], ctr[it["eng"]])
        final_counts = dict(ctr)
        per_eng = {e: [] for e in ENGS}
        hwm = {e: {} for e in ENGS}

        def mk_closure(i):
            it = self.items[i]
            eng_name = it["eng"]
            waits = []
            for j in sorted(deps[i]):
                jt = self.items[j]
                if jt["dma"]:
                    q, s, use = dma_idx[j]
                    waits.append((("d", q, s), 16 * use))
                else:
                    e2, seq = comp_seq[j]
                    waits.append((("c", e2), seq))
            if it["dma"]:
                q, s, use = dma_idx[i]
                if use > 1:
                    waits.append((("d", q, s), 16 * (use - 1)))
            h = hwm[eng_name]
            real_waits = []
            for key, val in waits:
                if h.get(key, 0) >= val:
                    continue
                h[key] = val
                real_waits.append((key, val))
            if it["dma"]:
                q, s, use = dma_idx[i]
                inc = (("d", q, s), 16)
            elif need_mark[i]:
                inc = (("c", eng_name), 1)
            else:
                inc = None

            def closure(eng):
                for key, val in real_waits:
                    sem = dma_sems[key[1]][key[2]] if key[0] == "d" else eng_sems[key[1]]
                    eng.wait_ge(sem, val)
                ins = it["fn"](eng)
                if inc is not None:
                    key, amt = inc
                    sem = dma_sems[key[1]][key[2]] if key[0] == "d" else eng_sems[key[1]]
                    ins.then_inc(sem, amt)
            return closure

        for i in range(n):
            per_eng[self.items[i]["eng"]].append(mk_closure(i))

        def final_join(eng):
            for q in DMA_QUEUES:
                for s in range(N_SLOTS):
                    if slot_uses[(q, s)]:
                        eng.wait_ge(dma_sems[q][s], 16 * slot_uses[(q, s)])
            for e, cnt in final_counts.items():
                if cnt and e != "sync":
                    eng.wait_ge(eng_sems[e], cnt)
        per_eng["sync"].append(final_join)
        return per_eng


def host_constants():
    ident = np.eye(128, dtype=bfnp)
    identf = np.eye(128, dtype=np.float32)
    perm = np.zeros((128, 128), dtype=bfnp)
    for h in range(2):
        b = 64 * h
        for i in range(32):
            perm[b + i, b + 32 + i] = 1.0
            perm[b + 32 + i, b + i] = 1.0
    ones_row = np.ones((1, 128), dtype=bfnp)
    inv_freq = 1.0 / (10000.0 ** (np.arange(0, HD, 2, dtype=np.float32) / HD))
    t = np.arange(T, dtype=np.float32)
    emb = np.concatenate([np.outer(t, inv_freq), np.outer(t, inv_freq)], axis=-1)
    cosT = np.cos(emb).T.astype(np.float32)
    sinT = np.sin(emb).T.astype(np.float32)
    sinT[0:32] = -sinT[0:32]
    rope_i = np.zeros((128, 2 * T), dtype=bfnp)
    rope_i[0:64, 0:T] = cosT.astype(bfnp)
    rope_i[64:128, 0:T] = cosT.astype(bfnp)
    rope_i[0:64, T:] = sinT.astype(bfnp)
    rope_i[64:128, T:] = sinT.astype(bfnp)
    masks = np.zeros((128, 4 * 512), dtype=bfnp)
    r = np.arange(128)[:, None]
    j = np.arange(512)[None, :]
    for b in range(4):
        masks[:, b * 512:(b + 1) * 512] = (j >= 128 * b + r).astype(bfnp)
    bmap = np.zeros((16, 1024), dtype=bfnp)
    for c in range(8):
        bmap[2 * c, c * 128: c * 128 + 64] = 1.0
        bmap[2 * c + 1, c * 128 + 64: c * 128 + 128] = 1.0
    ii = np.arange(128)
    tris = (ii[:, None] < ii[None, :]).astype(bfnp)
    ti, ei = ii[:, None] // 4, ii[:, None] % 4
    tj, ej = ii[None, :] // 4, ii[None, :] % 4
    dcmb = (((ei == ej) & (ti < tj)) | (ei < ej)).astype(bfnp)
    selc = (ii[:, None] % 4 < np.arange(4)[None, :])[:128].astype(bfnp)
    selc = np.ascontiguousarray(selc[:, :4])
    onesc = np.ones((128, 1), dtype=bfnp)
    onesfc = np.ones((1, 128), dtype=np.float32)
    iotap = np.arange(128, dtype=np.float32).reshape(128, 1)
    iotas4 = np.zeros((128, 128), dtype=np.float32)
    for tt in range(32):
        iotas4[:, tt * 4:(tt + 1) * 4] = (tt * 128 + ii)[:, None]
    pos8 = (np.arange(8, dtype=np.float32) * 512).reshape(1, 8)
    routec = dict(c_tris=tris, c_dcmb=dcmb, c_selc=selc, c_onesc=onesc,
                  c_onesfc=onesfc, c_iotap=iotap, c_iotas4=iotas4,
                  c_pos8=pos8)
    return ident, identf, perm, ones_row, rope_i, masks, bmap, routec


SV = {k: i for i, k in enumerate(
    ["bq", "bv", "bo", "ln1w", "ln1b", "ln2w", "ln2b", "w0", "w1", "w2"])}


def pack_smallvecs(ins):
    names = ["L0_attn_bq", "L0_attn_bv", "L0_attn_bo", "L0_ffn_ln1_w",
             "L0_ffn_ln1_b", "L0_ffn_ln2_w", "L0_ffn_ln2_b"]
    rows = [np.asarray(ins[n], np.float32).reshape(C) for n in names]
    cw = np.asarray(ins["L0_ffn_conv_w"], np.float32).reshape(C, 3)
    rows += [np.ascontiguousarray(cw[:, i]) for i in range(3)]
    return np.stack(rows)


class Build:
    def __init__(self, serial=True, upto="H", dbg=()):
        self.serial = serial
        self.upto = upto
        self.dbg = set(dbg)
        self.P = Prog(serial=serial)

    def rAR1(self, lo, hi):
        return _rgn("AR1", lo, hi, 512)

    def rAR1_stripe(self, cc, lo, w):
        a = cc * STRW + 2 + lo
        return self.rAR1(a, a + w)

    def rAR2(self, lo, hi):
        return _rgn("AR2", lo, hi, 512)

    def rQB(self, lo, hi):
        return _rgn("QB", lo, hi, 512)

    def rAR3(self, slot):
        return [f"AR3:{slot}"]

    def ar1(self, cc, lo, w):
        return self.AR1[:, cc * STRW + 2 + lo: cc * STRW + 2 + lo + w]

    def psA(self, b, w=512):
        return self.PS_A[:, b * 512: b * 512 + w]

    def psB(self, b, w=512):
        return self.PS_B[:, b * 512: b * 512 + w]

    def col(self, vname, cc):
        i = SV[vname] * 8 + cc
        return self.COLS[:, i:i + 1]

    def sreg(self, ap, sub):
        return f"{ap.tensor.name}:{sub}"

    def build(self):
        nc = bass.Bass("TRN2", target_bir_lowering=False)
        self.nc = nc
        PH = "ABCDEFGH"
        on = {p: PH.index(p) <= PH.index(self.upto) for p in PH}

        din = lambda name, shape, dt: nc.dram_tensor(name, shape, dt, kind="ExternalInput").ap()

        def scratch(name, shape, dt):
            kind = "ExternalOutput" if name in self.dbg else "Internal"
            return nc.dram_tensor(name, shape, dt, kind=kind).ap()

        self.x_in = din("x", [T, C], F32)
        self.cos_o = din("cos_o", [1024, HD], F32)
        self.sin_o = din("sin_o", [1024, HD], F32)
        self.wq = din("wq", [C, C], F32)
        self.wk = din("wk", [C, C], F32)
        self.wv = din("wv", [C, C], F32)
        self.wo = din("wo", [C, C], F32)
        self.attn_w = din("attn_w", [C, 3 * C], F32)
        self.proj_w = din("proj_w", [C, C], F32)
        self.router_w = din("router_w", [C, NE], F32)
        self.ewv = din("ewv", [NE, C, 2 * DFF], F32)
        self.eproj = din("eproj", [NE, DFF, C], F32)
        self.smallv = din("smallv", [10, C], F32)
        self.c_ident = din("c_ident", [128, 128], BF16)
        self.c_identf = din("c_identf", [128, 128], F32)
        self.c_perm = din("c_perm", [128, 128], BF16)
        self.c_ones = din("c_ones", [1, 128], BF16)
        self.c_ropei = din("c_ropei", [128, 2 * T], BF16)
        self.c_masks = din("c_masks", [128, 4 * 512], BF16)
        self.c_bmap = din("c_bmap", [16, 1024], BF16)
        self.c_tris = din("c_tris", [128, 128], BF16)
        self.c_dcmb = din("c_dcmb", [128, 128], BF16)
        self.c_selc = din("c_selc", [128, 4], BF16)
        self.c_onesc = din("c_onesc", [128, 1], BF16)
        self.c_onesfc = din("c_onesfc", [1, 128], F32)
        self.c_iotap = din("c_iotap", [128, 1], F32)
        self.c_iotas4 = din("c_iotas4", [128, 128], F32)
        self.c_pos8 = din("c_pos8", [1, 8], F32)

        self.y_out = nc.dram_tensor("y", [T, C], F32, kind="ExternalOutput").ap()

        self.s_ln0T = scratch("s_ln0T", [C, T], BF16)
        self.s_qT = scratch("s_qT", [C, T], BF16)
        self.s_aoT = scratch("s_aoT", [C, T], BF16)
        self.s_x1 = scratch("s_x1", [T, C], F32)
        self.s_fin = scratch("s_fin", [T, C], F32)
        self.s_hcT = scratch("s_hcT", [C, T], BF16)
        self.s_qiT = scratch("s_qiT", [C, T], BF16)
        self.s_aoiT = scratch("s_aoiT", [C, T], BF16)
        self.s_a = scratch("s_a", [T, C], F32)
        self.s_h2 = scratch("s_h2", [T, C], BF16)
        self.s_hs = scratch("s_hs", [T, C], BF16)
        self.s_g = scratch("s_g", [T, 4], F32)
        self.out_moe = scratch("out_moe", [T, C], BF16)
        self.s_den = scratch("s_den", [NH, T], BF16)
        self.s_deni = scratch("s_deni", [NH, T], BF16)
        self.s_dbg = (scratch("s_dbg", [128, 640], F32)
                      if "s_dbg" in self.dbg else None)
        self.s_ropeo = scratch("s_ropeo", [128, 2048], BF16)
        self.s_wvbf = scratch("s_wvbf", [NE, 16, 128, 4096], BF16)
        self.s_pjbf = scratch("s_pjbf", [NE, 8, 128, 4096], BF16)
        self.s_hT = scratch("s_hT", [C, T], BF16) if "s_hT" in self.dbg else None

        from contextlib import ExitStack
        ctx = self.ctx = ExitStack()
        sb = lambda name, shape, dt: ctx.enter_context(nc.sbuf_tensor(name, shape, dt))
        pst = lambda name, shape, dt: ctx.enter_context(nc.psum_tensor(name, shape, dt))

        self.AR1 = sb("AR1", [128, NCC * STRW], BF16)
        self.AR2 = sb("AR2", [128, 33280], BF16)
        self.AR3 = sb("AR3", [128, 3072], F32)
        self.WBUF = sb("WBUF", [128, 8192], BF16)
        self.QBUF = sb("QBUF", [128, 8192], BF16)
        self.MSK = sb("MSK", [128, 2048], BF16)
        self.BVB = self.MSK
        self._msk_loaded = False
        self._conv_jobs = self._make_conv_jobs()
        self.IDENT = sb("IDENT", [128, 128], BF16)
        self.IDENTF = sb("IDENTF", [128, 128], F32)
        self.PERM = sb("PERM", [128, 128], BF16)
        self.ONES = sb("ONES", [1, 128], BF16)
        self.COLS = sb("COLS", [128, 80], F32)
        self.PB = sb("PB", [128, 4 * 512], BF16)
        self.BMAP = sb("BMAP", [16, 1024], BF16)
        self.DENL = sb("DENL", [16, 512], BF16)
        self.DENR = sb("DENR", [16, 512], F32)
        self.DENB = sb("DENB", [16, 512], BF16)
        self.RT = sb("RT", [128, 512], BF16)
        self.TOK = sb("TOK", [128, 2 * C], BF16)
        self.STAT = sb("STAT", [128, 32], F32)
        self.MCOL = sb("MCOL", [128, 128], F32)
        self.SVTMP = sb("SVTMP", [80, 128], F32)
        self.ROPO = sb("ROPO", [128, 64], BF16)
        self.SILU = sb("SILU", [128, C], BF16)
        self.JUNK = self.SILU
        self.RWSB = sb("RWSB", [128, 32], BF16)
        self.STCM = sb("STCM", [128, 5120], BF16)
        self.MCOLB = self.TOK[:, 512:640]
        self.TRIS = self.TOK[:, 256:384]
        self.DCMB = self.TOK[:, 384:512]
        self.SELC = sb("SELC", [128, 4], BF16)
        self.ONESC = sb("ONESC", [128, 1], BF16)
        self.ONESFC = self.AR3[0:1, 2688:2816]
        self.IOTAP = sb("IOTAP", [128, 1], F32)
        self.IOTAS4 = self.AR3[:, 2560:2688]
        self.CNTC = sb("CNTC", [128, 1], BF16)
        self.MIX = self.AR3[0:1, 2816:2880]
        self.MCH = sb("MCH", [128, 20], F32)
        self.ESF = sb("ESF", [128, 32], F32)
        self.IXI = sb("IXI", [128, 32], mybir.dt.int32)
        self.IXW = sb("IXW", [128, 32], mybir.dt.int32)
        self.IXP = sb("IXP", [128, 16], mybir.dt.int32)
        self.IXH = sb("IXH", [128, 8], mybir.dt.int32)
        self.SIDI = sb("SIDI", [128, 8], mybir.dt.int32)
        self.IXF = self.AR3[:, 2880:2896]
        self.GQ = self.AR3[:, 2896:2912]

        self.PS_A = pst("PS_A", [128, 2048], F32)
        self.PS_B = pst("PS_B", [128, 1024], F32)
        self.PS_C = pst("PS_C", [128, 2048], BF16)

        self.setup_consts()
        if on["A"]:
            self.phase_A()
        if on["B"]:
            self.phase_B()
        if on["C"]:
            self.attention(self.s_qT, self.s_aoT, self.s_den, causal=False)
        if on["D"]:
            self.proj_ln_phase(which="D")
        if on["E"]:
            self.phase_E()
        if on["F"]:
            self.attention(self.s_qiT, self.s_aoiT, self.s_deni, causal=True)
        if on["G"]:
            self.proj_ln_phase(which="G")
        if on["H"]:
            self.phase_H()

        eng_sems = {e: ctx.enter_context(nc.semaphore(f"se_{e}")) for e in ENGS}
        dma_sems = {q: [ctx.enter_context(nc.semaphore(f"sd_{q}{s}"))
                        for s in range(N_SLOTS)] for q in DMA_QUEUES}
        block = ctx.enter_context(nc.Block())
        per_eng = self.P.realize(eng_sems, dma_sems)

        @block.tensor
        def _(eng):
            for cl in per_eng["tensor"]:
                cl(eng)

        @block.vector
        def _(eng):
            for cl in per_eng["vector"]:
                cl(eng)

        @block.scalar
        def _(eng):
            for cl in per_eng["scalar"]:
                cl(eng)

        @block.gpsimd
        def _(eng):
            for cl in per_eng["gpsimd"]:
                cl(eng)

        @block.sync
        def _(eng):
            for cl in per_eng["sync"]:
                cl(eng)

        ctx.close()
        return nc

    def setup_consts(self):
        E = self.P.emit
        S = self
        E("sync", lambda g: g.dma_start(S.IDENT[:, :], S.c_ident), writes=["IDENT"], dma=True)
        E("sync", lambda g: g.dma_start(S.IDENTF[:, :], S.c_identf), writes=["IDENTF"], dma=True)
        E("sync", lambda g: g.dma_start(S.PERM[:, :], S.c_perm), writes=["PERM"], dma=True)
        E("sync", lambda g: g.dma_start(S.ONES[:, :], S.c_ones), writes=["ONES"], dma=True)
        E("sync", lambda g: g.dma_start(S.BMAP[:, :], S.c_bmap), writes=["BMAP"], dma=True)
        E("sync", lambda g: g.dma_start(
            S.SVTMP[:, :], S.smallv.rearrange("v (cc p) -> (v cc) p", p=128)),
          writes=["SVTMP"], dma=True)
        E("tensor", lambda t: t.transpose(S.PS_A[:, 0:80], S.SVTMP[:, :], S.IDENTF[0:80, 0:80]),
          reads=["SVTMP", "IDENTF"], writes=["psA:0"])
        E("vector", lambda v: v.tensor_copy(S.COLS[:, :], S.PS_A[:, 0:80]),
          reads=["psA:0"], writes=["COLS"])
        E("sync", lambda g: g.dma_start(S.AR3[0:1, 0:1024],
                                        S.smallv[SV["bv"]:SV["bv"] + 1, :]),
          writes=["AR3:0"], dma=True)
        E("vector", lambda v: v.tensor_copy(S.TOK[0:1, 0:1024], S.AR3[0:1, 0:1024]),
          reads=["AR3:0"], writes=["TOK:0"])
        for h in range(2):
            E("tensor", lambda t, hh=h: t.matmul(
                S.psA(hh), S.ONES[0:1, 0:128], S.TOK[0:1, hh * 512:(hh + 1) * 512],
                start=True, stop=True),
              reads=["TOK:0", "ONES"], writes=[f"psA:{h}"])
        E("vector", lambda v: v.tensor_copy(S.BVB[:, 0:1024], S.PS_A[:, 0:1024]),
          reads=["psA:0", "psA:1"], writes=["MSK:0", "MSK:1"])
        for w, src in ((0, S.cos_o), (1, S.sin_o)):
            for i in range(8):
                E("gpsimd", lambda g, s=src, ii=i: g.dma_start(
                    S.ROPO[:, 0:64], s[ii * 128:(ii + 1) * 128, :]),
                  writes=["ROPO"], dma=True)
                E("tensor", lambda t: t.transpose(S.PS_C[0:64, 0:128],
                                                  S.ROPO[:, 0:64], S.IDENT[:, :]),
                  reads=["ROPO", "IDENT"], writes=["psC:0"])
                E("vector", lambda v, ww=w, ii=i: v.tensor_copy(
                    S.TOK[0:64, ww * 1024 + ii * 128: ww * 1024 + (ii + 1) * 128],
                    S.PS_C[0:64, 0:128]),
                  reads=["psC:0"], writes=[f"TOK:{w}"])
        E("vector", lambda v: v.tensor_scalar_mul(S.TOK[0:32, 1024:2048],
                                                  S.TOK[0:32, 1024:2048], -1.0),
          reads=["TOK:1"], writes=["TOK:1"])
        for r in range(2):
            E("gpsimd", lambda g, rr=r: g.dma_start(
                S.s_ropeo[rr * 64:(rr + 1) * 64, :], S.TOK[0:64, 0:2048]),
              reads=["TOK:0", "TOK:1"], writes=["s_ropeo"], dma=True)

    def rST(self, o):
        return f"STAT:{o // 8}"

    def ln_stats(self, src_ap, src_regions, o, junk=None):
        E = self.P.emit
        S = self
        r = [self.rST(o)]
        if junk is None:
            junk = (S.JUNK[:, :], "SILU")
        E("vector", lambda v: v.reduce_sum(S.STAT[:, o + 2:o + 3], src_ap, AX.X),
          reads=list(src_regions), writes=r)
        E("scalar", lambda s: s.activation(junk[0], src_ap, AF.Square,
                                           accum_out=S.STAT[:, o + 3:o + 4]),
          reads=list(src_regions), writes=[junk[1]] + r)
        E("vector", lambda v: v.tensor_scalar_mul(S.STAT[:, o:o + 1],
                                                  S.STAT[:, o + 2:o + 3], 1.0 / C),
          reads=r, writes=r)
        E("vector", lambda v: v.tensor_scalar_mul(S.STAT[:, o + 4:o + 5],
                                                  S.STAT[:, o + 3:o + 4], 1.0 / C),
          reads=r, writes=r)
        E("vector", lambda v: v.scalar_tensor_tensor(
            S.STAT[:, o + 5:o + 6], S.STAT[:, o:o + 1], S.STAT[:, o:o + 1],
            S.STAT[:, o + 4:o + 5], ALU.mult, ALU.subtract),
          reads=r, writes=r)
        E("vector", lambda v: v.tensor_scalar(S.STAT[:, o + 5:o + 6],
                                              S.STAT[:, o + 5:o + 6], -1.0, EPS,
                                              ALU.mult, ALU.add),
          reads=r, writes=r)
        E("scalar", lambda s: s.activation(S.STAT[:, o + 6:o + 7],
                                           S.STAT[:, o + 5:o + 6], AF.Sqrt),
          reads=r, writes=r)
        E("vector", lambda v: v.reciprocal(S.STAT[:, o + 1:o + 2],
                                           S.STAT[:, o + 6:o + 7]),
          reads=r, writes=r)

    def phase_A(self):
        E = self.P.emit
        S = self
        for tt in range(NTT):
            par = tt % 2
            xs = S.AR3[:, par * 1024:(par + 1) * 1024]
            xr = f"AR3:{par}"
            o = par * 16
            E("sync", lambda g, i=tt, x=xs, r=xr: g.dma_start(
                x, S.x_in[i * 128:(i + 1) * 128, :]),
              writes=[xr], dma=True)
            self.ln_stats(xs, [xr], o)
            E("vector", lambda v, x=xs, oo=o, p=par: v.tensor_scalar(
                S.TOK[:, p * 1024:(p + 1) * 1024], x,
                S.STAT[:, oo:oo + 1], S.STAT[:, oo + 1:oo + 2],
                ALU.subtract, ALU.mult),
              reads=[xr, self.rST(o)], writes=[f"TOK:{par}"])
            for cc in range(NCC):
                E("tensor", lambda t, c=cc, p=par: t.transpose(
                    S.PS_C[:, p * 1024 + c * 128: p * 1024 + (c + 1) * 128],
                    S.TOK[:, p * 1024 + c * 128: p * 1024 + (c + 1) * 128],
                    S.IDENT[:, :]),
                  reads=[f"TOK:{par}", "IDENT"], writes=[f"psC:{par}"])
            q = tt % 4
            for cc in range(NCC):
                E("vector", lambda v, c=cc, p=par, qq=q: v.tensor_copy(
                    S.QBUF[:, c * 512 + qq * 128: c * 512 + qq * 128 + 128],
                    S.PS_C[:, p * 1024 + c * 128: p * 1024 + (c + 1) * 128]),
                  reads=[f"psC:{par}"], writes=self.rQB(cc * 512, cc * 512 + 512))
            if q == 3:
                tk = tt // 4
                E("gpsimd", lambda g, k=tk: g.dma_start(
                    S.s_ln0T[:, k * 512:(k + 1) * 512].rearrange(
                        "(cc p) n -> p cc n", p=128),
                    S.QBUF[:, 0:4096].rearrange("p (cc n) -> p cc n", cc=NCC)),
                  reads=self.rQB(0, 4096),
                  writes=[self.sreg(S.s_ln0T, tk)], dma=True)

    def qk_sweep(self, w_dram, w_cols, srcT, dstT, to_ar1, rope_tbl_off,
                 rope_tks, bias_name):
        E = self.P.emit
        S = self
        lo, hi = w_cols
        E("gpsimd", lambda g: g.dma_start(
            S.WBUF[:, :].rearrange("p (cc n) -> p cc n", cc=NCC),
            w_dram[:, lo:hi].rearrange("(cc p) n -> p cc n", p=128)),
          writes=["WBUF"], dma=True)
        for tk in range(NTK):
            spar = tk % 2
            slab = S.QBUF[:, spar * 4096:(spar + 1) * 4096]
            sregs = self.rQB(spar * 4096, (spar + 1) * 4096)
            E("sync", lambda g, k=tk, sl=slab: g.dma_start(
                sl.rearrange("p (cc n) -> p cc n", cc=NCC),
                srcT[:, k * 512:(k + 1) * 512].rearrange(
                    "(cc p) n -> p cc n", p=128)),
              reads=[self.sreg(srcT, tk)], writes=sregs, dma=True)
            for ccp in range(NCC):
                bank = ccp % 2
                for cc in range(NCC):
                    E("tensor", lambda t, c=cc, cp=ccp, b=bank, sl=slab: t.matmul(
                        S.psA(b),
                        S.WBUF[:, c * 1024 + cp * 128: c * 1024 + (cp + 1) * 128],
                        sl[:, c * 512:(c + 1) * 512],
                        start=(c == 0), stop=(c == NCC - 1)),
                      reads=["WBUF"] + sregs,
                      writes=[f"psA:{bank}"])
                if to_ar1:
                    out_ap = self.ar1(ccp, tk * 512, 512)
                    out_reg = self.rAR1_stripe(ccp, tk * 512, 512)
                else:
                    po = (tk % 2) * 512
                    out_ap = S.PB[:, po:po + 512]
                    out_reg = [f"PB:{tk % 2}"]
                if bias_name is not None:
                    E("scalar", lambda s, b=bank, o=out_ap, c=ccp: s.activation(
                        o, S.psA(b), AF.Identity, bias=S.col(bias_name, c)),
                      reads=[f"psA:{bank}", "COLS"], writes=out_reg)
                else:
                    E("scalar", lambda s, b=bank, o=out_ap: s.activation(
                        o, S.psA(b), AF.Copy),
                      reads=[f"psA:{bank}"], writes=out_reg)
                if rope_tbl_off is not None and tk in rope_tks:
                    co, so = rope_tbl_off(tk)
                    pb2o = 1024 + (tk % 2) * 512
                    pb2 = S.PB[:, pb2o:pb2o + 512]
                    pb2r = [f"PB:{2 + tk % 2}"]
                    bank2 = ccp % 2
                    E("tensor", lambda t, o=out_ap, b=bank2: t.matmul(
                        S.psB(b), S.PERM[:, :], o, start=True, stop=True),
                      reads=out_reg + ["PERM"], writes=[f"psB:{bank2}"])
                    E("vector", lambda v, b=bank2, p=pb2, s=so: v.tensor_mul(
                        p, S.psB(b), S.AR2[:, s:s + 512]),
                      reads=[f"psB:{bank2}"] + self.rAR2(so, so + 512),
                      writes=pb2r)
                    E("vector", lambda v, o=out_ap, c=co: v.tensor_mul(
                        S.RT[:, 0:512], o, S.AR2[:, c:c + 512]),
                      reads=out_reg + self.rAR2(co, co + 512), writes=["RT"])
                    E("vector", lambda v, o=out_ap, p=pb2: v.tensor_add(
                        o, S.RT[:, 0:512], p),
                      reads=["RT"] + pb2r, writes=out_reg)
                if not to_ar1:
                    E("gpsimd", lambda g, o=out_ap, c=ccp, k=tk: g.dma_start(
                        dstT[c * 128:(c + 1) * 128, k * 512:(k + 1) * 512], o),
                      reads=out_reg, writes=[self.sreg(dstT, tk)], dma=True)

    def v_sweep(self, w_dram, w_cols, srcT, with_bias):
        E = self.P.emit
        S = self
        lo, hi = w_cols
        E("gpsimd", lambda g: g.dma_start(
            S.WBUF[:, :].rearrange("p (cc n) -> p cc n", cc=NCC),
            w_dram[:, lo:hi].rearrange("(cc p) n -> p cc n", p=128)),
          writes=["WBUF"], dma=True)
        E("vector", lambda v: v.memset(S.AR2[:, 64:33280:65], 1.0),
          writes=self.rAR2(0, 33280))
        for tt in range(NTT):
            tk, q = tt // 4, tt % 4
            spar = tk % 2
            slab = S.QBUF[:, spar * 4096:(spar + 1) * 4096]
            sregs = self.rQB(spar * 4096, (spar + 1) * 4096)
            if q == 0:
                E("sync", lambda g, k=tk, sl=slab: g.dma_start(
                    sl.rearrange("p (cc n) -> p cc n", cc=NCC),
                    srcT[:, k * 512:(k + 1) * 512].rearrange(
                        "(cc p) n -> p cc n", p=128)),
                  reads=[self.sreg(srcT, tk)], writes=sregs, dma=True)
            for half in range(2):
                bank = 2 + half
                for cc in range(NCC):
                    E("tensor", lambda t, c=cc, h=half, b=bank, qq=q, sl=slab: t.matmul(
                        S.psA(b),
                        sl[:, c * 512 + qq * 128: c * 512 + qq * 128 + 128],
                        S.WBUF[:, c * 1024 + h * 512: c * 1024 + (h + 1) * 512],
                        start=(c == 0), stop=(c == NCC - 1)),
                      reads=["WBUF"] + sregs,
                      writes=[f"psA:{bank}"])
            for h in range(NH):
                bank = 2 + (h // 8)
                off = 1024 + (h % 8) * 64 + (h // 8) * 512
                pos = (tt * NH + h) * 65
                dst = S.AR2[:, pos:pos + 64]
                regs = self.rAR2(pos, pos + 64)
                if with_bias:
                    E("vector", lambda v, d=dst, o=off, hh=h: v.scalar_tensor_tensor(
                        d, S.PS_A[:, o:o + 64], 1.0, S.BVB[:, hh * 64:(hh + 1) * 64],
                        ALU.mult, ALU.add),
                      reads=[f"psA:{bank}", "MSK:0", "MSK:1"], writes=regs)
                else:
                    E("vector", lambda v, d=dst, o=off: v.tensor_copy(
                        d, S.PS_A[:, o:o + 64]),
                      reads=[f"psA:{bank}"], writes=regs)

    def phase_B(self):
        E = self.P.emit
        S = self
        E("sync", lambda g: g.dma_start(S.AR2[:, 0:2048], S.s_ropeo),
          reads=["s_ropeo"], writes=self.rAR2(0, 2048), dma=True)
        rope_off = lambda tk: ((tk - 6) * 512, 1024 + (tk - 6) * 512)
        self.qk_sweep(S.wq, (0, C), S.s_ln0T, S.s_qT, False, rope_off, (6, 7), "bq")
        self.qk_sweep(S.wk, (0, C), S.s_ln0T, None, True, rope_off, (6, 7), None)
        self.v_sweep(S.wv, (0, C), S.s_ln0T, True)

    def phase_E(self):
        E = self.P.emit
        S = self
        for cc in range(NCC):
            E("vector", lambda v, c=cc: v.memset(
                S.AR1[:, c * STRW: c * STRW + 2], 0.0),
              writes=self.rAR1(cc * STRW, cc * STRW + 2))
        for cc in range(NCC):
            for tk in range(NTK):
                rd = self.rAR1(cc * STRW + tk * 512, cc * STRW + 2 + (tk + 1) * 512)
                E("vector", lambda v, c=cc, k=tk: v.tensor_scalar_mul(
                    S.TOK[:, 0:512],
                    S.AR1[:, c * STRW + k * 512: c * STRW + k * 512 + 512],
                    S.col("w0", c)),
                  reads=rd + ["COLS"], writes=["TOK:0"])
                E("vector", lambda v, c=cc, k=tk: v.scalar_tensor_tensor(
                    S.TOK[:, 512:1024],
                    S.AR1[:, c * STRW + 1 + k * 512: c * STRW + 1 + k * 512 + 512],
                    S.col("w1", c), S.TOK[:, 0:512], ALU.mult, ALU.add),
                  reads=rd + ["COLS", "TOK:0"], writes=["TOK:0"])
                E("vector", lambda v, c=cc, k=tk: v.scalar_tensor_tensor(
                    S.TOK[:, 0:512], self.ar1(c, k * 512, 512),
                    S.col("w2", c), S.TOK[:, 512:1024], ALU.mult, ALU.add),
                  reads=rd + ["COLS", "TOK:0"], writes=["TOK:0"])
                E("gpsimd", lambda g, c=cc, k=tk: g.dma_start(
                    S.s_hcT[c * 128:(c + 1) * 128, k * 512:(k + 1) * 512],
                    S.TOK[:, 0:512]),
                  reads=["TOK:0"], writes=[self.sreg(S.s_hcT, tk)], dma=True)
        E("sync", lambda g: g.dma_start(S.AR2[:, 0:4096], S.c_ropei[:, 0:4096]),
          writes=self.rAR2(0, 4096), dma=True)
        E("sync", lambda g: g.dma_start(S.AR2[:, 4096:8192], S.c_ropei[:, 4096:8192]),
          writes=self.rAR2(4096, 8192), dma=True)
        rope_off = lambda tk: (tk * 512, 4096 + tk * 512)
        self.qk_sweep(S.attn_w, (0, C), S.s_hcT, S.s_qiT, False, rope_off,
                      tuple(range(8)), None)
        self.qk_sweep(S.attn_w, (C, 2 * C), S.s_hcT, None, True, rope_off,
                      tuple(range(8)), None)
        self.v_sweep(S.attn_w, (2 * C, 3 * C), S.s_hcT, False)


    def _make_conv_jobs(self):
        S = self
        jobs = []
        for e in range(NE):
            for g2 in range(16):
                def j(E, ee=e, gg=g2):
                    E("gpsimd", lambda g: g.dma_start(
                        S.s_wvbf[ee, gg].rearrange("p (gv cc j) -> p gv cc j",
                                                   gv=2, cc=NCC),
                        S.ewv[ee].rearrange(
                            "(cc p) (gv gr j) -> gr p gv cc j",
                            p=128, gv=2, j=256)[gg]),
                      writes=["s_wvbf"], dma=True, slot=3)
                jobs.append(j)
            for j4 in range(8):
                def j(E, ee=e, jj4=j4):
                    E("gpsimd", lambda g: g.dma_start(
                        S.s_pjbf[ee, jj4].rearrange("p (jj n) -> p jj n", jj=4),
                        S.eproj[ee, jj4 * 512:(jj4 + 1) * 512, :].rearrange(
                            "(jj p) n -> p jj n", p=128)),
                      writes=["s_pjbf"], dma=True, slot=3)
                jobs.append(j)
        return jobs

    def drain_conv_jobs(self, k):
        for _ in range(k):
            if self._conv_jobs:
                self._conv_jobs.pop(0)(self.P.emit)

    def attention(self, qT_dram, aoT_dram, den_dram, causal):
        E = self.P.emit
        S = self
        if causal and not self._msk_loaded:
            self._msk_loaded = True
            E("sync", lambda g: g.dma_start(S.MSK[:, :], S.c_masks),
              writes=["MSK:0", "MSK:1"], dma=True)

        def emit_scores(h2, qc, kt, qregs, qbuf):
            par = kt % 2
            for hh in range(2):
                sb_ = 2 * par + hh
                kregs = self.rAR1_stripe(h2, kt * 128, 128)
                E("tensor", lambda t, p=h2, k=kt, b=sb_, r=hh, qb=qbuf,
                  q=qc: t.matmul(
                    S.psA(b),
                    S.AR1[r * 64:r * 64 + 64,
                          p * STRW + 2 + k * 128: p * STRW + 2 + k * 128 + 128],
                    qb[r * 64:r * 64 + 64, q * 512:(q + 1) * 512],
                    start=True, stop=True, tile_position=(r * 64, 0)),
                  reads=kregs + qregs, writes=[f"psA:{sb_}"])
            E("scalar", lambda s, pp=par: s.activation(
                S.PB[:, pp * 1024:(pp + 1) * 1024],
                S.PS_A[:, pp * 1024:(pp + 1) * 1024], AF.Exp, scale=SCALE),
              reads=[f"psA:{2 * par}", f"psA:{2 * par + 1}"],
              writes=[f"PB:{2 * par}", f"PB:{2 * par + 1}"])

        def emit_mask(qc, kt, n_kt):
            if not (causal and kt >= n_kt - 4):
                return
            bnd = kt - (n_kt - 4)
            for hh in range(2):
                po = (2 * (kt % 2) + hh) * 512
                preg = [f"PB:{2 * (kt % 2) + hh}"]
                E("vector", lambda v, o=po, bb=bnd: v.tensor_mul(
                    S.PB[:, o:o + 512], S.PB[:, o:o + 512],
                    S.MSK[:, bb * 512:(bb + 1) * 512]),
                  reads=preg + [f"MSK:{bnd // 2}"], writes=preg)

        def emit_pv(h2, qc, kt):
            for hh in range(2):
                h = 2 * h2 + hh
                po = (2 * (kt % 2) + hh) * 512
                preg = [f"PB:{2 * (kt % 2) + hh}"]
                pos = (kt * NH + h) * 65
                E("tensor", lambda t, o=po, hh_=hh, ps=pos,
                  first=(kt == 0): t.matmul(
                    S.PS_B[0:65, hh_ * 512:hh_ * 512 + 512],
                    S.AR2[:, ps:ps + 65], S.PB[:, o:o + 512],
                    start=first, stop=True),
                  reads=preg + self.rAR2(pos, pos + 65),
                  writes=[f"psB:{hh}"])

        def emit_stage(qc):
            p = qc % 2
            for hh in range(2):
                E("vector", lambda v, o=hh, pp=p: v.tensor_copy(
                    S.STCM[0:65, pp * 1024 + o * 512: pp * 1024 + (o + 1) * 512],
                    S.PS_B[0:65, o * 512:o * 512 + 512]),
                  reads=[f"psB:{hh}"],
                  writes=[f"STCM:{2 * p + hh}"])

        def emit_evac(h2, qc):
            p = qc % 2
            sregs = [f"STCM:{2 * p}", f"STCM:{2 * p + 1}"]
            E("gpsimd", lambda g, p2=h2, pp=p, q=qc: g.dma_start(
                aoT_dram[p2 * 128:(p2 + 1) * 128,
                         q * 512:(q + 1) * 512].rearrange(
                    "(hh pr) t -> pr hh t", hh=2),
                S.STCM[0:64, pp * 1024:(pp + 1) * 1024].rearrange(
                    "pr (hh t) -> pr hh t", hh=2)),
              reads=sregs, writes=[self.sreg(aoT_dram, qc)], dma=True)
            for hh in range(2):
                E("gpsimd", lambda g, p2=h2, pp=p, q=qc, o=hh: g.dma_start(
                    den_dram[2 * p2 + o:2 * p2 + o + 1, q * 512:(q + 1) * 512],
                    S.STCM[64:65, pp * 1024 + o * 512: pp * 1024 + o * 512 + 512]),
                  reads=[f"STCM:{2 * p + hh}"],
                  writes=[self.sreg(den_dram, "d")], dma=True)

        for h2 in range(NH // 2):
            par = h2 % 2
            qbuf = S.QBUF[:, par * 4096:(par + 1) * 4096]
            qregs = self.rQB(par * 4096, (par + 1) * 4096)
            E("sync", lambda g, p=h2, qb=qbuf: g.dma_start(
                qb, qT_dram[p * 128:(p + 1) * 128, :]),
              reads=[self.sreg(qT_dram, k) for k in range(8)],
              writes=qregs, dma=True)
            for qc in range(NTK):
                n_kt = (4 * qc + 4) if causal else 32
                for kt in range(n_kt):
                    emit_scores(h2, qc, kt, qregs, qbuf)
                    emit_mask(qc, kt, n_kt)
                    if kt >= 1:
                        emit_pv(h2, qc, kt - 1)
                emit_pv(h2, qc, n_kt - 1)
                emit_stage(qc)
                emit_evac(h2, qc)
                self.drain_conv_jobs(1)

    def proj_ln_phase(self, which):
        E = self.P.emit
        S = self
        is_d = which == "D"
        w_dram = S.wo if is_d else S.proj_w
        aoT = S.s_aoT if is_d else S.s_aoiT
        x_src = S.x_in if is_d else S.s_fin
        x1_dst = S.s_x1 if is_d else S.s_a
        E("gpsimd", lambda g: g.dma_start(
            S.WBUF[:, :].rearrange("p (cc n) -> p cc n", cc=NCC),
            w_dram.rearrange("(cc p) n -> p cc n", p=128)),
          writes=["WBUF"], dma=True)
        if not is_d:
            E("gpsimd", lambda g: g.dma_start(
                S.RWSB[:, :].rearrange("p (cc e) -> p cc e", cc=NCC),
                S.router_w.rearrange("(cc p) e -> p cc e", p=128)),
              writes=["RWSB"], dma=True)
            for vi, (svk, mo) in enumerate((("ln2w", 0), ("ln2b", 1))):
                E("sync", lambda g, k=svk: g.dma_start(
                    S.AR3[0:1, 0:1024], S.smallv[SV[k]:SV[k] + 1, :]),
                  writes=["AR3:0"], dma=True)
                E("vector", lambda v: v.tensor_copy(S.TOK[0:1, 0:1024],
                                                    S.AR3[0:1, 0:1024]),
                  reads=["AR3:0"], writes=["TOK:0"])
                for hf in range(2):
                    E("tensor", lambda t, h=hf: t.matmul(
                        S.psA(h), S.ONES[0:1, 0:128],
                        S.TOK[0:1, h * 512:(h + 1) * 512],
                        start=True, stop=True),
                      reads=["TOK:0", "ONES"], writes=[f"psA:{hf}"])
                E("vector", lambda v, m=mo: v.tensor_copy(
                    S.MSK[:, m * 1024:(m + 1) * 1024], S.PS_A[:, 0:1024]),
                  reads=["psA:0", "psA:1"], writes=[f"MSK:{mo}"])
        den = S.s_den if is_d else S.s_deni
        for tk in range(NTK):
            spar = tk % 2
            slab = S.QBUF[:, spar * 4096:(spar + 1) * 4096]
            sregs = self.rQB(spar * 4096, (spar + 1) * 4096)
            E("sync", lambda g, k=tk, sl=slab: g.dma_start(
                sl.rearrange("p (cc n) -> p cc n", cc=NCC),
                aoT[:, k * 512:(k + 1) * 512].rearrange(
                    "(cc p) n -> p cc n", p=128)),
              reads=[self.sreg(aoT, tk)], writes=sregs, dma=True)
            E("sync", lambda g, k=tk: g.dma_start(
                S.DENL[:, :], den[:, k * 512:(k + 1) * 512]),
              reads=[self.sreg(den, "d")], writes=["DENL"], dma=True)
            E("vector", lambda v: v.reciprocal(S.DENR[:, :], S.DENL[:, :]),
              reads=["DENL"], writes=["DENR"])
            E("vector", lambda v: v.tensor_copy(S.DENB[:, :], S.DENR[:, :]),
              reads=["DENR"], writes=["DENB"])
            for cc in range(NCC):
                bb = cc % 2
                E("tensor", lambda t, c=cc, b=bb: t.matmul(
                    S.psB(b), S.BMAP[:, c * 128:(c + 1) * 128], S.DENB[:, :],
                    start=True, stop=True),
                  reads=["BMAP", "DENB"], writes=[f"psB:{bb}"])
                E("vector", lambda v, c=cc, b=bb, sl=slab: v.tensor_mul(
                    sl[:, c * 512:(c + 1) * 512], sl[:, c * 512:(c + 1) * 512],
                    S.psB(b)),
                  reads=self.rQB(spar * 4096 + cc * 512,
                                 spar * 4096 + (cc + 1) * 512) + [f"psB:{bb}"],
                  writes=self.rQB(spar * 4096 + cc * 512,
                                  spar * 4096 + (cc + 1) * 512))
            for ccp in range(NCC):
                bank = ccp % 2
                for cc in range(NCC):
                    E("tensor", lambda t, c=cc, cp=ccp, b=bank, sl=slab: t.matmul(
                        S.psA(b),
                        S.WBUF[:, c * 1024 + cp * 128: c * 1024 + (cp + 1) * 128],
                        sl[:, c * 512:(c + 1) * 512],
                        start=(c == 0), stop=(c == NCC - 1)),
                      reads=["WBUF"] + sregs,
                      writes=[f"psA:{bank}"])
                dst = S.STCM[:, ccp * 512:(ccp + 1) * 512]
                sreg = [f"STCM:{ccp}"]
                if is_d:
                    E("scalar", lambda s, b=bank, d=dst, c=ccp: s.activation(
                        d, S.psA(b), AF.Identity, bias=S.col("bo", c)),
                      reads=[f"psA:{bank}", "COLS"], writes=sreg)
                else:
                    E("scalar", lambda s, b=bank, d=dst: s.activation(
                        d, S.psA(b), AF.Copy),
                      reads=[f"psA:{bank}"], writes=sreg)
            for q in range(4):
                tt = tk * 4 + q
                parc = tt % 2
                for cc in range(NCC):
                    E("tensor", lambda t, c=cc, p=parc, qq=q: t.transpose(
                        S.PS_C[:, p * 1024 + c * 128: p * 1024 + (c + 1) * 128],
                        S.STCM[:, c * 512 + qq * 128: c * 512 + qq * 128 + 128],
                        S.IDENT[:, :]),
                      reads=[f"STCM:{cc}", "IDENT"], writes=[f"psC:{parc}"])
                E("vector", lambda v, p=parc: v.tensor_copy(
                    S.TOK[:, 0:1024], S.PS_C[:, p * 1024:(p + 1) * 1024]),
                  reads=[f"psC:{parc}"], writes=["TOK:0"])
                E("sync", lambda g, i=tt: g.dma_start(
                    S.AR3[:, 0:1024], x_src[i * 128:(i + 1) * 128, :]),
                  reads=[self.sreg(x_src, "t")] if not is_d else [],
                  writes=["AR3:0"], dma=True)
                o1 = 0 if parc == 0 else 16
                o2 = 8 if parc == 0 else 24
                if is_d:
                    self.ln_stats(S.TOK[:, 0:1024], ["TOK:0"], o1)
                    E("vector", lambda v, o=o1: v.tensor_scalar(
                        S.AR3[:, 1024:2048], S.TOK[:, 0:1024],
                        S.STAT[:, o:o + 1], S.STAT[:, o + 1:o + 2],
                        ALU.subtract, ALU.mult),
                      reads=["TOK:0", self.rST(o1)], writes=["AR3:1"])
                    E("vector", lambda v: v.tensor_add(
                        S.AR3[:, 1024:2048], S.AR3[:, 1024:2048], S.AR3[:, 0:1024]),
                      reads=["AR3:1", "AR3:0"], writes=["AR3:1"])
                else:
                    E("vector", lambda v: v.tensor_add(
                        S.AR3[:, 1024:2048], S.TOK[:, 0:1024], S.AR3[:, 0:1024]),
                      reads=["TOK:0", "AR3:0"], writes=["AR3:1"])
                E("gpsimd", lambda g, i=tt: g.dma_start(
                    x1_dst[i * 128:(i + 1) * 128, :], S.AR3[:, 1024:2048]),
                  reads=["AR3:1"], writes=[self.sreg(x1_dst, "t")], dma=True)
                self.ln_stats(S.AR3[:, 1024:2048], ["AR3:1"], o2)
                E("vector", lambda v, o=o2: v.tensor_scalar(
                    S.AR3[:, 2048:3072], S.AR3[:, 1024:2048],
                    S.STAT[:, o:o + 1], S.STAT[:, o + 1:o + 2],
                    ALU.subtract, ALU.mult),
                  reads=["AR3:1", self.rST(o2)], writes=["AR3:2"])
                if is_d:
                    E("gpsimd", lambda g, i=tt: g.dma_start(
                        S.s_fin[i * 128:(i + 1) * 128, :], S.AR3[:, 2048:3072]),
                      reads=["AR3:2"], writes=[self.sreg(S.s_fin, "t")], dma=True)
                E("vector", lambda v: v.tensor_copy(
                    S.TOK[:, 1024:2048], S.AR3[:, 2048:3072]),
                  reads=["AR3:2"], writes=["TOK:1"])
                for cc in range(NCC):
                    E("tensor", lambda t, c=cc, p=parc: t.transpose(
                        S.PS_C[:, p * 1024 + c * 128: p * 1024 + (c + 1) * 128],
                        S.TOK[:, 1024 + c * 128: 1024 + (c + 1) * 128],
                        S.IDENT[:, :]),
                      reads=["TOK:1", "IDENT"], writes=[f"psC:{parc}"])
                lnw = "ln1w" if is_d else "ln2w"
                lnb = "ln1b" if is_d else "ln2b"
                for cc in range(NCC):
                    if is_d:
                        dsth = self.ar1(cc, tt * 128, 128)
                        rr = self.rAR1_stripe(cc, tt * 128, 128)
                    else:
                        dsth = S.STCM[:, 4096 + cc * 128: 4096 + (cc + 1) * 128]
                        rr = [f"STCM:{8 + cc // 4}"]
                    E("vector", lambda v, c=cc, p=parc, d=dsth, rg=rr: v.tensor_scalar(
                        d, S.PS_C[:, p * 1024 + c * 128: p * 1024 + (c + 1) * 128],
                        S.col(lnw, c), S.col(lnb, c), ALU.mult, ALU.add),
                      reads=[f"psC:{parc}", "COLS"], writes=rr)
                if is_d and S.s_hT is not None:
                    for cc in range(NCC):
                        E("gpsimd", lambda g, c=cc, i=tt: g.dma_start(
                            S.s_hT[c * 128:(c + 1) * 128, i * 128:(i + 1) * 128],
                            self.ar1(c, i * 128, 128)),
                          reads=self.rAR1_stripe(cc, tt * 128, 128),
                          writes=["hTdbg"], dma=True)
                if not is_d:
                    E("vector", lambda v: v.tensor_mul(
                        S.AR3[:, 0:1024], S.TOK[:, 1024:2048],
                        S.MSK[:, 0:1024]),
                      reads=["TOK:1", "MSK:0"], writes=["AR3:0"])
                    E("vector", lambda v: v.tensor_add(
                        S.AR3[:, 0:1024], S.AR3[:, 0:1024],
                        S.MSK[:, 1024:2048]),
                      reads=["AR3:0", "MSK:1"], writes=["AR3:0"])
                    E("gpsimd", lambda g, i=tt: g.dma_start(
                        S.s_h2[i * 128:(i + 1) * 128, :],
                        S.AR3[:, 0:1024]),
                      reads=["AR3:0"],
                      writes=[self.sreg(S.s_h2, tt)], dma=True)
                    for cc in range(NCC):
                        E("tensor", lambda t, c=cc: t.matmul(
                            S.PS_B[:, 0:4],
                            S.STCM[:, 4096 + c * 128: 4096 + (c + 1) * 128],
                            S.RWSB[:, c * 4:(c + 1) * 4],
                            start=(c == 0), stop=(c == NCC - 1)),
                          reads=[f"STCM:{8 + cc // 4}", "RWSB"], writes=["psB:0"])
                    E("vector", lambda v: v.reduce_max(
                        S.STAT[:, 16:17], S.PS_B[:, 0:4], AX.X),
                      reads=["psB:0"], writes=["STAT:2"])
                    E("vector", lambda v: v.tensor_single_scalar(
                        S.STAT[:, 20:24], S.PS_B[:, 0:4], S.STAT[:, 16:17],
                        ALU.is_ge),
                      reads=["psB:0", "STAT:2"], writes=["STAT:2"])
                    E("vector", lambda v, i=tt: v.tensor_copy(
                        S.MCOL[:, i * 4:i * 4 + 1], S.STAT[:, 20:21]),
                      reads=["STAT:2"], writes=["MCOL"])
                    E("vector", lambda v: v.tensor_scalar(
                        S.STAT[:, 17:18], S.STAT[:, 20:21], -1.0, 1.0,
                        ALU.mult, ALU.add),
                      reads=["STAT:2"], writes=["STAT:2"])
                    for e in range(1, NE):
                        E("vector", lambda v, i=tt, ee=e: v.tensor_mul(
                            S.MCOL[:, i * 4 + ee:i * 4 + ee + 1],
                            S.STAT[:, 20 + ee:21 + ee], S.STAT[:, 17:18]),
                          reads=["STAT:2"], writes=["MCOL"])
                        if e < NE - 1:
                            E("vector", lambda v, i=tt, ee=e: v.tensor_sub(
                                S.STAT[:, 17:18], S.STAT[:, 17:18],
                                S.MCOL[:, i * 4 + ee:i * 4 + ee + 1]),
                              reads=["STAT:2", "MCOL"], writes=["STAT:2"])

    def moe_machinery(self):
        E = self.P.emit
        S = self
        for dst, src, reg in ((S.TRIS, S.c_tris, "TOK:0"),
                              (S.DCMB, S.c_dcmb, "TOK:0"),
                              (S.SELC, S.c_selc, "SELC"),
                              (S.ONESC, S.c_onesc, "ONESC"),
                              (S.ONESFC, S.c_onesfc, "AR3:2"),
                              (S.IOTAP, S.c_iotap, "IOTAP"),
                              (S.IOTAS4, S.c_iotas4, "AR3:2")):
            E("sync", lambda g, d=dst, s=src: g.dma_start(d[:, :], s),
              writes=[reg], dma=True)
        E("sync", lambda g: g.dma_start(S.MIX[0:1, 32:40], S.c_pos8),
          writes=["AR3:2"], dma=True)
        E("vector", lambda v: v.tensor_copy(S.MCOLB[:, :], S.MCOL[:, :]),
          reads=["MCOL"], writes=["TOK:0"])
        E("tensor", lambda t: t.matmul(S.PS_B[:, 0:1], S.MCOLB[:, :],
                                       S.ONESC[:, :], start=True, stop=True),
          reads=["TOK:0", "ONESC"], writes=["psB:0"])
        E("vector", lambda v: v.tensor_copy(S.CNTC[:, :], S.PS_B[:, 0:1]),
          reads=["psB:0"], writes=["CNTC"])
        E("tensor", lambda t: t.matmul(S.PS_A[:, 0:128], S.TRIS[:, :],
                                       S.MCOLB[:, :], start=True, stop=True),
          reads=["TOK:0", "TOK:0"], writes=["psA:0"])
        E("vector", lambda v: v.tensor_copy(S.TOK[:, 0:128], S.PS_A[:, 0:128]),
          reads=["psA:0"], writes=["TOK:0"])
        E("tensor", lambda t: t.matmul(S.PS_B[:, 512:513], S.DCMB[:, :],
                                       S.CNTC[:, :], start=True, stop=True),
          reads=["TOK:0", "CNTC"], writes=["psB:1"])
        E("vector", lambda v: v.tensor_copy(S.MCH[:, 0:1], S.PS_B[:, 512:513]),
          reads=["psB:1"], writes=["MCH:0"])
        E("tensor", lambda t: t.matmul(S.PS_B[0:1, 4:8], S.CNTC[:, :],
                                       S.SELC[:, :], start=True, stop=True),
          reads=["CNTC", "SELC"], writes=["psB:0"])
        E("vector", lambda v: v.tensor_copy(S.MIX[0:1, 0:4], S.PS_B[0:1, 4:8]),
          reads=["psB:0"], writes=["AR3:2"])
        E("tensor", lambda t: t.transpose(S.PS_B[0:1, 640:768], S.MCH[:, 0:1],
                                          S.IDENTF[:, :]),
          reads=["MCH:0", "IDENTF"], writes=["psB:1"])
        E("vector", lambda v: v.tensor_copy(S.AR3[0:1, 0:128],
                                            S.PS_B[0:1, 640:768]),
          reads=["psB:1"], writes=["AR3:0"])
        E("tensor", lambda t: t.matmul(S.PS_A[:, 512:640], S.ONESFC[:, :],
                                       S.AR3[0:1, 0:128], start=True, stop=True),
          reads=["AR3:2", "AR3:0"], writes=["psA:1"])
        E("vector", lambda v: v.tensor_add(S.AR3[:, 1024:1152], S.TOK[:, 0:128],
                                           S.PS_A[:, 512:640]),
          reads=["TOK:0", "psA:1"], writes=["AR3:1"])
        E("vector", lambda v: v.tensor_mul(S.AR3[:, 2048:2176], S.MCOL[:, :],
                                           S.AR3[:, 1024:1152]),
          reads=["MCOL", "AR3:1"], writes=["AR3:2"])
        E("vector", lambda v: v.tensor_add(S.AR3[:, 1536:1568],
                                           S.AR3[:, 2048:2176:4],
                                           S.AR3[:, 2049:2176:4]),
          reads=["AR3:2"], writes=["AR3:1"])
        for off in (2050, 2051):
            E("vector", lambda v, o=off: v.tensor_add(
                S.AR3[:, 1536:1568], S.AR3[:, 1536:1568], S.AR3[:, o:2176:4]),
              reads=["AR3:1", "AR3:2"], writes=["AR3:1"])
        E("vector", lambda v: v.tensor_copy(S.IXI[:, :], S.AR3[:, 1536:1568]),
          reads=["AR3:1"], writes=["IXI"])
        for tt in range(NTT):
            par = tt % 2
            E("sync", lambda g, i=tt, p=par: g.dma_start(
                S.QBUF[:, p * 1024:(p + 1) * 1024],
                S.s_h2[i * 128:(i + 1) * 128, :]),
              reads=[self.sreg(S.s_h2, tt)],
              writes=self.rQB(par * 1024, (par + 1) * 1024), dma=True)
            E("gpsimd", lambda g, i=tt, p=par: g.indirect_dma_start(
                S.s_hs, bass.IndirectOffsetOnAxis(ap=S.IXI[:, i:i + 1], axis=0),
                S.QBUF[:, p * 1024:(p + 1) * 1024], None),
              reads=["IXI"] + self.rQB(par * 1024, (par + 1) * 1024),
              writes=[self.sreg(S.s_hs, "w")], dma=True)
            E("gpsimd", lambda g, i=tt: g.indirect_dma_start(
                S.s_g, bass.IndirectOffsetOnAxis(ap=S.IXI[:, i:i + 1], axis=0),
                S.IOTAS4[:, i * 4:(i + 1) * 4], None),
              reads=["IXI", "AR3:2"], writes=[self.sreg(S.s_g, "w")], dma=True)
        E("vector", lambda v: v.tensor_single_scalar(
            S.MIX[0:1, 40:48], S.MIX[0:1, 32:40], S.MIX[0:1, 1:2], ALU.is_ge),
          reads=["AR3:2"], writes=["AR3:2"])
        for e in (2, 3):
            E("vector", lambda v, ee=e: v.tensor_single_scalar(
                S.MIX[0:1, 48:56], S.MIX[0:1, 32:40], S.MIX[0:1, ee:ee + 1],
                ALU.is_ge),
              reads=["AR3:2"], writes=["AR3:2"])
            E("vector", lambda v: v.tensor_add(
                S.MIX[0:1, 40:48], S.MIX[0:1, 40:48], S.MIX[0:1, 48:56]),
              reads=["AR3:2"], writes=["AR3:2"])
        E("vector", lambda v: v.tensor_scalar(
            S.MIX[0:1, 16:20], S.MIX[0:1, 0:4], 512.0, 0.0, ALU.is_ge, ALU.add),
          reads=["AR3:2"], writes=["AR3:2"])
        for k in range(2, 8):
            E("vector", lambda v, kk=k: v.tensor_scalar(
                S.MIX[0:1, 24:28], S.MIX[0:1, 0:4], 512.0 * kk, 0.0,
                ALU.is_ge, ALU.add),
              reads=["AR3:2"], writes=["AR3:2"])
            E("vector", lambda v: v.tensor_add(
                S.MIX[0:1, 16:20], S.MIX[0:1, 16:20], S.MIX[0:1, 24:28]),
              reads=["AR3:2"], writes=["AR3:2"])
        E("vector", lambda v: v.scalar_tensor_tensor(
            S.MIX[0:1, 24:28], S.MIX[0:1, 16:20], -512.0, S.MIX[0:1, 0:4],
            ALU.mult, ALU.add),
          reads=["AR3:2"], writes=["AR3:2"])
        E("vector", lambda v: v.tensor_scalar(
            S.MIX[0:1, 20:24], S.MIX[0:1, 24:28], 0.0, 0.0, ALU.is_gt, ALU.add),
          reads=["AR3:2"], writes=["AR3:2"])
        E("tensor", lambda t: t.matmul(S.PS_A[:, 656:659], S.ONESFC[:, :],
                                       S.MIX[0:1, 1:4], start=True, stop=True),
          reads=["AR3:2", "AR3:2"], writes=["psA:1"])
        E("vector", lambda v: v.tensor_copy(S.MCH[:, 1:4], S.PS_A[:, 656:659]),
          reads=["psA:1"], writes=["MCH:0"])
        E("tensor", lambda t: t.matmul(S.PS_A[:, 640:648], S.ONESFC[:, :],
                                       S.MIX[0:1, 40:48], start=True, stop=True),
          reads=["AR3:2", "AR3:2"], writes=["psA:1"])
        E("vector", lambda v: v.tensor_copy(S.MCH[:, 4:12], S.PS_A[:, 640:648]),
          reads=["psA:1"], writes=["MCH:1"])
        E("tensor", lambda t: t.matmul(S.PS_A[:, 648:652], S.ONESFC[:, :],
                                       S.MIX[0:1, 16:20], start=True, stop=True),
          reads=["AR3:2", "AR3:2"], writes=["psA:1"])
        E("vector", lambda v: v.tensor_copy(S.MCH[:, 12:16], S.PS_A[:, 648:652]),
          reads=["psA:1"], writes=["MCH:2"])
        E("tensor", lambda t: t.matmul(S.PS_A[:, 652:656], S.ONESFC[:, :],
                                       S.MIX[0:1, 20:24], start=True, stop=True),
          reads=["AR3:2", "AR3:2"], writes=["psA:1"])
        E("vector", lambda v: v.tensor_copy(S.MCH[:, 16:20], S.PS_A[:, 652:656]),
          reads=["psA:1"], writes=["MCH:3"])
        E("vector", lambda v: v.tensor_single_scalar(
            S.ESF[:, :], S.IOTAS4[:, 0:128:4], S.MCH[:, 1:2], ALU.is_ge),
          reads=["AR3:2", "MCH:0"], writes=["ESF"])
        for e in (2, 3):
            E("vector", lambda v, ee=e: v.tensor_single_scalar(
                S.AR3[:, 0:32], S.IOTAS4[:, 0:128:4], S.MCH[:, ee:ee + 1],
                ALU.is_ge),
              reads=["AR3:2", "MCH:0"], writes=["AR3:0"])
            E("vector", lambda v: v.tensor_add(S.ESF[:, :], S.ESF[:, :],
                                               S.AR3[:, 0:32]),
              reads=["ESF", "AR3:0"], writes=["ESF"])
        if self.s_dbg is not None:
            E("gpsimd", lambda g: g.dma_start(S.s_dbg[:, 0:128], S.MCOL[:, :]),
              reads=["MCOL"], writes=["s_dbg"], dma=True)
            E("gpsimd", lambda g: g.dma_start(S.s_dbg[0:1, 418:482], S.MIX[0:1, :]),
              reads=["AR3:2"], writes=["s_dbg"], dma=True)
            E("gpsimd", lambda g: g.dma_start(S.s_dbg[:, 482:514], S.ESF[:, 0:32]),
              reads=["ESF"], writes=["s_dbg"], dma=True)
            E("gpsimd", lambda g: g.dma_start(S.s_dbg[:, 610:630], S.MCH[:, :]),
              reads=["MCH:0", "MCH:1", "MCH:2", "MCH:3"], writes=["s_dbg"], dma=True)

    def moe_pass(self, pidx):
        E = self.P.emit
        S = self
        BIG = 1.0e6
        WV_OFF = (8192, 12288)
        PJ_OFF = 16384
        primary = pidx < 8
        est = pidx - 7
        mp = pidx % 2
        o = mp * 8
        ixfr = ["AR3:2"]

        def fcol(c):
            return S.IXF[:, o + c:o + c + 1]

        if primary:
            epc = S.MCH[:, 4 + pidx:5 + pidx]
            E("vector", lambda v: v.scalar_tensor_tensor(
                fcol(0), epc, 2048.0, S.IOTAP[:, :], ALU.mult, ALU.add),
              reads=["MCH:1", "IOTAP"], writes=ixfr)
            E("vector", lambda v: v.scalar_tensor_tensor(
                fcol(1), epc, 1024.0, S.IOTAP[:, :], ALU.mult, ALU.add),
              reads=["MCH:1", "IOTAP"], writes=ixfr)
            for g2 in range(16):
                E("vector", lambda v, gg=g2: v.tensor_scalar_add(
                    S.IXW[:, mp * 16 + gg:mp * 16 + gg + 1], fcol(0), 128.0 * gg),
                  reads=ixfr, writes=[f"IXW:{mp}"])
            for j4 in range(8):
                E("vector", lambda v, jj=j4: v.tensor_scalar_add(
                    S.IXP[:, mp * 8 + jj:mp * 8 + jj + 1], fcol(1), 128.0 * jj),
                  reads=ixfr, writes=[f"IXP:{mp}"])
        else:
            E("vector", lambda v: v.scalar_tensor_tensor(
                fcol(0), S.MCH[:, 12 + est:13 + est], 512.0, S.IOTAP[:, :],
                ALU.mult, ALU.add),
              reads=["MCH:2", "IOTAP"], writes=ixfr)
            for q in range(4):
                E("vector", lambda v, qq=q: v.tensor_scalar_add(
                    fcol(2), fcol(0), 128.0 * qq),
                  reads=ixfr, writes=ixfr)
                E("vector", lambda v, qq=q: v.tensor_copy(
                    S.IXH[:, mp * 4 + qq:mp * 4 + qq + 1], fcol(2)),
                  reads=ixfr, writes=[f"IXH:{mp}"])

        for q in range(4):
            gsl = q % 2
            gt = S.QBUF[:, gsl * 1024:(gsl + 1) * 1024]
            gregs = self.rQB(gsl * 1024, (gsl + 1) * 1024)
            gq = S.GQ[:, q * 4:q * 4 + 4]
            gqr = ["AR3:2"]
            if primary:
                row0 = pidx * 512 + q * 128
                E("sync", lambda g, r=row0, d=gt: g.dma_start(
                    d, S.s_hs[r:r + 128, :]),
                  reads=[self.sreg(S.s_hs, "w")], writes=gregs, dma=True)
                E("sync", lambda g, r=row0, d=gq: g.dma_start(
                    d, S.s_g[r:r + 128, :]),
                  reads=[self.sreg(S.s_g, "w")], writes=gqr, dma=True)
            else:
                idxc = S.IXH[:, mp * 4 + q:mp * 4 + q + 1]
                E("gpsimd", lambda g, d=gt, ic=idxc: g.indirect_dma_start(
                    d, None, S.s_hs,
                    bass.IndirectOffsetOnAxis(ap=ic, axis=0)),
                  reads=[f"IXH:{mp}", self.sreg(S.s_hs, "w")],
                  writes=gregs, dma=True)
                E("gpsimd", lambda g, d=gq, ic=idxc: g.indirect_dma_start(
                    d, None, S.s_g,
                    bass.IndirectOffsetOnAxis(ap=ic, axis=0)),
                  reads=[f"IXH:{mp}", self.sreg(S.s_g, "w")],
                  writes=gqr, dma=True)
            if primary:
                esq = S.ESF[:, pidx * 4 + q:pidx * 4 + q + 1]
                E("vector", lambda v, e_=esq: v.tensor_single_scalar(
                    fcol(5), e_, S.MCH[:, 4 + pidx:5 + pidx], ALU.is_equal),
                  reads=["ESF", "MCH:1"], writes=ixfr)
            else:
                E("vector", lambda v, qq=q: v.tensor_scalar_add(
                    fcol(2), fcol(0), 128.0 * qq),
                  reads=ixfr, writes=ixfr)
                E("vector", lambda v: v.tensor_single_scalar(
                    fcol(3), fcol(2), S.MCH[:, 1:2], ALU.is_ge),
                  reads=ixfr + ["MCH:0"], writes=ixfr)
                for ee in (2, 3):
                    E("vector", lambda v, e2=ee: v.tensor_single_scalar(
                        fcol(4), fcol(2), S.MCH[:, e2:e2 + 1], ALU.is_ge),
                      reads=ixfr + ["MCH:0"], writes=ixfr)
                    E("vector", lambda v: v.tensor_add(fcol(3), fcol(3), fcol(4)),
                      reads=ixfr, writes=ixfr)
                E("vector", lambda v: v.tensor_single_scalar(
                    fcol(5), fcol(3), float(est), ALU.is_equal),
                  reads=ixfr, writes=ixfr)
                E("vector", lambda v: v.tensor_mul(
                    fcol(5), fcol(5), S.MCH[:, 16 + est:17 + est]),
                  reads=ixfr + ["MCH:3"], writes=ixfr)
            E("vector", lambda v: v.tensor_scalar(
                fcol(6), fcol(5), -BIG, BIG, ALU.mult, ALU.add),
              reads=ixfr, writes=ixfr)
            E("vector", lambda v, g_=gq: v.tensor_add(
                fcol(6), fcol(6), g_[:, 0:1]),
              reads=ixfr + gqr, writes=ixfr)
            E("vector", lambda v, qq=q: v.tensor_copy(
                S.SIDI[:, mp * 4 + qq:mp * 4 + qq + 1], fcol(6)),
              reads=ixfr, writes=[f"SIDI:{mp}"])
            for cc in range(NCC):
                E("tensor", lambda t, c=cc, qq=q, d=gt: t.transpose(
                    S.PS_C[:, (qq % 2) * 1024 + c * 128:(qq % 2) * 1024 + (c + 1) * 128],
                    d[:, c * 128:(c + 1) * 128], S.IDENT[:, :]),
                  reads=gregs + ["IDENT"], writes=[f"psC:{q % 2}"])
            for cc in range(NCC):
                E("vector", lambda v, c=cc, qq=q: v.tensor_copy(
                    S.AR2[:, c * 512 + qq * 128: c * 512 + qq * 128 + 128],
                    S.PS_C[:, (qq % 2) * 1024 + c * 128:(qq % 2) * 1024 + (c + 1) * 128]),
                  reads=[f"psC:{q % 2}"],
                  writes=self.rAR2(cc * 512, (cc + 1) * 512))

        for g2 in range(16):
            woff = WV_OFF[g2 % 2]
            wslot = S.AR2[:, woff:woff + 4096]
            wregs = self.rAR2(woff, woff + 4096)
            if primary:
                E("gpsimd", lambda g, gg=g2, w=wslot: g.indirect_dma_start(
                    w, None, S.s_wvbf.rearrange("e g p j -> (e g p) j"),
                    bass.IndirectOffsetOnAxis(
                        ap=S.IXW[:, mp * 16 + gg:mp * 16 + gg + 1], axis=0)),
                  reads=[f"IXW:{mp}", "s_wvbf"], writes=wregs, dma=True)
            else:
                E("sync", lambda g, gg=g2, w=wslot: g.dma_start(
                    w, S.s_wvbf[est, gg]),
                  reads=["s_wvbf"], writes=wregs, dma=True)
            for jh in range(2):
                jc = 2 * g2 + jh
                gb, vb = jc % 2, 2 + jc % 2
                for cc in range(NCC):
                    E("tensor", lambda t, c=cc, w=wslot, j=jh, b=gb: t.matmul(
                        S.psA(b),
                        w[:, c * 256 + j * 128: c * 256 + j * 128 + 128],
                        S.AR2[:, c * 512:(c + 1) * 512],
                        start=(c == 0), stop=(c == NCC - 1)),
                      reads=wregs[:4] + self.rAR2(cc * 512, (cc + 1) * 512),
                      writes=[f"psA:{gb}"])
                for cc in range(NCC):
                    E("tensor", lambda t, c=cc, w=wslot, j=jh, b=vb: t.matmul(
                        S.psA(b),
                        w[:, 2048 + c * 256 + j * 128: 2048 + c * 256 + j * 128 + 128],
                        S.AR2[:, c * 512:(c + 1) * 512],
                        start=(c == 0), stop=(c == NCC - 1)),
                      reads=wregs[4:] + self.rAR2(cc * 512, (cc + 1) * 512),
                      writes=[f"psA:{vb}"])
                E("scalar", lambda s, b=gb, j=jc: s.activation(
                    S.SILU[:, (j % 2) * 512:(j % 2) * 512 + 512], S.psA(b),
                    AF.Silu),
                  reads=[f"psA:{gb}"], writes=["SILU"])
                E("vector", lambda v, j=jc, b=vb: v.tensor_mul(
                    S.AR1[:, j * 512:(j + 1) * 512],
                    S.SILU[:, (j % 2) * 512:(j % 2) * 512 + 512], S.psA(b)),
                  reads=["SILU", f"psA:{vb}"],
                  writes=self.rAR1(jc * 512, (jc + 1) * 512))

        for grp, tqs in enumerate(((0, 1, 2), (3,))):
            for jc in range(NJC):
                if jc % 4 == 0:
                    j4 = jc // 4
                    poff4 = PJ_OFF + (j4 % 2) * 4096
                    pslot4 = S.AR2[:, poff4:poff4 + 4096]
                    if primary:
                        E("gpsimd", lambda g, jj=j4, ps=pslot4: g.indirect_dma_start(
                            ps, None, S.s_pjbf.rearrange("e j p n -> (e j p) n"),
                            bass.IndirectOffsetOnAxis(
                                ap=S.IXP[:, mp * 8 + jj:mp * 8 + jj + 1], axis=0)),
                          reads=[f"IXP:{mp}", "s_pjbf"],
                          writes=self.rAR2(poff4, poff4 + 4096), dma=True)
                    else:
                        E("sync", lambda g, jj=j4, ps=pslot4: g.dma_start(
                            ps, S.s_pjbf[est, jj]),
                          reads=["s_pjbf"],
                          writes=self.rAR2(poff4, poff4 + 4096), dma=True)
                poff = PJ_OFF + ((jc // 4) % 2) * 4096 + (jc % 4) * 1024
                pslot = S.AR2[:, poff:poff + 1024]
                pregs = self.rAR2(poff, poff + 1024)
                for i, tq in enumerate(tqs):
                    for half in range(2):
                        bank = 2 * i + half
                        bs = S.psA(bank) if bank < 4 else S.psB(bank - 4)
                        breg = (f"psA:{bank}" if bank < 4 else f"psB:{bank - 4}")
                        E("tensor", lambda t, j=jc, tq_=tq, h=half, b=bs,
                          ps=pslot: t.matmul(
                            b,
                            S.AR1[:, j * 512 + tq_ * 128: j * 512 + tq_ * 128 + 128],
                            ps[:, h * 512:(h + 1) * 512],
                            start=(j == 0), stop=(j == NJC - 1)),
                          reads=pregs + self.rAR1(jc * 512 + tq * 128,
                                                  jc * 512 + tq * 128 + 128),
                          writes=[breg])
            for i, tq in enumerate(tqs):
                if i < 2:
                    src = S.PS_A[:, 2 * i * 512: 2 * i * 512 + 1024]
                    regs = [f"psA:{2 * i}", f"psA:{2 * i + 1}"]
                else:
                    src = S.PS_B[:, 0:1024]
                    regs = ["psB:0", "psB:1"]
                yt = S.QBUF[:, 4096 + tq * 1024: 4096 + (tq + 1) * 1024]
                ytregs = self.rQB(4096 + tq * 1024, 4096 + (tq + 1) * 1024)
                E("vector", lambda v, s=src, d=yt: v.tensor_copy(d, s),
                  reads=regs, writes=ytregs)
                E("gpsimd", lambda g, tq_=tq, d=yt: g.indirect_dma_start(
                    S.out_moe,
                    bass.IndirectOffsetOnAxis(
                        ap=S.SIDI[:, mp * 4 + tq_:mp * 4 + tq_ + 1], axis=0),
                    d, None, bounds_check=T - 1, oob_is_err=False),
                  reads=[f"SIDI:{mp}"] + ytregs,
                  writes=[self.sreg(S.out_moe, "w")], dma=True)

    def phase_H(self):
        self.moe_machinery()
        for pidx in range(11):
            self.moe_pass(pidx)
        E = self.P.emit
        S = self
        for tt in range(NTT):
            par = tt % 2
            om = S.TOK[:, par * 1024:(par + 1) * 1024]
            E("gpsimd", lambda g, i=tt, m=om: g.dma_start(
                m, S.out_moe[i * 128:(i + 1) * 128, :]),
              reads=[self.sreg(S.out_moe, "w")], writes=[f"TOK:{par}"], dma=True)
            E("gpsimd", lambda g, i=tt: g.dma_start(
                S.AR3[:, 0:1024], S.s_a[i * 128:(i + 1) * 128, :]),
              reads=[self.sreg(S.s_a, "t")], writes=["AR3:0"], dma=True)
            E("gpsimd", lambda g, i=tt: g.dma_start(
                S.AR3[:, 1024:2048], S.s_x1[i * 128:(i + 1) * 128, :]),
              reads=[self.sreg(S.s_x1, "t")], writes=["AR3:1"], dma=True)
            of = par * 8
            E("vector", lambda v, m=om, p=par: v.tensor_add(
                S.AR3[:, 2048:3072], S.AR3[:, 0:1024], m),
              reads=["AR3:0", f"TOK:{par}"], writes=["AR3:2"])
            self.ln_stats(S.AR3[:, 2048:3072], ["AR3:2"], of)
            E("vector", lambda v, o=of: v.tensor_scalar(
                S.AR3[:, 0:1024], S.AR3[:, 2048:3072],
                S.STAT[:, o:o + 1], S.STAT[:, o + 1:o + 2],
                ALU.subtract, ALU.mult),
              reads=["AR3:2", self.rST(of)], writes=["AR3:0"])
            E("vector", lambda v: v.tensor_add(
                S.AR3[:, 0:1024], S.AR3[:, 0:1024], S.AR3[:, 1024:2048]),
              reads=["AR3:0", "AR3:1"], writes=["AR3:0"])
            E("gpsimd", lambda g, i=tt: g.dma_start(
                S.y_out[i * 128:(i + 1) * 128, :], S.AR3[:, 0:1024]),
              reads=["AR3:0"], writes=["y_out"], dma=True)


_BUILD_CACHE = {}


def get_nc(serial=True, upto="H", dbg=()):
    key = (serial, upto, tuple(sorted(dbg)))
    if key not in _BUILD_CACHE:
        _BUILD_CACHE[key] = Build(serial=serial, upto=upto, dbg=dbg).build()
    return _BUILD_CACHE[key]


def make_inmaps(inputs, cores=8):
    ident, identf, perm, ones_row, rope_i, masks, bmap, routec = host_constants()
    sv = pack_smallvecs(inputs)
    x = np.ascontiguousarray(np.asarray(inputs["L0_attn_hidden_states"], np.float32))
    f32c = lambda k: np.ascontiguousarray(np.asarray(inputs[k], np.float32))
    shared = dict(
        cos_o=f32c("L0_attn_cos"), sin_o=f32c("L0_attn_sin"),
        wq=f32c("L0_attn_wq"), wk=f32c("L0_attn_wk"), wv=f32c("L0_attn_wv"),
        wo=f32c("L0_attn_wo"), attn_w=f32c("L0_ffn_attn_w"),
        proj_w=f32c("L0_ffn_proj_w"), router_w=f32c("L0_ffn_router_w"),
        ewv=f32c("L0_ffn_expert_wv"), eproj=f32c("L0_ffn_expert_proj"),
        smallv=sv, c_ident=ident, c_identf=identf, c_perm=perm,
        c_ones=ones_row, c_ropei=rope_i, c_masks=masks, c_bmap=bmap,
        **routec,
    )
    return [dict(shared, x=np.ascontiguousarray(x[b % x.shape[0]]))
            for b in range(cores)]


def kernel(**inputs):
    nc = get_nc(serial=False, upto="H", dbg=())
    in_maps = make_inmaps(inputs, cores=8)
    res = run_bass_kernel_spmd(nc, in_maps, core_ids=list(range(8)))
    return np.stack([np.asarray(r["y"], np.float32) for r in res.results])
